# version 01
# baseline (speedup 1.0000x reference)
import os
os.environ.setdefault("JAX_COMPILATION_CACHE_DIR", "/tmp/jaxcache")
os.environ.setdefault("JAX_PERSISTENT_CACHE_MIN_COMPILE_TIME_SECS", "1")
os.environ.setdefault("JAX_PERSISTENT_CACHE_MIN_ENTRY_SIZE_BYTES", "-1")

import contextlib
import numpy as np

import concourse.bass as bass
import concourse.mybir as mybir
from concourse.bass_utils import run_bass_kernel_spmd

F32 = mybir.dt.float32
AF = mybir.ActivationFunctionType
ALU = mybir.AluOpType
AX = mybir.AxisListType

D = 1024
SEQ = 8192
DH = 64
WINDOW = 512
HID = 2730
HID_PAD = 2816
NHP = HID_PAD // 128
QK_SCALE = 8.0

T0_L = 1536
KA_L = [1280, 640]
QA_L = [768, 128]
ML_L = [704, 0]
NT0 = 2048 + T0_L
NKA = [2048 + KA_L[0], 2048 + KA_L[1]]
NQA = [2048 + QA_L[0], 2048 + QA_L[1]]
NML = [2048 + ML_L[0], 2048]

NEG = np.float32(-1e30)
DEBUG_STAGE = None


def _tiles(n, t=512):
    out, o = [], 0
    while o < n:
        w = min(t, n - o)
        out.append((o, w))
        o += w
    return out



def _rotary_tables(pos):
    half = np.arange(0, DH, 2, dtype=np.float64)
    inv_freq = 1.0 / 10000.0 ** (half / DH)
    t = pos.astype(np.float64)
    freqs = np.concatenate([t[:, None] * inv_freq[None, :]] * 2, axis=-1)
    base = (half + 0.4 * DH) / (1.4 * DH)
    power = (t - SEQ // 2) / float(WINDOW // 2)
    sc = base[None, :] ** power[:, None]
    sc2 = np.concatenate([sc, sc], axis=-1)
    cos, sin = np.cos(freqs), np.sin(freqs)
    sgn = np.concatenate([-np.ones(32), np.ones(32)])
    Aq = (cos * sc2).T
    Bq = (sgn[None, :] * sin * sc2).T
    Ak = (cos / sc2).T
    Bk = (sgn[None, :] * sin / sc2).T
    t2 = lambda m: np.ascontiguousarray(np.tile(m.astype(np.float32), (2, 1)))
    return t2(Aq), t2(Bq), t2(Ak), t2(Bk)


def _ffn_masks(s, li):
    qa0 = s - QA_L[li]
    cols = []
    for (qo, qw) in _tiles(NQA[li]):
        q_abs = qa0 + qo + np.arange(qw)
        k_lo = (qa0 + qo) - 512
        nsub = (qw + 512) // 128
        for u in range(nsub):
            k_abs = k_lo + u * 128 + np.arange(128)
            i = q_abs[None, :]
            j = k_abs[:, None]
            m = (i < j) | (i > j + WINDOW) | (j < 0)
            first = (j == k_lo) & (i < 0)
            m = m & ~first
            cols.append(np.where(m, NEG, np.float32(0.0)))
    return np.ascontiguousarray(np.concatenate(cols, axis=1), dtype=np.float32)


def _mask_cols(li):
    return sum(((qw + 512) // 128) * qw for (qo, qw) in _tiles(NQA[li]))


def _prepare(inputs):
    f32 = lambda a: np.asarray(a, dtype=np.float32)
    x = f32(inputs["L0_attn_x"])
    w_q = f32(inputs["L0_attn_w_q"])
    w_kv = f32(inputs["L0_attn_w_kv"])
    w_out = f32(inputs["L0_attn_w_out"])
    b_out = f32(inputs["L0_attn_b_out"])
    scale = f32(inputs["L0_attn_scale"]).reshape(16)
    wqkv = f32(inputs["L0_ffn_wqkv"])
    qsc = f32(inputs["L0_ffn_q_scale"])
    ksc = f32(inputs["L0_ffn_k_scale"])
    wo = f32(inputs["L0_ffn_wo"])
    ln_g = f32(inputs["L0_ffn_ln_g"])
    ln_b = f32(inputs["L0_ffn_ln_b"])
    w1 = f32(inputs["L0_ffn_w1"])
    b1 = f32(inputs["L0_ffn_b1"])
    w2 = f32(inputs["L0_ffn_w2"])
    b2 = f32(inputs["L0_ffn_b2"])

    def reorder_w1(w1l, b1l):
        a = np.pad(w1l[:, :HID], ((0, 0), (0, HID_PAD - HID)))
        g = np.pad(w1l[:, HID:], ((0, 0), (0, HID_PAD - HID)))
        blocks = []
        for i in range(NHP):
            blocks += [a[:, i * 128:(i + 1) * 128], g[:, i * 128:(i + 1) * 128]]
        w1r = np.ascontiguousarray(np.concatenate(blocks, axis=1), dtype=np.float32)
        ab = np.pad(b1l[:HID], (0, HID_PAD - HID))
        gb = np.pad(b1l[HID:], (0, HID_PAD - HID))
        b1r = np.concatenate([np.concatenate([ab[i * 128:(i + 1) * 128],
                                              gb[i * 128:(i + 1) * 128]])
                              for i in range(NHP)]).astype(np.float32)
        return w1r, np.ascontiguousarray(b1r.reshape(2 * NHP, 128).T)

    w1r_b1c = [reorder_w1(w1[l], b1[l]) for l in range(2)]
    w2p = [np.ascontiguousarray(np.pad(w2[l], ((0, HID_PAD - HID), (0, 0))))
           for l in range(2)]
    colmaj = lambda v: np.ascontiguousarray(v.reshape(-1, 128).T)

    blk2 = np.zeros((128, 2), dtype=np.float32)
    blk2[0:64, 0] = 1.0
    blk2[64:128, 1] = 1.0
    blk2T = np.ascontiguousarray(blk2.T)

    in_maps = []
    for c in range(8):
        b, g = c // 4, c % 4
        s = 2048 * g
        xb = np.ascontiguousarray(x[b].T)
        idx = np.arange(s - T0_L, s + 2048)
        xres = np.ascontiguousarray(xb[:, idx % SEQ] * (idx >= 0))
        m = {
            "xg": xb,
            "x_res": xres,
            "wq_g": np.ascontiguousarray(w_q[:, g * 256:(g + 1) * 256]),
            "wk_g": np.ascontiguousarray(w_kv[:, g * 256:(g + 1) * 256]),
            "wv_g": np.ascontiguousarray(w_kv[:, 1024 + g * 256:1024 + (g + 1) * 256]),
            "w_out": w_out,
            "b_out_c": colmaj(b_out),
            "scale_my": np.ascontiguousarray(scale[g * 4:(g + 1) * 4].reshape(1, 4)),
            "flag": np.full((128, 1), 0.0 if g == 0 else 1.0, dtype=np.float32),
            "blk2": blk2,
            "blk2T": blk2T,
        }
        for l in range(2):
            ka0 = s - KA_L[l]
            Aq, Bq, Ak, Bk = _rotary_tables(np.arange(ka0, s + 2048))
            m[f"rAq{l}"], m[f"rBq{l}"] = Aq, Bq
            m[f"rAk{l}"], m[f"rBk{l}"] = Ak, Bk
            m[f"mask{l}"] = _ffn_masks(s, l)
            m[f"wqkv{l}"] = np.ascontiguousarray(wqkv[l])
            m[f"wo{l}"] = np.ascontiguousarray(wo[l])
            m[f"w1r{l}"] = w1r_b1c[l][0]
            m[f"b1c{l}"] = w1r_b1c[l][1]
            m[f"w2p{l}"] = w2p[l]
            m[f"b2c{l}"] = colmaj(b2[l])
            m[f"lgc{l}"] = colmaj(ln_g[l])
            m[f"lbc{l}"] = colmaj(ln_b[l])
            m[f"qsc{l}"] = np.ascontiguousarray(np.tile(qsc[l], 2).reshape(128, 1))
            m[f"ksc{l}"] = np.ascontiguousarray(np.tile(ksc[l], 2).reshape(128, 1))
            m[f"qsr{l}"] = np.ascontiguousarray(qsc[l].reshape(1, 64))
            m[f"ksr{l}"] = np.ascontiguousarray(ksc[l].reshape(1, 64))
        in_maps.append(m)
    return in_maps



def build_nc(debug_stage=None):
    nc = bass.Bass("TRN2", target_bir_lowering=False, debug=False,
                   detect_race_conditions=False)
    dt = lambda name, shape, kind: nc.dram_tensor(name, shape, F32, kind=kind).ap()
    I = {}
    for nm, shape in [
        ("xg", [D, SEQ]), ("x_res", [D, NT0]),
        ("wq_g", [D, 256]), ("wk_g", [D, 256]), ("wv_g", [D, 256]),
        ("w_out", [D, D]), ("b_out_c", [128, 8]), ("scale_my", [1, 4]),
        ("flag", [128, 1]), ("blk2", [128, 2]), ("blk2T", [2, 128]),
    ]:
        I[nm] = dt(nm, shape, "ExternalInput")
    for l in range(2):
        for nm, shape in [
            ("rAq", [128, NKA[l]]), ("rBq", [128, NKA[l]]),
            ("rAk", [128, NKA[l]]), ("rBk", [128, NKA[l]]),
            ("mask", [128, _mask_cols(l)]),
            ("wqkv", [D, 1536]), ("wo", [512, D]),
            ("w1r", [D, 2 * HID_PAD]), ("b1c", [128, 2 * NHP]),
            ("w2p", [HID_PAD, D]), ("b2c", [128, 8]),
            ("lgc", [128, 8]), ("lbc", [128, 8]),
            ("qsc", [128, 1]), ("ksc", [128, 1]),
            ("qsr", [1, 64]), ("ksr", [1, 64]),
        ]:
            I[(nm, l)] = dt(f"{nm}{l}", shape, "ExternalInput")

    S = {
        "q_scr": dt("q_scr", [256, SEQ], "Internal"),
        "cc_in": dt("cc_in", [256, SEQ], "Internal"),
        "cc_out": dt("cc_out", [D, SEQ], "Internal"),
        "att_w": dt("att_w", [D, NT0], "Internal"),
        "x1_scr": dt("x1_scr", [D, NT0], "Internal"),
        "xmid_scr": dt("xmid_scr", [D, max(NQA)],
                       "ExternalOutput" if debug_stage in ("x2", "l1a") else "Internal"),
        "x2_scr": dt("x2_scr", [D, NML[0]],
                     "ExternalOutput" if debug_stage in ("x2", "l1a") else "Internal"),
        "qr_scr": dt("qr_scr", [512, NKA[0]],
                     "ExternalOutput" if debug_stage in ("x2", "l1a") else "Internal"),
        "kr_scr": dt("kr_scr", [512, NKA[0]],
                     "ExternalOutput" if debug_stage in ("x2", "l1a") else "Internal"),
        "vt_scr": dt("vt_scr", [128, (NKA[0] // 128) * 544],
                     "ExternalOutput" if debug_stage in ("x2", "l1a") else "Internal"),
    }
    out_w = NT0 if debug_stage == "x1" else (NML[0] if debug_stage == "x2" else 2048)
    y_out = dt("y_out", [D, out_w], "ExternalOutput")

    es = contextlib.ExitStack()
    with es:
        block = es.enter_context(nc.Block())
        sems = {}

        def sem(name):
            if name not in sems:
                sems[name] = es.enter_context(nc.semaphore(name))
            return sems[name]

        bases = {}

        def SW(name, rel):
            return bases.get(name, 0) + rel

        def bump(name, total):
            bases[name] = bases.get(name, 0) + total

        ctx = dict(nc=nc, block=block, sem=sem, I=I, S=S, y_out=y_out,
                   dbg=debug_stage, done=[], SW=SW, bump=bump)
        ctx["ones"] = es.enter_context(nc.sbuf_tensor("ones", [128, 128], F32))
        ctx["blk2_sb"] = es.enter_context(nc.sbuf_tensor("blk2_sb", [128, 2], F32))
        ctx["blk2T_sb"] = es.enter_context(nc.sbuf_tensor("blk2T_sb", [2, 128], F32))
        ctx["fl_sb"] = es.enter_context(nc.sbuf_tensor("fl_sb", [128, 1], F32))
        with contextlib.ExitStack() as es_ab:
            emit_phase_AB(ctx, es_ab)
        emit_phase_C(ctx)
        with contextlib.ExitStack() as es_d:
            emit_phase_D(ctx, es_d)
        if debug_stage != "x1":
            for l in range(1 if debug_stage == "x2" else 2):
                with contextlib.ExitStack() as es_q:
                    emit_ffn_qkv(ctx, es_q, l)
                with contextlib.ExitStack() as es_e:
                    emit_ffn_attn(ctx, es_e, l)
                if debug_stage == "l1a" and l == 1:
                    break
                with contextlib.ExitStack() as es_m:
                    emit_ffn_mlp(ctx, es_m, l)
        emit_finish(ctx)
    return nc



def emit_phase_AB(ctx, es):
    nc, block, sem, I, S = ctx["nc"], ctx["block"], ctx["sem"], ctx["I"], ctx["S"]
    sb = lambda nm, shape: es.enter_context(nc.sbuf_tensor(nm, shape, F32))

    k_n = sb("k_n", [128, 2 * SEQ])
    v_tm = sb("v_tm", [128, 64 * 272])
    negm = sb("negm", [128, 4])
    ones = ctx["ones"]
    es_pa = contextlib.ExitStack()
    sba = lambda nm, shape: es_pa.enter_context(nc.sbuf_tensor(nm, shape, F32))
    a_w = sba("a_w", [128, 8 * 3 * 256])
    a_x = sba("a_x", [128, 2 * 8 * 512])
    a_sq = sba("a_sq", [128, 2 * 512])
    a_cp = sba("a_cp", [128, 2 * 512])
    a_ss = sba("a_ss", [2, 2 * 512])
    a_row = sba("a_row", [2, 64])
    a_out = sba("a_out", [128, 2 * 512])
    blk2_sb = ctx["blk2_sb"]
    blk2T_sb = ctx["blk2T_sb"]

    psa = lambda nm, shape: es_pa.enter_context(nc.psum_tensor(nm, shape, F32))
    p_qk = [psa(f"p_qk{i}", [128, 512]) for i in range(4)]
    p_v = [psa(f"p_v{i}", [128, 256]) for i in range(2)]
    p_s = psa("p_s", [2, 512])
    p_b = psa("p_b", [128, 512])

    NT = 16
    xg, q_scr = I["xg"], S["q_scr"]
    a_w4 = a_w[:].rearrange("p (c k n) -> p c k n", k=3, n=256)
    a_x4 = a_x[:].rearrange("p (b c n) -> p b c n", b=2, n=512)

    def grp(t, gi):
        return 8 * t + gi + 1

    @block.sync
    def _(sync):
        for ki, w in enumerate([I["wq_g"], I["wk_g"], I["wv_g"]]):
            sync.dma_start(out=a_w4[:, :, ki, :],
                           in_=w.rearrange("(c p) n -> p c n", p=128)
                           ).then_inc(sem("AW"), 16)
        sync.dma_start(out=a_row[0:1, 0:4], in_=I["scale_my"][0:1, :]
                       ).then_inc(sem("AW"), 16)
        sync.dma_start(out=blk2_sb[:], in_=I["blk2"][:, :]).then_inc(sem("AW"), 16)
        sync.dma_start(out=blk2T_sb[:], in_=I["blk2T"][:, :]).then_inc(sem("AW"), 16)
        sync.dma_start(out=ctx["fl_sb"][:], in_=I["flag"][:, :]).then_inc(sem("AW"), 16)
        for t in range(NT):
            if t >= 2:
                sync.wait_ge(sem("AMM"), grp(t - 2, 7))
            sync.dma_start(
                out=a_x4[:, t % 2],
                in_=xg.rearrange("(c p) s -> p c s", p=128)[:, :, 512 * t:512 * (t + 1)],
            ).then_inc(sem(f"AX{t % 2}"), 16)

    @block.gpsimd
    def _(gpsimd):
        gpsimd.memset(v_tm[:], 1.0)
        gpsimd.memset(ones[:], 1.0).then_inc(sem("AOn"), 1)
        gpsimd.wait_ge(sem("ANg"), 1)
        for j in range(2):
            gpsimd.dma_start(
                out=a_row[j:j + 1, 32:34],
                in_=bass.AP(a_row[0:1, 0:1].tensor, 4 + j, [[64, 1], [2, 2]]),
                allow_slow_non_contiguous=True,
            ).then_inc(sem("AEsc"), 16)
        for t in range(NT):
            for oc in range(2):
                gpsimd.wait_ge(sem("AFin"), 4 * t + oc + 1)
                gpsimd.dma_start(
                    out=q_scr[128 * oc:128 * (oc + 1), 512 * t:512 * (t + 1)],
                    in_=a_out[:, 512 * oc:512 * (oc + 1)],
                ).then_inc(sem(f"AQs{oc}"), 16)

    @block.scalar
    def _(scalar):
        scalar.wait_ge(sem("AW"), 112)
        scalar.activation(a_row[0:1, 4:8], a_row[0:1, 0:4], AF.Exp
                          ).then_inc(sem("ANg"), 1)
        def a_square(t, ci):
            n = 4 * t + ci
            sl2 = 512 * (ci % 2)
            scalar.wait_ge(sem("AMM"), grp(t, ci))
            if n >= 2:
                scalar.wait_ge(sem("ABlk"), n - 1)
                scalar.wait_ge(sem("AFin"), n - 1)
            scalar.activation(a_sq[:, sl2:sl2 + 512], p_qk[ci][:], AF.Square)
            scalar.copy(a_cp[:, sl2:sl2 + 512], p_qk[ci][:]).then_inc(sem("ASq"), 1)

        def a_sqrt(t, ci):
            sl2 = 512 * (ci % 2)
            scalar.wait_ge(sem("AMx"), 4 * t + ci + 1)
            sl = a_ss[:, sl2:sl2 + 512]
            ins = scalar.activation(sl, sl, AF.Sqrt)
            if ci < 2:
                if t == 0 and ci == 0:
                    scalar.wait_ge(sem("AEsc"), 32)
                ins = scalar.mul(sl, sl, a_row[0:2, 32 + ci:33 + ci])
            ins.then_inc(sem("ARs"), 1)

        for t in range(NT):
            for ci in range(4):
                a_square(t, ci)
                if ci >= 1:
                    a_sqrt(t, ci - 1)
            a_sqrt(t, 3)

    @block.vector
    def _(vector):
        vector.wait_ge(sem("ANg"), 1)
        vector.tensor_scalar(a_row[0:1, 8:12], a_row[0:1, 4:8], -1.0, -0.01,
                             ALU.mult, ALU.add).then_inc(sem("ANg"), 1)
        vector.wait_ge(sem("ANg"), 3)
        vector.tensor_copy(negm[:], p_b[:, 0:4]).then_inc(sem("ANg"), 1)
        for t in range(NT):
            for vs in range(4):
                vector.wait_ge(sem("AMM"), grp(t, 4 + vs))
                kc = 4 * t + vs
                vector.tensor_copy(
                    v_tm[:, 272 * kc:272 * (kc + 1)].rearrange(
                        "p (h n) -> p h n", n=68)[:, :, 0:64],
                    p_v[vs % 2][:].rearrange("p (h n) -> p h n", n=64),
                ).then_inc(sem("AVe"), 1)
            def a_mx(ci):
                n = 4 * t + ci
                vector.wait_ge(sem("ABlk"), n + 1)
                if n >= 2:
                    vector.wait_ge(sem("ABc"), n - 1)
                sl = a_ss[:, 512 * (ci % 2):512 * (ci % 2) + 512]
                vector.tensor_scalar_max(sl, p_s[:], 1e-24)
                vector.reciprocal(sl, sl).then_inc(sem("AMx"), 1)

            def a_fin(ci):
                vector.wait_ge(sem("ABc"), 4 * t + ci + 1)
                srcap = a_cp[:, 512 * (ci % 2):512 * (ci % 2) + 512]
                if ci < 2:
                    if t > 0:
                        vector.wait_ge(sem(f"AQs{ci}"), 16 * t)
                    dst = a_out[:, 512 * ci:512 * (ci + 1)]
                else:
                    dst = k_n[:, SEQ * (ci - 2) + 512 * t:SEQ * (ci - 2) + 512 * (t + 1)]
                vector.tensor_tensor(dst, srcap, p_b[:], ALU.mult
                                     ).then_inc(sem("AFin"), 1)

            for ci in range(4):
                a_mx(ci)
                if ci >= 1:
                    a_fin(ci - 1)
            a_fin(3)

    @block.tensor
    def _(tensor):
        tensor.wait_ge(sem("AW"), 112)
        tensor.wait_ge(sem("AOn"), 1)
        tensor.wait_ge(sem("ANg"), 2)
        tensor.matmul(p_b[:, 0:4], ones[0:1, :], a_row[0:1, 8:12],
                      start=True, stop=True).then_inc(sem("ANg"), 1)
        for t in range(NT):
            tensor.wait_ge(sem(f"AX{t % 2}"), 16 * (t // 2 + 1))
            xt = a_x4[:, t % 2]
            for ci in range(4):
                ki = 0 if ci < 2 else 1
                base = 128 * (ci % 2)
                if t > 0:
                    tensor.wait_ge(sem("AFin"), 4 * (t - 1) + ci + 1)
                for dc in range(8):
                    mm = tensor.matmul(p_qk[ci][:], a_w4[:, dc, ki, base:base + 128],
                                       xt[:, dc, :], start=(dc == 0), stop=(dc == 7))
                mm.then_inc(sem("AMM"), 1)
            for vs in range(4):
                n = 4 * t + vs
                if n >= 2:
                    tensor.wait_ge(sem("AVe"), n - 1)
                for dc in range(8):
                    mm = tensor.matmul(p_v[vs % 2][:], xt[:, dc, 128 * vs:128 * (vs + 1)],
                                       a_w4[:, dc, 2, :], start=(dc == 0), stop=(dc == 7))
                mm.then_inc(sem("AMM"), 1)
            def a_blk(ci):
                tensor.wait_ge(sem("ASq"), 4 * t + ci + 1)
                if 4 * t + ci > 0:
                    tensor.wait_ge(sem("AMx"), 4 * t + ci)
                tensor.matmul(p_s[:], blk2_sb[:, 0:2],
                              a_sq[:, 512 * (ci % 2):512 * (ci % 2) + 512],
                              start=True, stop=True).then_inc(sem("ABlk"), 1)

            def a_bc(ci):
                tensor.wait_ge(sem("ARs"), 4 * t + ci + 1)
                if t == 0 and ci == 0:
                    tensor.wait_ge(sem("ANg"), 4)
                else:
                    tensor.wait_ge(sem("AFin"), 4 * t + ci)
                tensor.matmul(p_b[:], blk2T_sb[0:2, :],
                              a_ss[:, 512 * (ci % 2):512 * (ci % 2) + 512],
                              start=True, stop=True).then_inc(sem("ABc"), 1)

            for ci in range(4):
                a_blk(ci)
                if ci >= 1:
                    a_bc(ci - 1)
            a_bc(3)

    es_pa.close()
    ctx["k_n"], ctx["v_tm"], ctx["negm"] = k_n, v_tm, negm
    emit_phase_B(ctx, es)



def emit_phase_B(ctx, es):
    nc, block, sem = ctx["nc"], ctx["block"], ctx["sem"]
    k_n, v_tm, negm, ones = ctx["k_n"], ctx["v_tm"], ctx["negm"], ctx["ones"]
    q_scr, cc_in = ctx["S"]["q_scr"], ctx["S"]["cc_in"]
    sb = lambda nm, shape: es.enter_context(nc.sbuf_tensor(nm, shape, F32))

    qld = sb("b_qld", [128, 2 * 512])
    pt = sb("b_pt", [128, 4 * 1536])
    oo = sb("b_oo", [128, 2 * 512])
    otmp = sb("b_otmp", [128, 2 * 512])
    orc = sb("b_orc", [128, 2 * 512])

    es_pb = contextlib.ExitStack()
    psb = lambda nm, shape: es_pb.enter_context(nc.psum_tensor(nm, shape, F32))
    ps_s = [psb("b_s0", [128, 1536]), psb("b_s1", [128, 1536])]
    po = [psb("b_o0", [128, 512]), psb("b_o1", [128, 512])]

    GROUPS = [3] * 21 + [1]
    NG = len(GROUPS)
    NIT = 32

    def base(it):
        return it * NG

    def o_mms(tensor, it, p, g, kc0, gw):
        for u in range(gw):
            kc = kc0 + u
            for j in range(2):
                mm = tensor.matmul(
                    po[j][0:65, :],
                    v_tm[:, 272 * kc + 68 * (2 * p + j):272 * kc + 68 * (2 * p + j) + 65],
                    pt[:, 1536 * (2 * j + g % 2) + 512 * u:
                       1536 * (2 * j + g % 2) + 512 * (u + 1)],
                    start=(kc == 0), stop=(kc == 63))
        mm.then_inc(sem("bO"), 1)

    @block.sync
    def _(sync):
        for it in range(NIT):
            qt, p = it // 2, it % 2
            if it == 0:
                sync.wait_ge(sem("AQs0"), 16 * 16)
                sync.wait_ge(sem("AQs1"), 16 * 16)
            if it >= 2:
                sync.wait_ge(sem("bS"), base(it - 2) + NG)
            sync.dma_start(
                out=qld[:, 512 * (it % 2):512 * (it % 2 + 1)],
                in_=q_scr[128 * p:128 * (p + 1), 512 * qt:512 * (qt + 1)],
            ).then_inc(sem(f"bQ{it % 2}"), 16)

    @block.tensor
    def _(tensor):
        tensor.wait_ge(sem("AFin"), 64)
        tensor.wait_ge(sem("AVe"), 64)
        for it in range(NIT):
            qt, p = it // 2, it % 2
            q_ap = qld[:, 512 * (it % 2):512 * (it % 2 + 1)]
            tensor.wait_ge(sem(f"bQ{it % 2}"), 16 * (it // 2 + 1))
            if it > 0:
                tensor.wait_ge(sem("bT"), 2 * it)
            kc = 0
            for g, gw in enumerate(GROUPS):
                if g > 0:
                    tensor.wait_ge(sem("bE"), 2 * base(it) + 2 * g)
                for u in range(gw):
                    for j in range(2):
                        mm = tensor.matmul(
                            ps_s[j][:, 512 * u:512 * (u + 1)],
                            k_n[64 * j:64 * (j + 1),
                                SEQ * p + 128 * (kc + u):SEQ * p + 128 * (kc + u + 1)],
                            q_ap[64 * j:64 * (j + 1), :],
                            start=True, stop=True, tile_position=(64 * j, 0))
                mm.then_inc(sem("bS"), 1)
                if g > 0:
                    o_mms(tensor, it, p, g - 1, kc - GROUPS[g - 1], GROUPS[g - 1])
                kc += gw
            tensor.wait_ge(sem("bE"), 2 * base(it) + 2 * NG)
            o_mms(tensor, it, p, NG - 1, kc - GROUPS[-1], GROUPS[-1])
            for j in range(2):
                tensor.wait_ge(sem("bR"), 2 * it + j + 1)
                tensor.matmul(ps_s[0][:, 512 * j:512 * (j + 1)],
                              ones[64:65, :], orc[64:65, 512 * j:512 * (j + 1)],
                              start=True, stop=True,
                              tile_position=(64, 0)).then_inc(sem("bB"), 1)

    @block.scalar
    def _(scalar):
        for it in range(NIT):
            p = it % 2
            for g, gw in enumerate(GROUPS):
                G = base(it) + g
                scalar.wait_ge(sem("bS"), G + 1)
                if G >= 2:
                    scalar.wait_ge(sem("bO"), G - 1)
                for j in range(2):
                    scalar.activation(
                        pt[:, 1536 * (2 * j + g % 2):1536 * (2 * j + g % 2) + 512 * gw],
                        ps_s[j][:, 0:512 * gw], AF.Exp,
                        bias=negm[:, 2 * p + j:2 * p + j + 1],
                    ).then_inc(sem("bE"), 1)

    @block.vector
    def _(vector):
        for it in range(NIT):
            vector.wait_ge(sem("bO"), base(it) + NG)
            for j in range(2):
                vector.reciprocal(orc[64:65, 512 * j:512 * (j + 1)],
                                  po[j][64:65, 0:512]).then_inc(sem("bR"), 1)
            for j in range(2):
                vector.wait_ge(sem("bB"), 2 * it + j + 1)
                if it > 0:
                    vector.wait_ge(sem("bSt"), 32 * it)
                vector.tensor_copy(otmp[0:64, 512 * j:512 * (j + 1)],
                                   po[j][0:64, :])
                vector.tensor_tensor(oo[0:64, 512 * j:512 * (j + 1)],
                                     otmp[0:64, 512 * j:512 * (j + 1)],
                                     ps_s[0][0:64, 512 * j:512 * (j + 1)],
                                     ALU.mult).then_inc(sem("bT"), 1)

    @block.gpsimd
    def _(gpsimd):
        for it in range(NIT):
            qt, p = it // 2, it % 2
            for j in range(2):
                gpsimd.wait_ge(sem("bT"), 2 * it + j + 1)
                hh = 2 * p + j
                gpsimd.dma_start(
                    out=cc_in[64 * hh:64 * (hh + 1), 512 * qt:512 * (qt + 1)],
                    in_=oo[0:64, 512 * j:512 * (j + 1)],
                ).then_inc(sem("bSt"), 16)

    es_pb.close()



def emit_phase_C(ctx):
    block, sem, S = ctx["block"], ctx["sem"], ctx["S"]
    cc_in, cc_out, att_w = S["cc_in"], S["cc_out"], S["att_w"]

    @block.gpsimd
    def _(gpsimd):
        gpsimd.wait_ge(sem("bSt"), 32 * 32)
        gpsimd.collective_compute(
            "AllGather", ALU.bypass,
            replica_groups=[[0, 1, 2, 3], [4, 5, 6, 7]],
            ins=[cc_in[:]], outs=[cc_out[:]],
        ).then_inc(sem("cAG"), 1)
        gpsimd.wait_ge(sem("cAG"), 1)
        for arm in gpsimd.switch_core_id(n=128):
            g = (arm.logical % 8) % 4
            s = 2048 * g
            lo = s - T0_L
            if lo < 0:
                gpsimd.dma_start(out=att_w[:, 0:-lo],
                                 in_=cc_out[:, SEQ + lo:SEQ]).then_inc(sem("cW"), 16)
                gpsimd.dma_start(out=att_w[:, -lo:NT0],
                                 in_=cc_out[:, 0:s + 2048]).then_inc(sem("cW"), 16)
            else:
                mid = NT0 // 2
                gpsimd.dma_start(out=att_w[:, 0:mid],
                                 in_=cc_out[:, lo:lo + mid]).then_inc(sem("cW"), 16)
                gpsimd.dma_start(out=att_w[:, mid:NT0],
                                 in_=cc_out[:, lo + mid:s + 2048]).then_inc(sem("cW"), 16)



def emit_phase_D(ctx, es):
    nc, block, sem, I, S = ctx["nc"], ctx["block"], ctx["sem"], ctx["I"], ctx["S"]
    att_w, x1_scr = S["att_w"], S["x1_scr"]
    y_out, dbg = ctx["y_out"], ctx["dbg"]
    sb = lambda nm, shape: es.enter_context(nc.sbuf_tensor(nm, shape, F32))

    d_w = sb("d_w", [128, 8 * 1024])
    d_bo = sb("d_bo", [128, 8])
    d_at = sb("d_at", [128, 2 * 8 * 512])
    d_xr = sb("d_xr", [128, 2 * 8 * 512])
    d_y = sb("d_y", [128, 8 * 512])
    d_ysq = sb("d_ysq", [128, 8 * 512])
    d_t1 = sb("d_t1", [128, 8 * 512])
    d_x1 = sb("d_x1", [128, 8 * 512])
    d_row = sb("d_row", [1, 2048])
    ones = ctx["ones"]

    psd = lambda nm, shape: es.enter_context(nc.psum_tensor(nm, shape, F32))
    p_d = [psd("p_d0", [128, 512]), psd("p_d1", [128, 512])]
    p_s1 = psd("p_s1", [1, 512])
    p_s2 = psd("p_s2", [1, 512])
    p_bm = psd("p_bm", [128, 512])
    p_br = psd("p_br", [128, 512])

    NTD = 7
    d_at4 = d_at[:].rearrange("p (b c n) -> p b c n", b=2, n=512)
    d_xr4 = d_xr[:].rearrange("p (b c n) -> p b c n", b=2, n=512)
    d_y3 = d_y[:].rearrange("p (c n) -> p c n", n=512)
    d_ysq3 = d_ysq[:].rearrange("p (c n) -> p c n", n=512)
    d_t13 = d_t1[:].rearrange("p (c n) -> p c n", n=512)
    d_x13 = d_x1[:].rearrange("p (c n) -> p c n", n=512)
    d_w3 = d_w[:].rearrange("p (c n) -> p c n", n=1024)

    apc = [8 + (1 if j == 2 else 0) for j in range(NTD)]
    apcum = [sum(apc[:j + 1]) for j in range(NTD)]

    @block.sync
    def _(sync):
        sync.wait_ge(sem("bSt"), 32 * 32)
        sync.dma_start(out=d_w3, in_=I["w_out"].rearrange("(c p) n -> p c n", p=128)
                       ).then_inc(sem("dW"), 16)
        sync.dma_start(out=d_bo[:], in_=I["b_out_c"][:, :]).then_inc(sem("dW"), 16)
        sync.dma_start(out=d_bo[:, 0:8], in_=I["b_out_c"][:, :]).then_inc(sem("dW"), 16) if False else None
        sync.nop() if False else None
        for j in range(NTD):
            if j == 0:
                sync.wait_ge(sem("cW"), 32)
            if j >= 2:
                sync.wait_ge(sem("dMM"), 8 * (j - 1))
                sync.wait_ge(sem("dY"), 8 * (j - 1))
            sync.dma_start(
                out=d_at4[:, j % 2],
                in_=att_w.rearrange("(c p) n -> p c n", p=128)[:, :, 512 * j:512 * (j + 1)],
            ).then_inc(sem(f"dA{j % 2}"), 16)
            sync.dma_start(
                out=d_xr4[:, j % 2],
                in_=I["x_res"].rearrange("(c p) n -> p c n", p=128)[:, :, 512 * j:512 * (j + 1)],
            ).then_inc(sem(f"dA{j % 2}"), 16)

    @block.tensor
    def _(tensor):
        tensor.wait_ge(sem("dW"), 32)
        for j in range(NTD):
            tensor.wait_ge(sem(f"dA{j % 2}"), 32 * (j // 2 + 1))
            at = d_at4[:, j % 2]
            for dc in range(8):
                n = 8 * j + dc
                if n >= 2:
                    tensor.wait_ge(sem("dY"), n - 1)
                for ic in range(8):
                    mm = tensor.matmul(p_d[dc % 2][:],
                                       d_w3[:, ic, 128 * dc:128 * (dc + 1)],
                                       at[:, ic, :],
                                       start=(ic == 0), stop=(ic == 7))
                mm.then_inc(sem("dMM"), 1)
            tensor.wait_ge(sem("dY"), 8 * (j + 1))
            if j > 0:
                tensor.wait_ge(sem("dRX"), 4 * j)
            for dc in range(8):
                mm = tensor.matmul(p_s1[:], ones[:, 0:1], d_y3[:, dc, :],
                                   start=(dc == 0), stop=(dc == 7))
            mm.then_inc(sem("dS1"), 1)
            tensor.wait_ge(sem("dSq"), j + 1)
            for dc in range(8):
                mm = tensor.matmul(p_s2[:], ones[:, 0:1], d_ysq3[:, dc, :],
                                   start=(dc == 0), stop=(dc == 7))
            mm.then_inc(sem("dS2"), 1)
            tensor.wait_ge(sem("dRX"), 4 * j + 4)
            if j > 0:
                tensor.wait_ge(sem("dAp"), apcum[j - 1])
            tensor.matmul(p_bm[:], ones[0:1, :], d_row[0:1, 0:512],
                          start=True, stop=True)
            tensor.matmul(p_br[:], ones[0:1, :], d_row[0:1, 512:1024],
                          start=True, stop=True).then_inc(sem("dBc"), 1)

    @block.scalar
    def _(scalar):
        for j in range(NTD):
            scalar.wait_ge(sem("dY"), 8 * (j + 1))
            if j > 0:
                scalar.wait_ge(sem("dS2"), j)
            scalar.activation(d_ysq[:], d_y[:], AF.Square).then_inc(sem("dSq"), 1)
            scalar.wait_ge(sem("dRX"), 4 * j + 1)
            scalar.activation(d_row[0:1, 1024:1536], d_row[0:1, 0:512], AF.Square
                              ).then_inc(sem("dRX"), 1)
            scalar.wait_ge(sem("dRX"), 4 * j + 3)
            scalar.activation(d_row[0:1, 512:1024], d_row[0:1, 512:1024], AF.Sqrt
                              ).then_inc(sem("dRX"), 1)

    @block.vector
    def _(vector):
        for j in range(NTD):
            for dc in range(8):
                vector.wait_ge(sem("dMM"), 8 * j + dc + 1)
                if j > 0 and dc == 0:
                    vector.wait_ge(sem("dS1"), j)
                    vector.wait_ge(sem("dSq"), j)
                vector.scalar_tensor_tensor(
                    d_y3[:, dc, :], p_d[dc % 2][:], d_bo[:, dc:dc + 1],
                    d_xr4[:, j % 2, dc, :], ALU.add, ALU.add,
                ).then_inc(sem("dY"), 1)
            vector.wait_ge(sem("dS1"), j + 1)
            vector.tensor_scalar_mul(d_row[0:1, 0:512], p_s1[0:1, :], 1.0 / 1024.0
                                     ).then_inc(sem("dRX"), 1)
            vector.wait_ge(sem("dS2"), j + 1)
            vector.tensor_scalar_mul(d_row[0:1, 1536:2048], p_s2[0:1, :], 1.0 / 1024.0)
            vector.wait_ge(sem("dRX"), 4 * j + 2)
            vector.tensor_sub(d_row[0:1, 512:1024], d_row[0:1, 1536:2048],
                              d_row[0:1, 1024:1536])
            vector.tensor_scalar_add(d_row[0:1, 512:1024], d_row[0:1, 512:1024], 1e-5)
            vector.reciprocal(d_row[0:1, 512:1024], d_row[0:1, 512:1024]
                              ).then_inc(sem("dRX"), 1)
            vector.wait_ge(sem("dBc"), j + 1)
            if j > 0:
                vector.wait_ge(sem("dX1"), 16 * j)
                if dbg == "x1":
                    vector.wait_ge(sem("dYo"), 16 * j)
            for dc in range(8):
                vector.tensor_sub(d_t13[:, dc, :], d_y3[:, dc, :], p_bm[:])
                vector.tensor_tensor(d_x13[:, dc, :], d_t13[:, dc, :], p_br[:],
                                     ALU.mult).then_inc(sem("dAp"), 1)
            if j == 2:
                vector.tensor_scalar_mul(
                    d_x1[:].rearrange("p (c n) -> p c n", n=512)[:, :, 511:512],
                    d_x1[:].rearrange("p (c n) -> p c n", n=512)[:, :, 511:512],
                    ctx["fl_sb"][:, 0:1]).then_inc(sem("dAp"), 1)

    @block.gpsimd
    def _(gpsimd):
        for j in range(NTD):
            gpsimd.wait_ge(sem("dAp"), apcum[j])
            gpsimd.dma_start(
                out=x1_scr.rearrange("(c p) n -> p c n", p=128)[:, :, 512 * j:512 * (j + 1)],
                in_=d_x13[:, :, :],
            ).then_inc(sem("dX1"), 16)
            if dbg == "x1":
                gpsimd.dma_start(
                    out=y_out.rearrange("(c p) n -> p c n", p=128)[:, :, 512 * j:512 * (j + 1)],
                    in_=d_x13[:, :, :],
                ).then_inc(sem("dYo"), 16)

    if dbg == "x1":
        ctx["done"].append(("dYo", 16 * NTD))
    ctx["done"].append(("dX1", 16 * NTD))



def emit_ffn_qkv(ctx, es, l):
    nc, block, sem, I, S = ctx["nc"], ctx["block"], ctx["sem"], ctx["I"], ctx["S"]
    SW, bump = ctx["SW"], ctx["bump"]
    sb = lambda nm, shape: es.enter_context(nc.sbuf_tensor(f"L{l}{nm}", shape, F32))
    ps = lambda nm, shape: es.enter_context(nc.psum_tensor(f"L{l}{nm}", shape, F32))

    x_src = S["x1_scr"] if l == 0 else S["x2_scr"]
    X0 = (T0_L - KA_L[0]) if l == 0 else (ML_L[0] - KA_L[1])
    TK = _tiles(NKA[l])
    NT = len(TK)

    xt = sb("qXt", [128, 8 * 513])
    tbl = sb("qTbl", [128, 4 * 512])
    wqc = sb("qWqc", [128, 2 * 8 * 128])
    wv = sb("qWv", [128, 8 * 512])
    qfb = sb("qQfb", [128, 4 * 512])
    kfb = sb("qKfb", [128, 4 * 512])
    qsw = sb("qQsw", [128, 4 * 512])
    ksw = sb("qKsw", [128, 4 * 512])
    vtb = sb("qVtb", [128, 4 * 544])
    rsc = sb("qRsc", [128, 512])
    sq_s = sb("qSqs", [128, 2 * 512])
    cp_s = sb("qCps", [128, 2 * 512])
    ss_s = sb("qSss", [2, 2 * 512])
    scl = sb("qScl", [128, 2])

    p_qk = [ps("qPqk0", [128, 512]), ps("qPqk1", [128, 512])]
    p_s = ps("qPs", [2, 512])
    p_b = ps("qPb", [128, 512])

    xt3 = xt[:].rearrange("p (c n) -> p c n", n=513)
    wqc4 = wqc[:].rearrange("p (b c n) -> p b c n", b=2, n=128)
    wv3 = wv[:].rearrange("p (c n) -> p c n", n=512)
    blk2_sb, blk2T_sb = ctx["blk2_sb"], ctx["blk2T_sb"]

    if l == 0:
        gates = [("dX1", 16 * 7, False)]
    else:
        gates = [("mX2", 0, True)]

    FMN, GS, CONS = {}, {}, {}
    n = g = nv = 0
    for i, (o, w) in enumerate(TK):
        for ci in range(8):
            FMN[(i, ci)] = n
            GS[(i, ("fm", ci))] = g
            CONS[g] = ("qSQ", n)
            n += 1
            g += 1
        for u in range(w // 128):
            GS[(i, ("v", u))] = g
            CONS[g] = ("qVE", nv)
            nv += 1
            g += 1
    N_FM, N_G, N_V = n, g, nv
    FMG = [GS[(i, ("fm", ci))] for i in range(NT) for ci in range(8)]
    VTOT = [sum(TK[j][1] // 128 for j in range(i + 1)) for i in range(NT)]

    @block.sync
    def _(sync):
        for gs, gv, rel in gates:
            sync.wait_ge(sem(gs), SW(gs, gv) if rel else gv)
        sync.dma_start(out=wv3, in_=I[("wqkv", l)].rearrange("(c p) n -> p c n", p=128)
                       [:, :, 1024:1536]).then_inc(sem("qW"), 16)
        sync.dma_start(out=scl[:, 0:1], in_=I[("qsc", l)][:, :]).then_inc(sem("qW"), 16)
        sync.dma_start(out=scl[:, 1:2], in_=I[("ksc", l)][:, :]).then_inc(sem("qW"), 16)
        wq = 0
        for i, (o, w) in enumerate(TK):
            if i >= 1:
                sync.wait_ge(sem("qMM"), SW("qMM", GS[(i - 1, ("v", TK[i - 1][1] // 128 - 1))] + 1))
            sync.dma_start(
                out=xt3[:, :, 0:w + 1],
                in_=x_src.rearrange("(c p) n -> p c n", p=128)[:, :, X0 + o - 1:X0 + o + w],
            ).then_inc(sem("qX"), 16)
            if i >= 1:
                sync.wait_ge(sem("qRotD"), SW("qRotD", 8 * i))
            for ti, nm in enumerate(["rAq", "rBq", "rAk", "rBk"]):
                sync.dma_start(out=tbl[:, 512 * ti:512 * ti + w],
                               in_=I[(nm, l)][:, o:o + w]).then_inc(sem("qT"), 16)
            for ci in range(8):
                if wq >= 2:
                    sync.wait_ge(sem("qMM"), SW("qMM", FMG[wq - 2] + 1))
                sync.dma_start(
                    out=wqc4[:, wq % 2, :, :],
                    in_=I[("wqkv", l)].rearrange("(c p) n -> p c n", p=128)
                    [:, :, 128 * ci:128 * (ci + 1)],
                ).then_inc(sem(f"qWQ{wq % 2}"), 16)
                wq += 1

    @block.gpsimd
    def _(gpsimd):
        for gs, gv, rel in gates:
            gpsimd.wait_ge(sem(gs), SW(gs, gv) if rel else gv)
        gpsimd.memset(vtb[:], 1.0)
        for i, (o, w) in enumerate(TK):
            nsub = w // 128
            gpsimd.wait_ge(sem("qFIN"), SW("qFIN", 8 * i + 4))
            for blk in range(4):
                so = 32 * ((blk + 1) if blk % 2 == 0 else (blk - 1))
                gpsimd.dma_start(out=qsw[32 * blk:32 * (blk + 1), :],
                                 in_=qfb[so:so + 32, :]).then_inc(sem("qSWQ"), 16)
            gpsimd.wait_ge(sem("qFIN"), SW("qFIN", 8 * i + 8))
            for blk in range(4):
                so = 32 * ((blk + 1) if blk % 2 == 0 else (blk - 1))
                gpsimd.dma_start(out=ksw[32 * blk:32 * (blk + 1), :],
                                 in_=kfb[so:so + 32, :]).then_inc(sem("qSWK"), 16)
            gpsimd.wait_ge(sem("qRotD"), SW("qRotD", 8 * i + 4))
            gpsimd.dma_start(
                out=S["qr_scr"].rearrange("(c p) n -> p c n", p=128)[:, :, o:o + w],
                in_=qfb[:].rearrange("p (c n) -> p c n", n=512)[:, :, 0:w],
            ).then_inc(sem("qQS"), 16)
            gpsimd.wait_ge(sem("qRotD"), SW("qRotD", 8 * i + 8))
            gpsimd.dma_start(
                out=S["kr_scr"].rearrange("(c p) n -> p c n", p=128)[:, :, o:o + w],
                in_=kfb[:].rearrange("p (c n) -> p c n", n=512)[:, :, 0:w],
            ).then_inc(sem("qKS"), 16)
            gpsimd.wait_ge(sem("qVE"), SW("qVE", VTOT[i]))
            gpsimd.dma_start(
                out=S["vt_scr"][:, 544 * (o // 128):544 * (o // 128 + nsub)],
                in_=vtb[:, 0:544 * nsub],
            ).then_inc(sem("qVS"), 16)

    @block.tensor
    def _(tensor):
        tensor.wait_ge(sem("qW"), SW("qW", 48))
        wq = 0
        for i, (o, w) in enumerate(TK):
            nsub = w // 128
            tensor.wait_ge(sem("qX"), SW("qX", 16 * (i + 1)))

            def t_blk(ci):
                nn = FMN[(i, ci)]
                tensor.wait_ge(sem("qSQ"), SW("qSQ", nn + 1))
                if nn > 0:
                    tensor.wait_ge(sem("qMX"), SW("qMX", nn))
                tensor.matmul(p_s[:, 0:w], blk2_sb[:, 0:2],
                              sq_s[:, 512 * (ci % 2):512 * (ci % 2) + w],
                              start=True, stop=True).then_inc(sem("qBLK"), 1)

            def t_bc(ci):
                nn = FMN[(i, ci)]
                tensor.wait_ge(sem("qRS"), SW("qRS", nn + 1))
                if nn > 0:
                    tensor.wait_ge(sem("qFIN"), SW("qFIN", nn))
                tensor.matmul(p_b[:, 0:w], blk2T_sb[0:2, :],
                              ss_s[:, 512 * (ci % 2):512 * (ci % 2) + w],
                              start=True, stop=True).then_inc(sem("qBC"), 1)

            for ci in range(8):
                g = GS[(i, ("fm", ci))]
                tensor.wait_ge(sem(f"qWQ{wq % 2}"), SW(f"qWQ{wq % 2}", 16 * (wq // 2 + 1)))
                if g >= 2:
                    kind, idx = CONS[g - 2]
                    tensor.wait_ge(sem(kind), SW(kind, idx + 1))
                for dc in range(8):
                    rhs = xt3[:, dc, 1:1 + w] if dc < 4 else xt3[:, dc, 0:w]
                    mm = tensor.matmul(p_qk[g % 2][:, 0:w], wqc4[:, wq % 2, dc, :],
                                       rhs, start=(dc == 0), stop=(dc == 7))
                mm.then_inc(sem("qMM"), 1)
                wq += 1
                t_blk(ci)
                if ci >= 1:
                    t_bc(ci - 1)
            t_bc(7)
            for u in range(nsub):
                g = GS[(i, ("v", u))]
                if g >= 2:
                    kind, idx = CONS[g - 2]
                    tensor.wait_ge(sem(kind), SW(kind, idx + 1))
                for dc in range(8):
                    lhs = xt3[:, dc, 1 + 128 * u:1 + 128 * (u + 1)] if dc < 4 \
                        else xt3[:, dc, 128 * u:128 * (u + 1)]
                    mm = tensor.matmul(p_qk[g % 2][:], lhs, wv3[:, dc, :],
                                       start=(dc == 0), stop=(dc == 7))
                mm.then_inc(sem("qMM"), 1)

    @block.scalar
    def _(scalar):
        for i, (o, w) in enumerate(TK):
            def s_sqrt(ci):
                nn = FMN[(i, ci)]
                scalar.wait_ge(sem("qMX"), SW("qMX", nn + 1))
                sl = ss_s[:, 512 * (ci % 2):512 * (ci % 2) + w]
                scalar.activation(sl, sl, AF.Sqrt).then_inc(sem("qRS"), 1)

            for ci in range(8):
                nn = FMN[(i, ci)]
                g = GS[(i, ("fm", ci))]
                scalar.wait_ge(sem("qMM"), SW("qMM", g + 1))
                if nn >= 2:
                    scalar.wait_ge(sem("qBLK"), SW("qBLK", nn - 1))
                    scalar.wait_ge(sem("qFIN"), SW("qFIN", nn - 1))
                sl2 = 512 * (ci % 2)
                scalar.activation(sq_s[:, sl2:sl2 + w], p_qk[g % 2][:, 0:w], AF.Square)
                scalar.copy(cp_s[:, sl2:sl2 + w], p_qk[g % 2][:, 0:w]
                            ).then_inc(sem("qSQ"), 1)
                if ci >= 1:
                    s_sqrt(ci - 1)
            s_sqrt(7)

    @block.vector
    def _(vector):
        for i, (o, w) in enumerate(TK):
            nsub = w // 128

            def v_mx(ci):
                nn = FMN[(i, ci)]
                vector.wait_ge(sem("qBLK"), SW("qBLK", nn + 1))
                if nn >= 2:
                    vector.wait_ge(sem("qBC"), SW("qBC", nn - 1))
                sl = ss_s[:, 512 * (ci % 2):512 * (ci % 2) + w]
                vector.tensor_scalar_max(sl, p_s[:, 0:w], 1e-24)
                if ci < 4:
                    vector.tensor_scalar_mul(sl, sl, 1.0 / 64.0)
                vector.reciprocal(sl, sl).then_inc(sem("qMX"), 1)

            def v_fin(ci):
                nn = FMN[(i, ci)]
                vector.wait_ge(sem("qBC"), SW("qBC", nn + 1))
                srcap = cp_s[:, 512 * (ci % 2):512 * (ci % 2) + w]
                if ci < 4:
                    if i > 0:
                        vector.wait_ge(sem("qQS"), SW("qQS", 16 * i))
                    dst = qfb[:, 512 * ci:512 * ci + w]
                    sc = scl[:, 0:1]
                else:
                    if i > 0:
                        vector.wait_ge(sem("qKS"), SW("qKS", 16 * i))
                    dst = kfb[:, 512 * (ci - 4):512 * (ci - 4) + w]
                    sc = scl[:, 1:2]
                vector.scalar_tensor_tensor(dst, srcap, sc, p_b[:, 0:w],
                                            ALU.mult, ALU.mult
                                            ).then_inc(sem("qFIN"), 1)

            for ci in range(8):
                v_mx(ci)
                if ci >= 1:
                    v_fin(ci - 1)
            v_fin(7)
            for u in range(nsub):
                g = GS[(i, ("v", u))]
                vector.wait_ge(sem("qMM"), SW("qMM", g + 1))
                if i > 0 and u == 0:
                    vector.wait_ge(sem("qVS"), SW("qVS", 16 * i))
                vector.tensor_copy(
                    vtb[:, 544 * u:544 * (u + 1)].rearrange(
                        "p (h e) -> p h e", e=68)[:, :, 0:64],
                    p_qk[g % 2][:].rearrange("p (h e) -> p h e", e=64),
                ).then_inc(sem("qVE"), 1)
            vector.wait_ge(sem("qSWQ"), SW("qSWQ", 64 * (i + 1)))
            vector.wait_ge(sem("qT"), SW("qT", 64 * (i + 1)))
            for ci in range(4):
                q_ap = qfb[:, 512 * ci:512 * ci + w]
                vector.tensor_mul(rsc[:, 0:w], q_ap, tbl[:, 0:w])
                vector.tensor_mul(q_ap, qsw[:, 512 * ci:512 * ci + w],
                                  tbl[:, 512:512 + w])
                vector.tensor_add(q_ap, q_ap, rsc[:, 0:w]).then_inc(sem("qRotD"), 1)
            vector.wait_ge(sem("qSWK"), SW("qSWK", 64 * (i + 1)))
            for ci in range(4):
                k_ap = kfb[:, 512 * ci:512 * ci + w]
                vector.tensor_mul(rsc[:, 0:w], k_ap, tbl[:, 1024:1024 + w])
                vector.tensor_mul(k_ap, ksw[:, 512 * ci:512 * ci + w],
                                  tbl[:, 1536:1536 + w])
                vector.tensor_add(k_ap, k_ap, rsc[:, 0:w]).then_inc(sem("qRotD"), 1)

    bump("qW", 48)
    bump("qX", 16 * NT)
    bump("qT", 64 * NT)
    bump("qWQ0", 16 * ((8 * NT + 1) // 2))
    bump("qWQ1", 16 * (8 * NT // 2))
    bump("qMM", N_G)
    bump("qSQ", N_FM)
    bump("qBLK", N_FM)
    bump("qMX", N_FM)
    bump("qRS", N_FM)
    bump("qBC", N_FM)
    bump("qFIN", N_FM)
    bump("qVE", N_V)
    bump("qSWQ", 64 * NT)
    bump("qSWK", 64 * NT)
    bump("qRotD", 8 * NT)
    bump("qQS", 16 * NT)
    bump("qKS", 16 * NT)
    bump("qVS", 16 * NT)
    ctx["q_tot"] = dict(NT=NT)


def emit_ffn_attn(ctx, es, l):
    nc, block, sem, I, S = ctx["nc"], ctx["block"], ctx["sem"], ctx["I"], ctx["S"]
    SW, bump = ctx["SW"], ctx["bump"]
    ones = ctx["ones"]
    sb = lambda nm, shape: es.enter_context(nc.sbuf_tensor(f"L{l}{nm}", shape, F32))
    ps = lambda nm, shape: es.enter_context(nc.psum_tensor(f"L{l}{nm}", shape, F32))

    x_src = S["x1_scr"] if l == 0 else S["x2_scr"]
    X0 = (T0_L - KA_L[0]) if l == 0 else (ML_L[0] - KA_L[1])
    xmid = S["xmid_scr"]
    QT = _tiles(NQA[l])
    NQT = len(QT)
    NT_Q = len(_tiles(NKA[l]))

    q_sb = sb("aQ", [128, 4 * 512])
    k_sb = sb("aK", [128, 4 * 1024])
    vt_sb = sb("aVt", [128, 8 * 544])
    msk = sb("aMsk", [128, 8 * 512])
    xt = sb("aXt", [128, 8 * 512])
    wos = sb("aWos", [128, 4 * 1024])
    ptE = sb("aPtE", [128, 4 * 1024])
    of = sb("aOf", [128, 4 * 512])
    otm = sb("aOtm", [128, 2 * 512])
    orc = sb("aOrc", [128, 2 * 512])
    xm = sb("aXm", [128, 8 * 512])
    ngM = sb("aNgM", [128, 1])
    row = sb("aRow", [1, 256])

    p_se = [ps("aPse0", [128, 1024]), ps("aPse1", [128, 1024])]
    p_oe = [ps("aPoe0", [128, 512]), ps("aPoe1", [128, 512])]
    p_wo = [ps("aPwo0", [128, 512]), ps("aPwo1", [128, 512])]

    k_sb3 = k_sb[:].rearrange("p (c n) -> p c n", n=1024)
    xt3 = xt[:].rearrange("p (c n) -> p c n", n=512)
    wos3 = wos[:].rearrange("p (c n) -> p c n", n=1024)
    xm3 = xm[:].rearrange("p (c n) -> p c n", n=512)

    moff, mc = [], 0
    for (qo, qw) in QT:
        moff.append(mc)
        mc += ((qw + 512) // 128) * qw

    FL_COL = QA_L[l] - 1
    FL_QT = next(qi for qi, (qo, qw) in enumerate(QT) if qo <= FL_COL < qo + qw)
    FL_LOC = FL_COL - QT[FL_QT][0]

    STM, OMC, RCC, AS, WIDE = {}, {}, {}, {}, []
    stm = om = rc = 0
    for qi, (qo, qw) in enumerate(QT):
        asub = (qw + 512) // 128
        awp = (asub + 1) // 2
        AS[qi] = (asub, awp)
        for p in range(4):
            for wp in range(awp):
                STM[(qi, p, wp)] = stm
                WIDE.append((qi, p, wp))
                stm += 1
            for u in range(asub):
                OMC[(qi, p, u)] = om
                om += 1
            RCC[(qi, p)] = rc
            rc += 1
    N_STM, N_OM, N_RC = stm, om, rc

    def om_of_wide(sgl):
        qi, p, wp = WIDE[sgl]
        asub, _ = AS[qi]
        last_u = min(2, asub - 2 * wp) + 2 * wp - 1
        return OMC[(qi, p, last_u)] + 1

    @block.sync
    def _(sync):
        sync.wait_ge(sem("qQS"), SW("qQS", 0))
        sync.wait_ge(sem("qKS"), SW("qKS", 0))
        sync.wait_ge(sem("qVS"), SW("qVS", 0))
        sync.dma_start(out=wos3, in_=I[("wo", l)].rearrange("(c p) n -> p c n", p=128)
                       ).then_inc(sem("aW"), 16)
        sync.dma_start(out=row[0:1, 0:64], in_=I[("qsr", l)][0:1, :]).then_inc(sem("aW"), 16)
        sync.dma_start(out=row[0:1, 64:128], in_=I[("ksr", l)][0:1, :]).then_inc(sem("aW"), 16)
        for qi, (qo, qw) in enumerate(QT):
            asub, awp = AS[qi]
            if qi >= 1:
                sync.wait_ge(sem("aSTM"), SW("aSTM", STM[(qi - 1, 3, AS[qi - 1][1] - 1)] + 1))
            sync.dma_start(
                out=q_sb[:].rearrange("p (c n) -> p c n", n=512)[:, :, 0:qw],
                in_=S["qr_scr"].rearrange("(c p) n -> p c n", p=128)
                [:, :, qo + 512:qo + 512 + qw],
            ).then_inc(sem("aQL"), 16)
            sync.dma_start(
                out=k_sb3[:, :, 0:qw + 512],
                in_=S["kr_scr"].rearrange("(c p) n -> p c n", p=128)
                [:, :, qo:qo + qw + 512],
            ).then_inc(sem("aQL"), 16)
            if qi >= 1:
                sync.wait_ge(sem("aOM"), SW("aOM", OMC[(qi - 1, 3, AS[qi - 1][0] - 1)] + 1))
            sync.dma_start(
                out=vt_sb[:, 0:544 * asub],
                in_=S["vt_scr"][:, 544 * (qo // 128):544 * (qo // 128 + asub)],
            ).then_inc(sem("aVL"), 16)
            if qi >= 1:
                sync.wait_ge(sem("aMS"), SW("aMS", STM[(qi - 1, 3, AS[qi - 1][1] - 1)] + 1))
            sync.dma_start(
                out=msk[:, 0:asub * qw],
                in_=I[("mask", l)][:, moff[qi]:moff[qi] + asub * qw],
            ).then_inc(sem("aML"), 16)
            if qi >= 1:
                sync.wait_ge(sem("aXM"), SW("aXM", 8 * qi))
            sync.dma_start(
                out=xt3[:, :, 0:qw],
                in_=x_src.rearrange("(c p) n -> p c n", p=128)
                [:, :, X0 + qo + 512:X0 + qo + 512 + qw],
            ).then_inc(sem("aXL"), 16)

    @block.tensor
    def _(tensor):
        tensor.wait_ge(sem("aNG"), SW("aNG", 1))
        tensor.matmul(p_wo[0][:, 0:1], ones[0:1, :], row[0:1, 72:73],
                      start=True, stop=True).then_inc(sem("aNG"), 1)
        for qi, (qo, qw) in enumerate(QT):
            asub, awp = AS[qi]
            tensor.wait_ge(sem("aQL"), SW("aQL", 32 * (qi + 1)))
            tensor.wait_ge(sem("aVL"), SW("aVL", 16 * (qi + 1)))
            for p in range(4):
                pidx = RCC[(qi, p)]
                for wp in range(awp):
                    nsu = min(2, asub - 2 * wp)
                    sgl = STM[(qi, p, wp)]
                    if sgl >= 1:
                        tensor.wait_ge(sem("aEX"), SW("aEX", 2 * sgl))
                    if wp == 0 and pidx >= 1:
                        tensor.wait_ge(sem("aOF"), SW("aOF", 2 * pidx))
                    for u2 in range(nsu):
                        u = 2 * wp + u2
                        for j in range(2):
                            mm = tensor.matmul(
                                p_se[j][:, 512 * u2:512 * u2 + qw],
                                k_sb3[64 * j:64 * (j + 1), p, 128 * u:128 * (u + 1)],
                                q_sb[64 * j:64 * (j + 1), 512 * p:512 * p + qw],
                                start=True, stop=True, tile_position=(64 * j, 0))
                    mm.then_inc(sem("aSTM"), 1)
                    tensor.wait_ge(sem("aEX"), SW("aEX", 2 * (sgl + 1)))
                    for u2 in range(nsu):
                        u = 2 * wp + u2
                        par = sgl % 2
                        h0 = 2 * p
                        h1 = 2 * p + 1
                        mm = tensor.matmul(
                            p_oe[0][0:65, 0:qw],
                            vt_sb[:, 544 * u + 68 * h0:544 * u + 68 * h0 + 65],
                            ptE[:, 1024 * (0 + 2 * par) + 512 * u2:
                                1024 * (0 + 2 * par) + 512 * u2 + qw],
                            start=(u == 0), stop=(u == asub - 1))
                        mm2 = tensor.matmul(
                            p_oe[1][64:128, 0:qw],
                            vt_sb[:, 544 * u + 68 * h1:544 * u + 68 * h1 + 64],
                            ptE[:, 1024 * (1 + 2 * par) + 512 * u2:
                                1024 * (1 + 2 * par) + 512 * u2 + qw],
                            start=(u == 0), stop=(u == asub - 1),
                            tile_position=(0, 64))
                        mm3 = tensor.matmul(
                            p_oe[1][0:1, 0:qw],
                            vt_sb[:, 544 * u + 68 * h1 + 64:544 * u + 68 * h1 + 65],
                            ptE[:, 1024 * (1 + 2 * par) + 512 * u2:
                                1024 * (1 + 2 * par) + 512 * u2 + qw],
                            start=(u == 0), stop=(u == asub - 1),
                            tile_position=(0, 0))
                        mm3.then_inc(sem("aOM"), 1)
                tensor.wait_ge(sem("aRC"), SW("aRC", 2 * pidx + 2))
                tensor.matmul(p_se[0][:, 0:qw], ones[64:65, :],
                              orc[64:65, 0:qw], start=True, stop=True,
                              tile_position=(64, 0)).then_inc(sem("aB2"), 1)
                tensor.matmul(p_se[0][:, 512:512 + qw], ones[0:1, :],
                              orc[0:1, 512:512 + qw], start=True, stop=True
                              ).then_inc(sem("aB2"), 1)
            tensor.wait_ge(sem("aOF"), SW("aOF", 8 * (qi + 1)))
            tensor.wait_ge(sem("aXL"), SW("aXL", 16 * (qi + 1)))
            for dc in range(8):
                nwo = 8 * qi + dc
                if nwo >= 2:
                    tensor.wait_ge(sem("aXM"), SW("aXM", nwo - 1))
                for ic in range(4):
                    mm = tensor.matmul(p_wo[dc % 2][:, 0:qw],
                                       wos3[:, ic, 128 * dc:128 * (dc + 1)],
                                       of[:, 512 * ic:512 * ic + qw],
                                       start=(ic == 0), stop=(ic == 3))
                mm.then_inc(sem("aWOM"), 1)

    @block.scalar
    def _(scalar):
        for qi, (qo, qw) in enumerate(QT):
            asub, awp = AS[qi]
            for p in range(4):
                for wp in range(awp):
                    nsu = min(2, asub - 2 * wp)
                    sgl = STM[(qi, p, wp)]
                    par = sgl % 2
                    scalar.wait_ge(sem("aMS"), SW("aMS", sgl + 1))
                    if sgl >= 2:
                        scalar.wait_ge(sem("aOM"), SW("aOM", om_of_wide(sgl - 2)))
                    for j in range(2):
                        scalar.activation(
                            ptE[:, 1024 * (j + 2 * par):
                                1024 * (j + 2 * par) + 512 * nsu].rearrange(
                                "p (u e) -> p u e", e=512)[:, :, 0:qw],
                            p_se[j][:, 0:512 * nsu].rearrange(
                                "p (u e) -> p u e", e=512)[:, :, 0:qw],
                            AF.Exp, bias=ngM[:, 0:1],
                        ).then_inc(sem("aEX"), 1)

    @block.vector
    def _(vector):
        vector.wait_ge(sem("aW"), SW("aW", 48))
        vector.reduce_max(row[0:1, 128:129], row[0:1, 0:64], AX.X,
                          apply_absolute_value=True)
        vector.reduce_max(row[0:1, 129:130], row[0:1, 64:128], AX.X,
                          apply_absolute_value=True)
        vector.tensor_mul(row[0:1, 130:131], row[0:1, 128:129], row[0:1, 129:130])
        vector.tensor_scalar(row[0:1, 72:73], row[0:1, 130:131], -8.0, -1.0,
                             ALU.mult, ALU.add).then_inc(sem("aNG"), 1)
        vector.wait_ge(sem("aNG"), SW("aNG", 2))
        vector.tensor_copy(ngM[:, 0:1], p_wo[0][:, 0:1]).then_inc(sem("aNG"), 1)
        for qi, (qo, qw) in enumerate(QT):
            asub, awp = AS[qi]
            vector.wait_ge(sem("aML"), SW("aML", 16 * (qi + 1)))
            for p in range(4):
                pidx = RCC[(qi, p)]
                for wp in range(awp):
                    nsu = min(2, asub - 2 * wp)
                    sgl = STM[(qi, p, wp)]
                    vector.wait_ge(sem("aSTM"), SW("aSTM", sgl + 1))
                    mrow = (2 * wp) * qw
                    for j in range(2):
                        ins = vector.scalar_tensor_tensor(
                            p_se[j][:, 0:512 * nsu].rearrange(
                                "p (u e) -> p u e", e=512)[:, :, 0:qw],
                            p_se[j][:, 0:512 * nsu].rearrange(
                                "p (u e) -> p u e", e=512)[:, :, 0:qw],
                            0.0,
                            msk[:, mrow:mrow + nsu * qw].rearrange(
                                "p (u e) -> p u e", e=qw),
                            ALU.add, ALU.add)
                    ins.then_inc(sem("aMS"), 1)
                vector.wait_ge(sem("aOM"), SW("aOM", OMC[(qi, p, asub - 1)] + 1))
                vector.reciprocal(orc[64:65, 0:qw],
                                  p_oe[0][64:65, 0:qw]).then_inc(sem("aRC"), 1)
                vector.reciprocal(orc[0:1, 512:512 + qw],
                                  p_oe[1][0:1, 0:qw]).then_inc(sem("aRC"), 1)
                vector.wait_ge(sem("aB2"), SW("aB2", 2 * pidx + 2))
                vector.tensor_copy(otm[0:64, 0:qw], p_oe[0][0:64, 0:qw])
                vector.tensor_tensor(of[0:64, 512 * p:512 * p + qw],
                                     otm[0:64, 0:qw],
                                     p_se[0][0:64, 0:qw],
                                     ALU.mult).then_inc(sem("aOF"), 1)
                vector.tensor_copy(otm[64:128, 512:512 + qw], p_oe[1][64:128, 0:qw])
                vector.tensor_tensor(of[64:128, 512 * p:512 * p + qw],
                                     otm[64:128, 512:512 + qw],
                                     p_se[0][64:128, 512:512 + qw],
                                     ALU.mult).then_inc(sem("aOF"), 1)
            for dc in range(8):
                vector.wait_ge(sem("aWOM"), SW("aWOM", 8 * qi + dc + 1))
                ins = vector.scalar_tensor_tensor(
                    xm3[:, dc, 0:qw], p_wo[dc % 2][:, 0:qw], 0.0,
                    xt3[:, dc, 0:qw], ALU.add, ALU.add)
                ins.then_inc(sem("aXM"), 1)
            if qi == FL_QT:
                vector.tensor_scalar_mul(
                    xm3[:, :, FL_LOC:FL_LOC + 1], xm3[:, :, FL_LOC:FL_LOC + 1],
                    ctx["fl_sb"][:, 0:1]).then_inc(sem("aFLG"), 1)

    @block.gpsimd
    def _(gpsimd):
        for qi, (qo, qw) in enumerate(QT):
            gpsimd.wait_ge(sem("aXM"), SW("aXM", 8 * (qi + 1)))
            if qi == FL_QT:
                gpsimd.wait_ge(sem("aFLG"), SW("aFLG", 1))
            gpsimd.dma_start(
                out=xmid.rearrange("(c p) n -> p c n", p=128)[:, :, qo:qo + qw],
                in_=xm3[:, :, 0:qw],
            ).then_inc(sem("aXMS"), 16)

    bump("aW", 48)
    bump("aNG", 3)
    bump("aQL", 32 * NQT)
    bump("aVL", 16 * NQT)
    bump("aML", 16 * NQT)
    bump("aXL", 16 * NQT)
    bump("aSTM", N_STM)
    bump("aEX", 2 * N_STM)
    bump("aMS", N_STM)
    bump("aOM", N_OM)
    bump("aRC", 2 * N_RC)
    bump("aB2", 2 * N_RC)
    bump("aOF", 2 * N_RC)
    bump("aWOM", 8 * NQT)
    bump("aXM", 8 * NQT)
    bump("aFLG", 1)
    bump("aXMS", 16 * NQT)
    ctx["a_stores"] = 16 * NQT


def emit_ffn_mlp(ctx, es, l):
    nc, block, sem, I, S = ctx["nc"], ctx["block"], ctx["sem"], ctx["I"], ctx["S"]
    SW, bump = ctx["SW"], ctx["bump"]
    ones = ctx["ones"]
    y_out = ctx["y_out"]
    sb = lambda nm, shape: es.enter_context(nc.sbuf_tensor(f"L{l}{nm}", shape, F32))
    ps = lambda nm, shape: es.enter_context(nc.psum_tensor(f"L{l}{nm}", shape, F32))

    xmid = S["xmid_scr"]
    TM = _tiles(NML[l])
    NTM = len(TM)
    MO = QA_L[l] - ML_L[l]

    xmt = sb("mXmt", [128, 8 * 513])
    w1c = sb("mW1c", [128, 2 * 8 * 256])
    w2c = sb("mW2c", [128, 2 * 22 * 128])
    hsq = sb("mHsq", [128, 8 * 512])
    hn = sb("mHn", [128, 8 * 512])
    hid = sb("mHid", [128, 22 * 512])
    gg = sb("mGg", [128, 512])
    x2o = sb("mX2o", [128, 8 * 512])
    x1t = sb("mX1t", [128, 8 * 512])
    outb = sb("mOut", [128, 8 * 512])
    rowm = sb("mRow", [1, 2048])
    cns = sb("mCns", [128, 2 * NHP + 24])

    p_m1 = ps("mPm1", [1, 512])
    p_m2 = ps("mPm2", [1, 512])
    p_bm = ps("mPbm", [128, 512])
    p_br = ps("mPbr", [128, 512])
    p_a = ps("mPa", [128, 512])
    p_g = ps("mPg", [128, 512])
    p_w = [ps("mPw0", [128, 512]), ps("mPw1", [128, 512])]

    xmt3 = xmt[:].rearrange("p (c n) -> p c n", n=513)
    w1c4 = w1c[:].rearrange("p (b c n) -> p b c n", b=2, n=256)
    w2c4 = w2c[:].rearrange("p (b c n) -> p b c n", b=2, n=128)
    hsq3 = hsq[:].rearrange("p (c n) -> p c n", n=512)
    hn3 = hn[:].rearrange("p (c n) -> p c n", n=512)
    hid3 = hid[:].rearrange("p (c n) -> p c n", n=512)
    x2o3 = x2o[:].rearrange("p (c n) -> p c n", n=512)
    x1t3 = x1t[:].rearrange("p (c n) -> p c n", n=512)
    outb3 = outb[:].rearrange("p (c n) -> p c n", n=512)
    B1C = 0
    B2C = 2 * NHP
    LGC = 2 * NHP + 8
    LBC = 2 * NHP + 16

    FLM_COL = ML_L[l] - 1
    if l == 0:
        FLM_MT = FLM_COL // 512
        FLM_LOC = FLM_COL - 512 * FLM_MT
    dst = S["x2_scr"] if l == 0 else None

    def hm_ap(dc, W):
        return xmt3[:, dc, 1:1 + W] if dc < 4 else xmt3[:, dc, 0:W]

    @block.sync
    def _(sync):
        sync.wait_ge(sem("aXMS"), SW("aXMS", 0))
        sync.dma_start(out=cns[:, B1C:B1C + 2 * NHP], in_=I[("b1c", l)][:, :]
                       ).then_inc(sem("mW"), 16)
        sync.dma_start(out=cns[:, B2C:B2C + 8], in_=I[("b2c", l)][:, :]
                       ).then_inc(sem("mW"), 16)
        sync.dma_start(out=cns[:, LGC:LGC + 8], in_=I[("lgc", l)][:, :]
                       ).then_inc(sem("mW"), 16)
        sync.dma_start(out=cns[:, LBC:LBC + 8], in_=I[("lbc", l)][:, :]
                       ).then_inc(sem("mW"), 16)
        wq = 0
        w2q = 0
        for mi, (mo, W) in enumerate(TM):
            if mi >= 1:
                sync.wait_ge(sem("mX2E"), SW("mX2E", 8 * mi))
            sync.dma_start(
                out=xmt3[:, :, 0:W + 1],
                in_=xmid.rearrange("(c p) n -> p c n", p=128)
                [:, :, MO + mo - 1:MO + mo + W],
            ).then_inc(sem("mXL"), 16)
            if l == 1:
                if mi >= 1:
                    sync.wait_ge(sem("mY3"), SW("mY3", 8 * mi))
                sync.dma_start(
                    out=x1t3[:, :, 0:W],
                    in_=S["x1_scr"].rearrange("(c p) n -> p c n", p=128)
                    [:, :, T0_L + mo:T0_L + mo + W],
                ).then_inc(sem("mXL"), 16)
            for hp in range(NHP):
                if wq >= 2:
                    sync.wait_ge(sem("mW1M"), SW("mW1M", wq - 1))
                sync.dma_start(
                    out=w1c4[:, wq % 2, :, :],
                    in_=I[("w1r", l)].rearrange("(c p) n -> p c n", p=128)
                    [:, :, 256 * hp:256 * (hp + 1)],
                ).then_inc(sem(f"mWQ{wq % 2}"), 16)
                wq += 1
            for dc in range(8):
                if w2q >= 2:
                    sync.wait_ge(sem("mW2M"), SW("mW2M", w2q - 1))
                sync.dma_start(
                    out=w2c4[:, w2q % 2, :, :],
                    in_=I[("w2p", l)].rearrange("(c p) n -> p c n", p=128)
                    [:, :, 128 * dc:128 * (dc + 1)],
                ).then_inc(sem(f"mW2Q{w2q % 2}"), 16)
                w2q += 1

    @block.tensor
    def _(tensor):
        tensor.wait_ge(sem("mW"), SW("mW", 64))
        wq = 0
        w2q = 0
        for mi, (mo, W) in enumerate(TM):
            nrow = 4 if l == 0 else 8
            tensor.wait_ge(sem("mXL"), SW("mXL", 16 * (1 if l == 0 else 2) * (mi + 1)))
            if mi > 0:
                tensor.wait_ge(sem("mRow"), SW("mRow", nrow * mi))
            for dc in range(8):
                mm = tensor.matmul(p_m1[:, 0:W], ones[:, 0:1], hm_ap(dc, W),
                                   start=(dc == 0), stop=(dc == 7))
            mm.then_inc(sem("mS1"), 1)
            tensor.wait_ge(sem("mSq"), SW("mSq", (2 if l == 1 else 1) * mi + 1))
            for dc in range(8):
                mm = tensor.matmul(p_m2[:, 0:W], ones[:, 0:1], hsq3[:, dc, 0:W],
                                   start=(dc == 0), stop=(dc == 7))
            mm.then_inc(sem("mS2"), 1)
            tensor.wait_ge(sem("mRow"), SW("mRow", nrow * mi + 4))
            if mi > 0:
                tensor.wait_ge(sem("mAP"), SW("mAP", 24 * mi))
            tensor.matmul(p_bm[:, 0:W], ones[0:1, :], rowm[0:1, 0:W],
                          start=True, stop=True)
            tensor.matmul(p_br[:, 0:W], ones[0:1, :], rowm[0:1, 512:512 + W],
                          start=True, stop=True).then_inc(sem("mBC"), 1)
            for hp in range(NHP):
                hpg = NHP * mi + hp
                tensor.wait_ge(sem(f"mWQ{wq % 2}"), SW(f"mWQ{wq % 2}", 16 * (wq // 2 + 1)))
                tensor.wait_ge(sem("mAP"), SW("mAP", 24 * mi + 24))
                if hpg > 0:
                    tensor.wait_ge(sem("mHP"), SW("mHP", hpg))
                    tensor.wait_ge(sem("mGE"), SW("mGE", hpg))
                for dc in range(8):
                    mm = tensor.matmul(p_a[:, 0:W], w1c4[:, wq % 2, dc, 0:128],
                                       hn3[:, dc, 0:W], start=(dc == 0), stop=(dc == 7))
                for dc in range(8):
                    mm2 = tensor.matmul(p_g[:, 0:W], w1c4[:, wq % 2, dc, 128:256],
                                        hn3[:, dc, 0:W], start=(dc == 0), stop=(dc == 7))
                mm2.then_inc(sem("mW1M"), 1)
                wq += 1
            for dc in range(8):
                ndc = 8 * mi + dc
                tensor.wait_ge(sem(f"mW2Q{w2q % 2}"), SW(f"mW2Q{w2q % 2}", 16 * (w2q // 2 + 1)))
                tensor.wait_ge(sem("mHP"), SW("mHP", NHP * (mi + 1)))
                if ndc >= 2:
                    tensor.wait_ge(sem("mX2E"), SW("mX2E", ndc - 1))
                for hp in range(NHP):
                    mm = tensor.matmul(p_w[dc % 2][:, 0:W], w2c4[:, w2q % 2, hp, :],
                                       hid3[:, hp, 0:W], start=(hp == 0),
                                       stop=(hp == NHP - 1))
                mm.then_inc(sem("mW2M"), 1)
                w2q += 1
            if l == 1:
                tensor.wait_ge(sem("mY3"), SW("mY3", 8 * (mi + 1)))
                tensor.wait_ge(sem("mRow"), SW("mRow", nrow * mi + 1))
                for dc in range(8):
                    mm = tensor.matmul(p_m1[:, 0:W], ones[:, 0:1], x2o3[:, dc, 0:W],
                                       start=(dc == 0), stop=(dc == 7))
                mm.then_inc(sem("mS1"), 1)
                tensor.wait_ge(sem("mSq"), SW("mSq", 2 * mi + 2))
                for dc in range(8):
                    mm = tensor.matmul(p_m2[:, 0:W], ones[:, 0:1], hsq3[:, dc, 0:W],
                                       start=(dc == 0), stop=(dc == 7))
                mm.then_inc(sem("mS2"), 1)
                tensor.wait_ge(sem("mRow"), SW("mRow", nrow * mi + 8))
                tensor.wait_ge(sem("mAP"), SW("mAP", 24 * (mi + 1)))
                if mi > 0:
                    tensor.wait_ge(sem("mAP2"), SW("mAP2", 16 * mi))
                tensor.matmul(p_bm[:, 0:W], ones[0:1, :], rowm[0:1, 0:W],
                              start=True, stop=True)
                tensor.matmul(p_br[:, 0:W], ones[0:1, :], rowm[0:1, 512:512 + W],
                              start=True, stop=True).then_inc(sem("mBC"), 1)

    @block.scalar
    def _(scalar):
        for mi, (mo, W) in enumerate(TM):
            nrow = 4 if l == 0 else 8
            scalar.wait_ge(sem("mXL"), SW("mXL", 16 * (1 if l == 0 else 2) * (mi + 1)))
            if mi > 0:
                scalar.wait_ge(sem("mS2"), SW("mS2", (2 if l == 1 else 1) * mi))
                scalar.wait_ge(sem("mAP"), SW("mAP", 24 * mi))
                if l == 1:
                    scalar.wait_ge(sem("mAP2"), SW("mAP2", 16 * mi))
            scalar.activation(hsq3[:, 0:4, 0:W], xmt3[:, 0:4, 1:1 + W], AF.Square)
            scalar.activation(hsq3[:, 4:8, 0:W], xmt3[:, 4:8, 0:W], AF.Square
                              ).then_inc(sem("mSq"), 1)
            scalar.wait_ge(sem("mRow"), SW("mRow", nrow * mi + 1))
            scalar.activation(rowm[0:1, 1024:1024 + W], rowm[0:1, 0:W], AF.Square
                              ).then_inc(sem("mRow"), 1)
            scalar.wait_ge(sem("mRow"), SW("mRow", nrow * mi + 3))
            scalar.activation(rowm[0:1, 512:512 + W], rowm[0:1, 512:512 + W], AF.Sqrt
                              ).then_inc(sem("mRow"), 1)
            for hp in range(NHP):
                hpg = NHP * mi + hp
                scalar.wait_ge(sem("mW1M"), SW("mW1M", hpg + 1))
                if hpg > 0:
                    scalar.wait_ge(sem("mHP"), SW("mHP", hpg))
                scalar.activation(gg[:, 0:W], p_g[:, 0:W], AF.Gelu,
                                  bias=cns[:, B1C + 2 * hp + 1:B1C + 2 * hp + 2]
                                  ).then_inc(sem("mGE"), 1)
            if l == 1:
                scalar.wait_ge(sem("mY3"), SW("mY3", 8 * (mi + 1)))
                scalar.activation(hsq[:, 0:8 * 512].rearrange(
                    "p (c n) -> p c n", n=512)[:, :, 0:W],
                    x2o3[:, :, 0:W], AF.Square).then_inc(sem("mSq"), 1)
                scalar.wait_ge(sem("mRow"), SW("mRow", nrow * mi + 5))
                scalar.activation(rowm[0:1, 1024:1024 + W], rowm[0:1, 0:W], AF.Square
                                  ).then_inc(sem("mRow"), 1)
                scalar.wait_ge(sem("mRow"), SW("mRow", nrow * mi + 7))
                scalar.activation(rowm[0:1, 512:512 + W], rowm[0:1, 512:512 + W], AF.Sqrt
                                  ).then_inc(sem("mRow"), 1)

    @block.vector
    def _(vector):
        for mi, (mo, W) in enumerate(TM):
            nrow = 4 if l == 0 else 8
            vector.wait_ge(sem("mS1"), SW("mS1", (2 if l == 1 else 1) * mi + 1))
            vector.tensor_scalar_mul(rowm[0:1, 0:W], p_m1[0:1, 0:W], 1.0 / 1024.0
                                     ).then_inc(sem("mRow"), 1)
            vector.wait_ge(sem("mRow"), SW("mRow", nrow * mi + 2))
            vector.wait_ge(sem("mS2"), SW("mS2", (2 if l == 1 else 1) * mi + 1))
            vector.tensor_scalar_mul(rowm[0:1, 1536:1536 + W], p_m2[0:1, 0:W],
                                     1.0 / 1024.0)
            vector.tensor_sub(rowm[0:1, 512:512 + W], rowm[0:1, 1536:1536 + W],
                              rowm[0:1, 1024:1024 + W])
            vector.tensor_scalar_add(rowm[0:1, 512:512 + W], rowm[0:1, 512:512 + W],
                                     1e-5)
            vector.reciprocal(rowm[0:1, 512:512 + W], rowm[0:1, 512:512 + W]
                              ).then_inc(sem("mRow"), 1)
            vector.wait_ge(sem("mBC"), SW("mBC", (2 if l == 1 else 1) * mi + 1))
            for dc in range(8):
                vector.tensor_sub(hsq3[:, dc, 0:W], hm_ap(dc, W), p_bm[:, 0:W]
                                  ).then_inc(sem("mAP"), 1)
                vector.tensor_tensor(hn3[:, dc, 0:W], hsq3[:, dc, 0:W], p_br[:, 0:W],
                                     ALU.mult).then_inc(sem("mAP"), 1)
                vector.tensor_scalar(hn3[:, dc, 0:W], hn3[:, dc, 0:W],
                                     cns[:, LGC + dc:LGC + dc + 1],
                                     cns[:, LBC + dc:LBC + dc + 1],
                                     ALU.mult, ALU.add).then_inc(sem("mAP"), 1)
            for hp in range(NHP):
                hpg = NHP * mi + hp
                vector.wait_ge(sem("mGE"), SW("mGE", hpg + 1))
                ins = vector.scalar_tensor_tensor(
                    hid3[:, hp, 0:W], p_a[:, 0:W],
                    cns[:, B1C + 2 * hp:B1C + 2 * hp + 1], gg[:, 0:W],
                    ALU.add, ALU.mult)
                ins.then_inc(sem("mHP"), 1)
            for dc in range(8):
                ndc = 8 * mi + dc
                vector.wait_ge(sem("mW2M"), SW("mW2M", ndc + 1))
                ins = vector.scalar_tensor_tensor(
                    x2o3[:, dc, 0:W], p_w[dc % 2][:, 0:W],
                    cns[:, B2C + dc:B2C + dc + 1], xmt3[:, dc, 1:1 + W],
                    ALU.add, ALU.add)
                ins.then_inc(sem("mX2E"), 1)
            if l == 0:
                if mi == FLM_MT:
                    vector.tensor_scalar_mul(
                        x2o3[:, :, FLM_LOC:FLM_LOC + 1],
                        x2o3[:, :, FLM_LOC:FLM_LOC + 1],
                        ctx["fl_sb"][:, 0:1]).then_inc(sem("mFLG"), 1)
            else:
                for dc in range(8):
                    vector.tensor_add(x2o3[:, dc, 0:W], x2o3[:, dc, 0:W],
                                      x1t3[:, dc, 0:W]).then_inc(sem("mY3"), 1)
                vector.wait_ge(sem("mS1"), SW("mS1", 2 * mi + 2))
                vector.tensor_scalar_mul(rowm[0:1, 0:W], p_m1[0:1, 0:W], 1.0 / 1024.0
                                         ).then_inc(sem("mRow"), 1)
                vector.wait_ge(sem("mRow"), SW("mRow", 8 * mi + 6))
                vector.wait_ge(sem("mS2"), SW("mS2", 2 * mi + 2))
                vector.tensor_scalar_mul(rowm[0:1, 1536:1536 + W], p_m2[0:1, 0:W],
                                         1.0 / 1024.0)
                vector.tensor_sub(rowm[0:1, 512:512 + W], rowm[0:1, 1536:1536 + W],
                                  rowm[0:1, 1024:1024 + W])
                vector.tensor_scalar_add(rowm[0:1, 512:512 + W],
                                         rowm[0:1, 512:512 + W], 1e-5)
                vector.reciprocal(rowm[0:1, 512:512 + W], rowm[0:1, 512:512 + W]
                                  ).then_inc(sem("mRow"), 1)
                vector.wait_ge(sem("mBC"), SW("mBC", 2 * mi + 2))
                if mi > 0:
                    vector.wait_ge(sem("mST"), SW("mST", 16 * mi))
                for dc in range(8):
                    vector.tensor_sub(hsq3[:, dc, 0:W], x2o3[:, dc, 0:W], p_bm[:, 0:W]
                                      ).then_inc(sem("mAP2"), 1)
                    vector.tensor_tensor(outb3[:, dc, 0:W], hsq3[:, dc, 0:W],
                                         p_br[:, 0:W], ALU.mult
                                         ).then_inc(sem("mAP2"), 1)

    @block.gpsimd
    def _(gpsimd):
        for mi, (mo, W) in enumerate(TM):
            if l == 0:
                gpsimd.wait_ge(sem("mX2E"), SW("mX2E", 8 * (mi + 1)))
                if mi == FLM_MT:
                    gpsimd.wait_ge(sem("mFLG"), SW("mFLG", 1))
                gpsimd.dma_start(
                    out=dst.rearrange("(c p) n -> p c n", p=128)[:, :, mo:mo + W],
                    in_=x2o3[:, :, 0:W],
                ).then_inc(sem("mX2"), 16)
            else:
                gpsimd.wait_ge(sem("mAP2"), SW("mAP2", 16 * (mi + 1)))
                gpsimd.dma_start(
                    out=y_out.rearrange("(c p) n -> p c n", p=128)[:, :, mo:mo + W],
                    in_=outb3[:, :, 0:W],
                ).then_inc(sem("mST"), 16)

    n_ln = 2 if l == 1 else 1
    bump("mW", 64)
    bump("mXL", 16 * (1 if l == 0 else 2) * NTM)
    bump("mWQ0", 16 * ((NHP * NTM + 1) // 2))
    bump("mWQ1", 16 * (NHP * NTM // 2))
    bump("mW2Q0", 16 * ((8 * NTM + 1) // 2))
    bump("mW2Q1", 16 * (8 * NTM // 2))
    bump("mS1", n_ln * NTM)
    bump("mS2", n_ln * NTM)
    bump("mSq", n_ln * NTM)
    bump("mRow", (4 if l == 0 else 8) * NTM)
    bump("mBC", n_ln * NTM)
    bump("mAP", 24 * NTM)
    bump("mW1M", NHP * NTM)
    bump("mGE", NHP * NTM)
    bump("mHP", NHP * NTM)
    bump("mW2M", 8 * NTM)
    bump("mX2E", 8 * NTM)
    if l == 0:
        bump("mFLG", 1)
        bump("mX2", 16 * NTM)
        ctx["m0_tiles"] = NTM
    else:
        bump("mY3", 8 * NTM)
        bump("mAP2", 16 * NTM)
        bump("mST", 16 * NTM)
        ctx["done"].append(("mST", ctx["SW"]("mST", 0)))

def emit_finish(ctx):
    block, sem = ctx["block"], ctx["sem"]
    done = list(ctx["done"])
    if ctx["dbg"] == "l1a":
        done.append(("aXMS", ctx["SW"]("aXMS", 0)))

    @block.gpsimd
    def _(gpsimd):
        for name, val in done:
            gpsimd.wait_ge(sem(name), val)



def kernel(**inputs):
    in_maps = _prepare(inputs)
    nc = build_nc(DEBUG_STAGE)
    res = run_bass_kernel_spmd(nc, in_maps, core_ids=list(range(8)))
    if DEBUG_STAGE is not None:
        return [r["y_out"] for r in res.results]
    out = np.empty((2, SEQ, D), dtype=np.float32)
    for c in range(8):
        b, g = c // 4, c % 4
        out[b, 2048 * g:2048 * (g + 1)] = res.results[c]["y_out"].T
    return out


# version 02
# speedup vs baseline: 1.0117x; 1.0117x over previous
import os
os.environ.setdefault("JAX_COMPILATION_CACHE_DIR", "/tmp/jaxcache")
os.environ.setdefault("JAX_PERSISTENT_CACHE_MIN_COMPILE_TIME_SECS", "1")
os.environ.setdefault("JAX_PERSISTENT_CACHE_MIN_ENTRY_SIZE_BYTES", "-1")

import contextlib
import numpy as np

import concourse.bass as bass
import concourse.mybir as mybir
from concourse.bass_utils import run_bass_kernel_spmd

F32 = mybir.dt.float32
BF16 = mybir.dt.bfloat16
AF = mybir.ActivationFunctionType
ALU = mybir.AluOpType
AX = mybir.AxisListType

D = 1024
SEQ = 8192
DH = 64
WINDOW = 512
HID = 2730
HID_PAD = 2816
NHP = HID_PAD // 128
QK_SCALE = 8.0

T0_L = 1536
KA_L = [1280, 640]
QA_L = [768, 128]
ML_L = [704, 0]
NT0 = 2048 + T0_L
NKA = [2048 + KA_L[0], 2048 + KA_L[1]]
NQA = [2048 + QA_L[0], 2048 + QA_L[1]]
NML = [2048 + ML_L[0], 2048]

NEG = np.float32(-1e30)
DEBUG_STAGE = None


def _tiles(n, t=512):
    out, o = [], 0
    while o < n:
        w = min(t, n - o)
        out.append((o, w))
        o += w
    return out



def _rotary_tables(pos):
    half = np.arange(0, DH, 2, dtype=np.float64)
    inv_freq = 1.0 / 10000.0 ** (half / DH)
    t = pos.astype(np.float64)
    freqs = np.concatenate([t[:, None] * inv_freq[None, :]] * 2, axis=-1)
    base = (half + 0.4 * DH) / (1.4 * DH)
    power = (t - SEQ // 2) / float(WINDOW // 2)
    sc = base[None, :] ** power[:, None]
    sc2 = np.concatenate([sc, sc], axis=-1)
    cos, sin = np.cos(freqs), np.sin(freqs)
    sgn = np.concatenate([-np.ones(32), np.ones(32)])
    Aq = (cos * sc2).T
    Bq = (sgn[None, :] * sin * sc2).T
    Ak = (cos / sc2).T
    Bk = (sgn[None, :] * sin / sc2).T
    t2 = lambda m: np.ascontiguousarray(np.tile(m.astype(np.float32), (2, 1)))
    return t2(Aq), t2(Bq), t2(Ak), t2(Bk)


def _ffn_masks(s, li):
    qa0 = s - QA_L[li]
    cols = []
    for (qo, qw) in _tiles(NQA[li]):
        q_abs = qa0 + qo + np.arange(qw)
        k_lo = (qa0 + qo) - 512
        nsub = (qw + 512) // 128
        for u in range(nsub):
            k_abs = k_lo + u * 128 + np.arange(128)
            i = q_abs[None, :]
            j = k_abs[:, None]
            m = (i < j) | (i > j + WINDOW) | (j < 0)
            first = (j == k_lo) & (i < 0)
            m = m & ~first
            cols.append(np.where(m, NEG, np.float32(0.0)))
    return np.ascontiguousarray(np.concatenate(cols, axis=1), dtype=np.float32)


def _mask_cols(li):
    return sum(((qw + 512) // 128) * qw for (qo, qw) in _tiles(NQA[li]))


def _prepare(inputs):
    f32 = lambda a: np.asarray(a, dtype=np.float32)
    x = f32(inputs["L0_attn_x"])
    w_q = f32(inputs["L0_attn_w_q"])
    w_kv = f32(inputs["L0_attn_w_kv"])
    w_out = f32(inputs["L0_attn_w_out"])
    b_out = f32(inputs["L0_attn_b_out"])
    scale = f32(inputs["L0_attn_scale"]).reshape(16)
    wqkv = f32(inputs["L0_ffn_wqkv"])
    qsc = f32(inputs["L0_ffn_q_scale"])
    ksc = f32(inputs["L0_ffn_k_scale"])
    wo = f32(inputs["L0_ffn_wo"])
    ln_g = f32(inputs["L0_ffn_ln_g"])
    ln_b = f32(inputs["L0_ffn_ln_b"])
    w1 = f32(inputs["L0_ffn_w1"])
    b1 = f32(inputs["L0_ffn_b1"])
    w2 = f32(inputs["L0_ffn_w2"])
    b2 = f32(inputs["L0_ffn_b2"])

    def reorder_w1(w1l, b1l):
        a = np.pad(w1l[:, :HID], ((0, 0), (0, HID_PAD - HID)))
        g = np.pad(w1l[:, HID:], ((0, 0), (0, HID_PAD - HID)))
        blocks = []
        for i in range(NHP):
            blocks += [a[:, i * 128:(i + 1) * 128], g[:, i * 128:(i + 1) * 128]]
        w1r = np.ascontiguousarray(np.concatenate(blocks, axis=1), dtype=np.float32)
        ab = np.pad(b1l[:HID], (0, HID_PAD - HID))
        gb = np.pad(b1l[HID:], (0, HID_PAD - HID))
        b1r = np.concatenate([np.concatenate([ab[i * 128:(i + 1) * 128],
                                              gb[i * 128:(i + 1) * 128]])
                              for i in range(NHP)]).astype(np.float32)
        return w1r, np.ascontiguousarray(b1r.reshape(2 * NHP, 128).T)

    w1r_b1c = [reorder_w1(w1[l], b1[l]) for l in range(2)]
    w2p = [np.ascontiguousarray(np.pad(w2[l], ((0, HID_PAD - HID), (0, 0))))
           for l in range(2)]
    colmaj = lambda v: np.ascontiguousarray(v.reshape(-1, 128).T)

    blk2 = np.zeros((128, 2), dtype=np.float32)
    blk2[0:64, 0] = 1.0
    blk2[64:128, 1] = 1.0
    blk2T = np.ascontiguousarray(blk2.T)

    in_maps = []
    for c in range(8):
        b, g = c // 4, c % 4
        s = 2048 * g
        xb = np.ascontiguousarray(x[b].T)
        idx = np.arange(s - T0_L, s + 2048)
        xres = np.ascontiguousarray(xb[:, idx % SEQ] * (idx >= 0))
        m = {
            "xg": xb,
            "x_res": xres,
            "wq_g": np.ascontiguousarray(w_q[:, g * 256:(g + 1) * 256]),
            "wk_g": np.ascontiguousarray(w_kv[:, g * 256:(g + 1) * 256]),
            "wv_g": np.ascontiguousarray(w_kv[:, 1024 + g * 256:1024 + (g + 1) * 256]),
            "w_out": w_out,
            "b_out_c": colmaj(b_out),
            "scale_my": np.ascontiguousarray(scale[g * 4:(g + 1) * 4].reshape(1, 4)),
            "flag": np.full((128, 1), 0.0 if g == 0 else 1.0, dtype=np.float32),
            "blk2": blk2,
            "blk2T": blk2T,
        }
        for l in range(2):
            ka0 = s - KA_L[l]
            Aq, Bq, Ak, Bk = _rotary_tables(np.arange(ka0, s + 2048))
            m[f"rAq{l}"], m[f"rBq{l}"] = Aq, Bq
            m[f"rAk{l}"], m[f"rBk{l}"] = Ak, Bk
            m[f"mask{l}"] = _ffn_masks(s, l)
            m[f"wqkv{l}"] = np.ascontiguousarray(wqkv[l])
            m[f"wo{l}"] = np.ascontiguousarray(wo[l])
            m[f"w1r{l}"] = w1r_b1c[l][0]
            m[f"b1c{l}"] = w1r_b1c[l][1]
            m[f"w2p{l}"] = w2p[l]
            m[f"b2c{l}"] = colmaj(b2[l])
            m[f"lgc{l}"] = colmaj(ln_g[l])
            m[f"lbc{l}"] = colmaj(ln_b[l])
            m[f"qsc{l}"] = np.ascontiguousarray(np.tile(qsc[l], 2).reshape(128, 1))
            m[f"ksc{l}"] = np.ascontiguousarray(np.tile(ksc[l], 2).reshape(128, 1))
            m[f"qsr{l}"] = np.ascontiguousarray(qsc[l].reshape(1, 64))
            m[f"ksr{l}"] = np.ascontiguousarray(ksc[l].reshape(1, 64))
        in_maps.append(m)
    return in_maps



def build_nc(debug_stage=None):
    nc = bass.Bass("TRN2", target_bir_lowering=False, debug=False,
                   detect_race_conditions=False)
    dt = lambda name, shape, kind: nc.dram_tensor(name, shape, F32, kind=kind).ap()
    I = {}
    for nm, shape in [
        ("xg", [D, SEQ]), ("x_res", [D, NT0]),
        ("wq_g", [D, 256]), ("wk_g", [D, 256]), ("wv_g", [D, 256]),
        ("w_out", [D, D]), ("b_out_c", [128, 8]), ("scale_my", [1, 4]),
        ("flag", [128, 1]), ("blk2", [128, 2]), ("blk2T", [2, 128]),
    ]:
        I[nm] = dt(nm, shape, "ExternalInput")
    for l in range(2):
        for nm, shape in [
            ("rAq", [128, NKA[l]]), ("rBq", [128, NKA[l]]),
            ("rAk", [128, NKA[l]]), ("rBk", [128, NKA[l]]),
            ("mask", [128, _mask_cols(l)]),
            ("wqkv", [D, 1536]), ("wo", [512, D]),
            ("w1r", [D, 2 * HID_PAD]), ("b1c", [128, 2 * NHP]),
            ("w2p", [HID_PAD, D]), ("b2c", [128, 8]),
            ("lgc", [128, 8]), ("lbc", [128, 8]),
            ("qsc", [128, 1]), ("ksc", [128, 1]),
            ("qsr", [1, 64]), ("ksr", [1, 64]),
        ]:
            I[(nm, l)] = dt(f"{nm}{l}", shape, "ExternalInput")

    S = {
        "q_scr": dt("q_scr", [256, SEQ], "Internal"),
        "cc_in": dt("cc_in", [256, SEQ], "Internal"),
        "cc_out0": dt("cc_out0", [512, SEQ], "Internal"),
        "cc_out1": dt("cc_out1", [512, SEQ], "Internal"),
        "att_w": dt("att_w", [D, NT0], "Internal"),
        "x1_scr": dt("x1_scr", [D, NT0], "Internal"),
        "xmid_scr": dt("xmid_scr", [D, max(NQA)],
                       "ExternalOutput" if debug_stage in ("x2", "l1a") else "Internal"),
        "x2_scr": dt("x2_scr", [D, NML[0]],
                     "ExternalOutput" if debug_stage in ("x2", "l1a") else "Internal"),
        "qr_scr": dt("qr_scr", [512, NKA[0]],
                     "ExternalOutput" if debug_stage in ("x2", "l1a") else "Internal"),
        "kr_scr": dt("kr_scr", [512, NKA[0]],
                     "ExternalOutput" if debug_stage in ("x2", "l1a") else "Internal"),
        "vt_scr": dt("vt_scr", [128, (NKA[0] // 128) * 544],
                     "ExternalOutput" if debug_stage in ("x2", "l1a") else "Internal"),
    }
    out_w = NT0 if debug_stage == "x1" else (NML[0] if debug_stage == "x2" else 2048)
    y_out = dt("y_out", [D, out_w], "ExternalOutput")

    es = contextlib.ExitStack()
    with es:
        block = es.enter_context(nc.Block())
        sems = {}

        def sem(name):
            if name not in sems:
                sems[name] = es.enter_context(nc.semaphore(name))
            return sems[name]

        bases = {}

        def SW(name, rel):
            return bases.get(name, 0) + rel

        def bump(name, total):
            bases[name] = bases.get(name, 0) + total

        ctx = dict(nc=nc, block=block, sem=sem, I=I, S=S, y_out=y_out,
                   dbg=debug_stage, done=[], SW=SW, bump=bump)
        ctx["ones"] = es.enter_context(nc.sbuf_tensor("ones", [128, 128], F32))
        ctx["blk2_sb"] = es.enter_context(nc.sbuf_tensor("blk2_sb", [128, 2], F32))
        ctx["blk2T_sb"] = es.enter_context(nc.sbuf_tensor("blk2T_sb", [2, 128], F32))
        ctx["fl_sb"] = es.enter_context(nc.sbuf_tensor("fl_sb", [128, 1], F32))
        with contextlib.ExitStack() as es_ab:
            emit_phase_AB(ctx, es_ab)
        emit_phase_C(ctx)
        with contextlib.ExitStack() as es_d:
            emit_phase_D(ctx, es_d)
        if debug_stage != "x1":
            for l in range(1 if debug_stage == "x2" else 2):
                with contextlib.ExitStack() as es_q:
                    emit_ffn_qkv(ctx, es_q, l)
                with contextlib.ExitStack() as es_e:
                    emit_ffn_attn(ctx, es_e, l)
                if debug_stage == "l1a" and l == 1:
                    break
                with contextlib.ExitStack() as es_m:
                    emit_ffn_mlp(ctx, es_m, l)
        emit_finish(ctx)
    return nc



def emit_phase_AB(ctx, es):
    nc, block, sem, I, S = ctx["nc"], ctx["block"], ctx["sem"], ctx["I"], ctx["S"]
    sb = lambda nm, shape: es.enter_context(nc.sbuf_tensor(nm, shape, F32))

    k_n = sb("k_n", [128, 2 * SEQ])
    v_tm = sb("v_tm", [128, 64 * 272])
    negm = sb("negm", [128, 4])
    ones = ctx["ones"]
    es_pa = contextlib.ExitStack()
    sba = lambda nm, shape: es_pa.enter_context(nc.sbuf_tensor(nm, shape, F32))
    a_w = sba("a_w", [128, 8 * 3 * 256])
    a_x = sba("a_x", [128, 2 * 8 * 512])
    a_sq = sba("a_sq", [128, 2 * 512])
    a_cp = sba("a_cp", [128, 2 * 512])
    a_ss = sba("a_ss", [2, 2 * 512])
    a_row = sba("a_row", [2, 64])
    a_out = sba("a_out", [128, 2 * 512])
    blk2_sb = ctx["blk2_sb"]
    blk2T_sb = ctx["blk2T_sb"]

    psa = lambda nm, shape: es_pa.enter_context(nc.psum_tensor(nm, shape, F32))
    p_qk = [psa(f"p_qk{i}", [128, 512]) for i in range(4)]
    p_v = [psa(f"p_v{i}", [128, 256]) for i in range(2)]
    p_s = psa("p_s", [2, 512])
    p_b = psa("p_b", [128, 512])

    NT = 16
    xg, q_scr = I["xg"], S["q_scr"]
    a_w4 = a_w[:].rearrange("p (c k n) -> p c k n", k=3, n=256)
    a_x4 = a_x[:].rearrange("p (b c n) -> p b c n", b=2, n=512)

    def grp(t, gi):
        return 8 * t + gi + 1

    @block.sync
    def _(sync):
        for ki, w in enumerate([I["wq_g"], I["wk_g"], I["wv_g"]]):
            sync.dma_start(out=a_w4[:, :, ki, :],
                           in_=w.rearrange("(c p) n -> p c n", p=128)
                           ).then_inc(sem("AW"), 16)
        sync.dma_start(out=a_row[0:1, 0:4], in_=I["scale_my"][0:1, :]
                       ).then_inc(sem("AW"), 16)
        sync.dma_start(out=blk2_sb[:], in_=I["blk2"][:, :]).then_inc(sem("AW"), 16)
        sync.dma_start(out=blk2T_sb[:], in_=I["blk2T"][:, :]).then_inc(sem("AW"), 16)
        sync.dma_start(out=ctx["fl_sb"][:], in_=I["flag"][:, :]).then_inc(sem("AW"), 16)
        for t in range(NT):
            if t >= 2:
                sync.wait_ge(sem("AMM"), grp(t - 2, 7))
            sync.dma_start(
                out=a_x4[:, t % 2],
                in_=xg.rearrange("(c p) s -> p c s", p=128)[:, :, 512 * t:512 * (t + 1)],
            ).then_inc(sem(f"AX{t % 2}"), 16)

    @block.gpsimd
    def _(gpsimd):
        gpsimd.memset(v_tm[:], 1.0)
        gpsimd.memset(ones[:], 1.0).then_inc(sem("AOn"), 1)
        gpsimd.wait_ge(sem("ANg"), 1)
        for j in range(2):
            gpsimd.dma_start(
                out=a_row[j:j + 1, 32:34],
                in_=bass.AP(a_row[0:1, 0:1].tensor, 4 + j, [[64, 1], [2, 2]]),
                allow_slow_non_contiguous=True,
            ).then_inc(sem("AEsc"), 16)
        for t in range(NT):
            for oc in range(2):
                gpsimd.wait_ge(sem("AFin"), 4 * t + oc + 1)
                gpsimd.dma_start(
                    out=q_scr[128 * oc:128 * (oc + 1), 512 * t:512 * (t + 1)],
                    in_=a_out[:, 512 * oc:512 * (oc + 1)],
                ).then_inc(sem(f"AQs{oc}"), 16)

    @block.scalar
    def _(scalar):
        scalar.wait_ge(sem("AW"), 112)
        scalar.activation(a_row[0:1, 4:8], a_row[0:1, 0:4], AF.Exp
                          ).then_inc(sem("ANg"), 1)
        def a_square(t, ci):
            n = 4 * t + ci
            sl2 = 512 * (ci % 2)
            scalar.wait_ge(sem("AMM"), grp(t, ci))
            if n >= 2:
                scalar.wait_ge(sem("ABlk"), n - 1)
                scalar.wait_ge(sem("AFin"), n - 1)
            scalar.activation(a_sq[:, sl2:sl2 + 512], p_qk[ci][:], AF.Square)
            scalar.copy(a_cp[:, sl2:sl2 + 512], p_qk[ci][:]).then_inc(sem("ASq"), 1)

        def a_sqrt(t, ci):
            sl2 = 512 * (ci % 2)
            scalar.wait_ge(sem("AMx"), 4 * t + ci + 1)
            sl = a_ss[:, sl2:sl2 + 512]
            ins = scalar.activation(sl, sl, AF.Sqrt)
            if ci < 2:
                if t == 0 and ci == 0:
                    scalar.wait_ge(sem("AEsc"), 32)
                ins = scalar.mul(sl, sl, a_row[0:2, 32 + ci:33 + ci])
            ins.then_inc(sem("ARs"), 1)

        for t in range(NT):
            for ci in range(4):
                a_square(t, ci)
                if ci >= 1:
                    a_sqrt(t, ci - 1)
            a_sqrt(t, 3)

    @block.vector
    def _(vector):
        vector.wait_ge(sem("ANg"), 1)
        vector.tensor_scalar(a_row[0:1, 8:12], a_row[0:1, 4:8], -1.0, -0.01,
                             ALU.mult, ALU.add).then_inc(sem("ANg"), 1)
        vector.wait_ge(sem("ANg"), 3)
        vector.tensor_copy(negm[:], p_b[:, 0:4]).then_inc(sem("ANg"), 1)
        for t in range(NT):
            for vs in range(4):
                vector.wait_ge(sem("AMM"), grp(t, 4 + vs))
                kc = 4 * t + vs
                vector.tensor_copy(
                    v_tm[:, 272 * kc:272 * (kc + 1)].rearrange(
                        "p (h n) -> p h n", n=68)[:, :, 0:64],
                    p_v[vs % 2][:].rearrange("p (h n) -> p h n", n=64),
                ).then_inc(sem("AVe"), 1)
            def a_mx(ci):
                n = 4 * t + ci
                vector.wait_ge(sem("ABlk"), n + 1)
                if n >= 2:
                    vector.wait_ge(sem("ABc"), n - 1)
                sl = a_ss[:, 512 * (ci % 2):512 * (ci % 2) + 512]
                vector.tensor_scalar_max(sl, p_s[:], 1e-24)
                vector.reciprocal(sl, sl).then_inc(sem("AMx"), 1)

            def a_fin(ci):
                vector.wait_ge(sem("ABc"), 4 * t + ci + 1)
                srcap = a_cp[:, 512 * (ci % 2):512 * (ci % 2) + 512]
                if ci < 2:
                    if t > 0:
                        vector.wait_ge(sem(f"AQs{ci}"), 16 * t)
                    dst = a_out[:, 512 * ci:512 * (ci + 1)]
                else:
                    dst = k_n[:, SEQ * (ci - 2) + 512 * t:SEQ * (ci - 2) + 512 * (t + 1)]
                vector.tensor_tensor(dst, srcap, p_b[:], ALU.mult
                                     ).then_inc(sem("AFin"), 1)

            for ci in range(4):
                a_mx(ci)
                if ci >= 1:
                    a_fin(ci - 1)
            a_fin(3)

    @block.tensor
    def _(tensor):
        tensor.wait_ge(sem("AW"), 112)
        tensor.wait_ge(sem("AOn"), 1)
        tensor.wait_ge(sem("ANg"), 2)
        tensor.matmul(p_b[:, 0:4], ones[0:1, :], a_row[0:1, 8:12],
                      start=True, stop=True).then_inc(sem("ANg"), 1)
        for t in range(NT):
            tensor.wait_ge(sem(f"AX{t % 2}"), 16 * (t // 2 + 1))
            xt = a_x4[:, t % 2]
            for ci in range(4):
                ki = 0 if ci < 2 else 1
                base = 128 * (ci % 2)
                if t > 0:
                    tensor.wait_ge(sem("AFin"), 4 * (t - 1) + ci + 1)
                for dc in range(8):
                    mm = tensor.matmul(p_qk[ci][:], a_w4[:, dc, ki, base:base + 128],
                                       xt[:, dc, :], start=(dc == 0), stop=(dc == 7))
                mm.then_inc(sem("AMM"), 1)
            for vs in range(4):
                n = 4 * t + vs
                if n >= 2:
                    tensor.wait_ge(sem("AVe"), n - 1)
                for dc in range(8):
                    mm = tensor.matmul(p_v[vs % 2][:], xt[:, dc, 128 * vs:128 * (vs + 1)],
                                       a_w4[:, dc, 2, :], start=(dc == 0), stop=(dc == 7))
                mm.then_inc(sem("AMM"), 1)
            def a_blk(ci):
                tensor.wait_ge(sem("ASq"), 4 * t + ci + 1)
                if 4 * t + ci > 0:
                    tensor.wait_ge(sem("AMx"), 4 * t + ci)
                tensor.matmul(p_s[:], blk2_sb[:, 0:2],
                              a_sq[:, 512 * (ci % 2):512 * (ci % 2) + 512],
                              start=True, stop=True).then_inc(sem("ABlk"), 1)

            def a_bc(ci):
                tensor.wait_ge(sem("ARs"), 4 * t + ci + 1)
                if t == 0 and ci == 0:
                    tensor.wait_ge(sem("ANg"), 4)
                else:
                    tensor.wait_ge(sem("AFin"), 4 * t + ci)
                tensor.matmul(p_b[:], blk2T_sb[0:2, :],
                              a_ss[:, 512 * (ci % 2):512 * (ci % 2) + 512],
                              start=True, stop=True).then_inc(sem("ABc"), 1)

            for ci in range(4):
                a_blk(ci)
                if ci >= 1:
                    a_bc(ci - 1)
            a_bc(3)

    es_pa.close()
    ctx["k_n"], ctx["v_tm"], ctx["negm"] = k_n, v_tm, negm
    emit_phase_B(ctx, es)



def emit_phase_B(ctx, es):
    nc, block, sem = ctx["nc"], ctx["block"], ctx["sem"]
    k_n, v_tm, negm, ones = ctx["k_n"], ctx["v_tm"], ctx["negm"], ctx["ones"]
    q_scr, cc_in = ctx["S"]["q_scr"], ctx["S"]["cc_in"]
    sb = lambda nm, shape: es.enter_context(nc.sbuf_tensor(nm, shape, F32))

    qld = sb("b_qld", [128, 2 * 512])
    pt = sb("b_pt", [128, 4 * 1536])
    oo = sb("b_oo", [128, 4 * 512])
    otmp = sb("b_otmp", [128, 2 * 512])
    orc = sb("b_orc", [128, 2 * 512])

    es_pb = contextlib.ExitStack()
    psb = lambda nm, shape: es_pb.enter_context(nc.psum_tensor(nm, shape, F32))
    ps_s = [psb("b_s0", [128, 1536]), psb("b_s1", [128, 1536])]
    po = [psb("b_o0", [128, 512]), psb("b_o1", [128, 512])]

    GROUPS = [3] * 21 + [1]
    NG = len(GROUPS)
    NIT = 32

    def base(it):
        return it * NG

    def o_mms(tensor, it, p, g, kc0, gw):
        for u in range(gw):
            kc = kc0 + u
            for j in range(2):
                mm = tensor.matmul(
                    po[j][0:65, :],
                    v_tm[:, 272 * kc + 68 * (2 * p + j):272 * kc + 68 * (2 * p + j) + 65],
                    pt[:, 1536 * (2 * j + g % 2) + 512 * u:
                       1536 * (2 * j + g % 2) + 512 * (u + 1)],
                    start=(kc == 0), stop=(kc == 63))
        mm.then_inc(sem("bO"), 1)

    @block.sync
    def _(sync):
        for it in range(NIT):
            p, qt = it // 16, it % 16
            if it == 0:
                sync.wait_ge(sem("AQs0"), 16 * 16)
                sync.wait_ge(sem("AQs1"), 16 * 16)
            if it >= 2:
                sync.wait_ge(sem("bS"), base(it - 2) + NG)
            sync.dma_start(
                out=qld[:, 512 * (it % 2):512 * (it % 2 + 1)],
                in_=q_scr[128 * p:128 * (p + 1), 512 * qt:512 * (qt + 1)],
            ).then_inc(sem(f"bQ{it % 2}"), 16)

    @block.tensor
    def _(tensor):
        tensor.wait_ge(sem("AFin"), 64)
        tensor.wait_ge(sem("AVe"), 64)
        for it in range(NIT):
            p, qt = it // 16, it % 16
            q_ap = qld[:, 512 * (it % 2):512 * (it % 2 + 1)]
            tensor.wait_ge(sem(f"bQ{it % 2}"), 16 * (it // 2 + 1))
            if it > 0:
                tensor.wait_ge(sem("bT"), 2 * it)
            kc = 0
            for g, gw in enumerate(GROUPS):
                if g > 0:
                    tensor.wait_ge(sem("bE"), 2 * base(it) + 2 * g)
                for u in range(gw):
                    for j in range(2):
                        mm = tensor.matmul(
                            ps_s[j][:, 512 * u:512 * (u + 1)],
                            k_n[64 * j:64 * (j + 1),
                                SEQ * p + 128 * (kc + u):SEQ * p + 128 * (kc + u + 1)],
                            q_ap[64 * j:64 * (j + 1), :],
                            start=True, stop=True, tile_position=(64 * j, 0))
                mm.then_inc(sem("bS"), 1)
                if g > 0:
                    o_mms(tensor, it, p, g - 1, kc - GROUPS[g - 1], GROUPS[g - 1])
                kc += gw
            tensor.wait_ge(sem("bE"), 2 * base(it) + 2 * NG)
            o_mms(tensor, it, p, NG - 1, kc - GROUPS[-1], GROUPS[-1])
            for j in range(2):
                tensor.wait_ge(sem("bR"), 2 * it + j + 1)
                tensor.matmul(ps_s[0][:, 512 * j:512 * (j + 1)],
                              ones[64:65, :], orc[64:65, 512 * j:512 * (j + 1)],
                              start=True, stop=True,
                              tile_position=(64, 0)).then_inc(sem("bB"), 1)

    @block.scalar
    def _(scalar):
        for it in range(NIT):
            p = it // 16
            for g, gw in enumerate(GROUPS):
                G = base(it) + g
                scalar.wait_ge(sem("bS"), G + 1)
                if G >= 2:
                    scalar.wait_ge(sem("bO"), G - 1)
                for j in range(2):
                    scalar.activation(
                        pt[:, 1536 * (2 * j + g % 2):1536 * (2 * j + g % 2) + 512 * gw],
                        ps_s[j][:, 0:512 * gw], AF.Exp,
                        bias=negm[:, 2 * p + j:2 * p + j + 1],
                    ).then_inc(sem("bE"), 1)

    @block.vector
    def _(vector):
        for it in range(NIT):
            vector.wait_ge(sem("bO"), base(it) + NG)
            for j in range(2):
                vector.reciprocal(orc[64:65, 512 * j:512 * (j + 1)],
                                  po[j][64:65, 0:512]).then_inc(sem("bR"), 1)
            for j in range(2):
                vector.wait_ge(sem("bB"), 2 * it + j + 1)
                if it > 1:
                    vector.wait_ge(sem("bSt"), 32 * (it - 1))
                sl = 2 * (it % 2) + j
                vector.tensor_copy(otmp[0:64, 512 * j:512 * (j + 1)],
                                   po[j][0:64, :])
                vector.tensor_tensor(oo[0:64, 512 * sl:512 * (sl + 1)],
                                     otmp[0:64, 512 * j:512 * (j + 1)],
                                     ps_s[0][0:64, 512 * j:512 * (j + 1)],
                                     ALU.mult).then_inc(sem("bT"), 1)

    @block.gpsimd
    def _(gpsimd):
        for it in range(NIT):
            p, qt = it // 16, it % 16
            if it == 16:
                gpsimd.wait_ge(sem("bSt"), 32 * 16)
                gpsimd.collective_compute(
                    "AllGather", ALU.bypass,
                    replica_groups=[[0, 1, 2, 3], [4, 5, 6, 7]],
                    ins=[cc_in[0:128, :]], outs=[ctx["S"]["cc_out0"][:]],
                ).then_inc(sem("cAG"), 1)
                gpsimd.wait_ge(sem("cAG"), 1)
                _emit_attw_copy(ctx, gpsimd, 0)
            for j in range(2):
                gpsimd.wait_ge(sem("bT"), 2 * it + j + 1)
                hh = 2 * p + j
                sl = 2 * (it % 2) + j
                gpsimd.dma_start(
                    out=cc_in[64 * hh:64 * (hh + 1), 512 * qt:512 * (qt + 1)],
                    in_=oo[0:64, 512 * sl:512 * (sl + 1)],
                ).then_inc(sem("bSt"), 16)

    es_pb.close()



def _emit_attw_copy(ctx, gpsimd, half):
    sem, S = ctx["sem"], ctx["S"]
    att_w = S["att_w"]
    src = S[f"cc_out{half}"]
    for arm in gpsimd.switch_core_id(n=128):
        g = (arm.logical % 8) % 4
        s = 2048 * g
        lo = s - T0_L
        for gg in range(4):
            dst = att_w[256 * gg + 128 * half:256 * gg + 128 * half + 128, :]
            sg = src[128 * gg:128 * (gg + 1), :]
            if lo < 0:
                gpsimd.dma_start(out=dst[:, 0:-lo],
                                 in_=sg[:, SEQ + lo:SEQ]).then_inc(sem("cW"), 16)
                gpsimd.dma_start(out=dst[:, -lo:NT0],
                                 in_=sg[:, 0:s + 2048]).then_inc(sem("cW"), 16)
            else:
                mid = NT0 // 2
                gpsimd.dma_start(out=dst[:, 0:mid],
                                 in_=sg[:, lo:lo + mid]).then_inc(sem("cW"), 16)
                gpsimd.dma_start(out=dst[:, mid:NT0],
                                 in_=sg[:, lo + mid:s + 2048]).then_inc(sem("cW"), 16)


def emit_phase_C(ctx):
    block, sem, S = ctx["block"], ctx["sem"], ctx["S"]
    cc_in = S["cc_in"]

    @block.gpsimd
    def _(gpsimd):
        gpsimd.wait_ge(sem("bSt"), 32 * 32)
        gpsimd.collective_compute(
            "AllGather", ALU.bypass,
            replica_groups=[[0, 1, 2, 3], [4, 5, 6, 7]],
            ins=[cc_in[128:256, :]], outs=[S["cc_out1"][:]],
        ).then_inc(sem("cAG"), 1)
        gpsimd.wait_ge(sem("cAG"), 2)
        _emit_attw_copy(ctx, gpsimd, 1)



def emit_phase_D(ctx, es):
    nc, block, sem, I, S = ctx["nc"], ctx["block"], ctx["sem"], ctx["I"], ctx["S"]
    att_w, x1_scr = S["att_w"], S["x1_scr"]
    y_out, dbg = ctx["y_out"], ctx["dbg"]
    sb = lambda nm, shape: es.enter_context(nc.sbuf_tensor(nm, shape, F32))

    d_w = sb("d_w", [128, 8 * 1024])
    d_bo = sb("d_bo", [128, 8])
    d_at = sb("d_at", [128, 2 * 8 * 512])
    d_xr = sb("d_xr", [128, 2 * 8 * 512])
    d_y = sb("d_y", [128, 8 * 512])
    d_ysq = sb("d_ysq", [128, 8 * 512])
    d_t1 = sb("d_t1", [128, 8 * 512])
    d_x1 = sb("d_x1", [128, 8 * 512])
    d_row = sb("d_row", [1, 2048])
    ones = ctx["ones"]

    psd = lambda nm, shape: es.enter_context(nc.psum_tensor(nm, shape, F32))
    p_d = [psd("p_d0", [128, 512]), psd("p_d1", [128, 512])]
    p_s1 = psd("p_s1", [1, 512])
    p_s2 = psd("p_s2", [1, 512])
    p_bm = psd("p_bm", [128, 512])
    p_br = psd("p_br", [128, 512])

    NTD = 7
    d_at4 = d_at[:].rearrange("p (b c n) -> p b c n", b=2, n=512)
    d_xr4 = d_xr[:].rearrange("p (b c n) -> p b c n", b=2, n=512)
    d_y3 = d_y[:].rearrange("p (c n) -> p c n", n=512)
    d_ysq3 = d_ysq[:].rearrange("p (c n) -> p c n", n=512)
    d_t13 = d_t1[:].rearrange("p (c n) -> p c n", n=512)
    d_x13 = d_x1[:].rearrange("p (c n) -> p c n", n=512)
    d_w3 = d_w[:].rearrange("p (c n) -> p c n", n=1024)

    apc = [8 + (1 if j == 2 else 0) for j in range(NTD)]
    apcum = [sum(apc[:j + 1]) for j in range(NTD)]

    @block.sync
    def _(sync):
        sync.wait_ge(sem("bSt"), 32 * 32)
        sync.dma_start(out=d_w3, in_=I["w_out"].rearrange("(c p) n -> p c n", p=128)
                       ).then_inc(sem("dW"), 16)
        sync.dma_start(out=d_bo[:], in_=I["b_out_c"][:, :]).then_inc(sem("dW"), 16)
        sync.dma_start(out=d_bo[:, 0:8], in_=I["b_out_c"][:, :]).then_inc(sem("dW"), 16) if False else None
        sync.nop() if False else None
        for j in range(NTD):
            if j == 0:
                sync.wait_ge(sem("cW"), 256)
            if j >= 2:
                sync.wait_ge(sem("dMM"), 8 * (j - 1))
                sync.wait_ge(sem("dY"), 8 * (j - 1))
            sync.dma_start(
                out=d_at4[:, j % 2],
                in_=att_w.rearrange("(c p) n -> p c n", p=128)[:, :, 512 * j:512 * (j + 1)],
            ).then_inc(sem(f"dA{j % 2}"), 16)
            sync.dma_start(
                out=d_xr4[:, j % 2],
                in_=I["x_res"].rearrange("(c p) n -> p c n", p=128)[:, :, 512 * j:512 * (j + 1)],
            ).then_inc(sem(f"dA{j % 2}"), 16)

    @block.tensor
    def _(tensor):
        tensor.wait_ge(sem("dW"), 32)
        for j in range(NTD):
            tensor.wait_ge(sem(f"dA{j % 2}"), 32 * (j // 2 + 1))
            at = d_at4[:, j % 2]
            for dc in range(8):
                n = 8 * j + dc
                if n >= 2:
                    tensor.wait_ge(sem("dY"), n - 1)
                for ic in range(8):
                    mm = tensor.matmul(p_d[dc % 2][:],
                                       d_w3[:, ic, 128 * dc:128 * (dc + 1)],
                                       at[:, ic, :],
                                       start=(ic == 0), stop=(ic == 7))
                mm.then_inc(sem("dMM"), 1)
            tensor.wait_ge(sem("dY"), 8 * (j + 1))
            if j > 0:
                tensor.wait_ge(sem("dRX"), 4 * j)
            for dc in range(8):
                mm = tensor.matmul(p_s1[:], ones[:, 0:1], d_y3[:, dc, :],
                                   start=(dc == 0), stop=(dc == 7))
            mm.then_inc(sem("dS1"), 1)
            tensor.wait_ge(sem("dSq"), j + 1)
            for dc in range(8):
                mm = tensor.matmul(p_s2[:], ones[:, 0:1], d_ysq3[:, dc, :],
                                   start=(dc == 0), stop=(dc == 7))
            mm.then_inc(sem("dS2"), 1)
            tensor.wait_ge(sem("dRX"), 4 * j + 4)
            if j > 0:
                tensor.wait_ge(sem("dAp"), apcum[j - 1])
            tensor.matmul(p_bm[:], ones[0:1, :], d_row[0:1, 0:512],
                          start=True, stop=True)
            tensor.matmul(p_br[:], ones[0:1, :], d_row[0:1, 512:1024],
                          start=True, stop=True).then_inc(sem("dBc"), 1)

    @block.scalar
    def _(scalar):
        for j in range(NTD):
            scalar.wait_ge(sem("dY"), 8 * (j + 1))
            if j > 0:
                scalar.wait_ge(sem("dS2"), j)
            scalar.activation(d_ysq[:], d_y[:], AF.Square).then_inc(sem("dSq"), 1)
            scalar.wait_ge(sem("dRX"), 4 * j + 1)
            scalar.activation(d_row[0:1, 1024:1536], d_row[0:1, 0:512], AF.Square
                              ).then_inc(sem("dRX"), 1)
            scalar.wait_ge(sem("dRX"), 4 * j + 3)
            scalar.activation(d_row[0:1, 512:1024], d_row[0:1, 512:1024], AF.Sqrt
                              ).then_inc(sem("dRX"), 1)

    @block.vector
    def _(vector):
        for j in range(NTD):
            for dc in range(8):
                vector.wait_ge(sem("dMM"), 8 * j + dc + 1)
                if j > 0 and dc == 0:
                    vector.wait_ge(sem("dS1"), j)
                    vector.wait_ge(sem("dSq"), j)
                vector.scalar_tensor_tensor(
                    d_y3[:, dc, :], p_d[dc % 2][:], d_bo[:, dc:dc + 1],
                    d_xr4[:, j % 2, dc, :], ALU.add, ALU.add,
                ).then_inc(sem("dY"), 1)
            vector.wait_ge(sem("dS1"), j + 1)
            vector.tensor_scalar_mul(d_row[0:1, 0:512], p_s1[0:1, :], 1.0 / 1024.0
                                     ).then_inc(sem("dRX"), 1)
            vector.wait_ge(sem("dS2"), j + 1)
            vector.tensor_scalar_mul(d_row[0:1, 1536:2048], p_s2[0:1, :], 1.0 / 1024.0)
            vector.wait_ge(sem("dRX"), 4 * j + 2)
            vector.tensor_sub(d_row[0:1, 512:1024], d_row[0:1, 1536:2048],
                              d_row[0:1, 1024:1536])
            vector.tensor_scalar_add(d_row[0:1, 512:1024], d_row[0:1, 512:1024], 1e-5)
            vector.reciprocal(d_row[0:1, 512:1024], d_row[0:1, 512:1024]
                              ).then_inc(sem("dRX"), 1)
            vector.wait_ge(sem("dBc"), j + 1)
            if j > 0:
                vector.wait_ge(sem("dX1"), 16 * j)
                if dbg == "x1":
                    vector.wait_ge(sem("dYo"), 16 * j)
            for dc in range(8):
                vector.tensor_sub(d_t13[:, dc, :], d_y3[:, dc, :], p_bm[:])
                vector.tensor_tensor(d_x13[:, dc, :], d_t13[:, dc, :], p_br[:],
                                     ALU.mult).then_inc(sem("dAp"), 1)
            if j == 2:
                vector.tensor_scalar_mul(
                    d_x1[:].rearrange("p (c n) -> p c n", n=512)[:, :, 511:512],
                    d_x1[:].rearrange("p (c n) -> p c n", n=512)[:, :, 511:512],
                    ctx["fl_sb"][:, 0:1]).then_inc(sem("dAp"), 1)

    @block.gpsimd
    def _(gpsimd):
        for j in range(NTD):
            gpsimd.wait_ge(sem("dAp"), apcum[j])
            gpsimd.dma_start(
                out=x1_scr.rearrange("(c p) n -> p c n", p=128)[:, :, 512 * j:512 * (j + 1)],
                in_=d_x13[:, :, :],
            ).then_inc(sem("dX1"), 16)
            if dbg == "x1":
                gpsimd.dma_start(
                    out=y_out.rearrange("(c p) n -> p c n", p=128)[:, :, 512 * j:512 * (j + 1)],
                    in_=d_x13[:, :, :],
                ).then_inc(sem("dYo"), 16)

    if dbg == "x1":
        ctx["done"].append(("dYo", 16 * NTD))
    ctx["done"].append(("dX1", 16 * NTD))



def emit_ffn_qkv(ctx, es, l):
    nc, block, sem, I, S = ctx["nc"], ctx["block"], ctx["sem"], ctx["I"], ctx["S"]
    SW, bump = ctx["SW"], ctx["bump"]
    sb = lambda nm, shape: es.enter_context(nc.sbuf_tensor(f"L{l}{nm}", shape, F32))
    ps = lambda nm, shape: es.enter_context(nc.psum_tensor(f"L{l}{nm}", shape, F32))

    x_src = S["x1_scr"] if l == 0 else S["x2_scr"]
    X0 = (T0_L - KA_L[0]) if l == 0 else (ML_L[0] - KA_L[1])
    TK = _tiles(NKA[l])
    NT = len(TK)

    xt = sb("qXt", [128, 8 * 513])
    tbl = sb("qTbl", [128, 4 * 512])
    wqc = sb("qWqc", [128, 2 * 8 * 128])
    wv = sb("qWv", [128, 8 * 512])
    qfb = sb("qQfb", [128, 4 * 512])
    kfb = sb("qKfb", [128, 4 * 512])
    qsw = sb("qQsw", [128, 4 * 512])
    ksw = sb("qKsw", [128, 4 * 512])
    vtb = sb("qVtb", [128, 4 * 544])
    rsc = sb("qRsc", [128, 512])
    sq_s = sb("qSqs", [128, 2 * 512])
    cp_s = sb("qCps", [128, 2 * 512])
    ss_s = sb("qSss", [2, 2 * 512])
    scl = sb("qScl", [128, 2])

    p_qk = [ps("qPqk0", [128, 512]), ps("qPqk1", [128, 512])]
    p_s = ps("qPs", [2, 512])
    p_b = ps("qPb", [128, 512])

    xt3 = xt[:].rearrange("p (c n) -> p c n", n=513)
    wqc4 = wqc[:].rearrange("p (b c n) -> p b c n", b=2, n=128)
    wv3 = wv[:].rearrange("p (c n) -> p c n", n=512)
    blk2_sb, blk2T_sb = ctx["blk2_sb"], ctx["blk2T_sb"]

    if l == 0:
        gates = [("dX1", 16 * 7, False)]
    else:
        gates = [("mX2", 0, True)]

    FMN, GS, CONS = {}, {}, {}
    n = g = nv = 0
    for i, (o, w) in enumerate(TK):
        for ci in range(8):
            FMN[(i, ci)] = n
            GS[(i, ("fm", ci))] = g
            CONS[g] = ("qSQ", n)
            n += 1
            g += 1
        for u in range(w // 128):
            GS[(i, ("v", u))] = g
            CONS[g] = ("qVE", nv)
            nv += 1
            g += 1
    N_FM, N_G, N_V = n, g, nv
    FMG = [GS[(i, ("fm", ci))] for i in range(NT) for ci in range(8)]
    VTOT = [sum(TK[j][1] // 128 for j in range(i + 1)) for i in range(NT)]

    @block.sync
    def _(sync):
        for gs, gv, rel in gates:
            sync.wait_ge(sem(gs), SW(gs, gv) if rel else gv)
        sync.dma_start(out=wv3, in_=I[("wqkv", l)].rearrange("(c p) n -> p c n", p=128)
                       [:, :, 1024:1536]).then_inc(sem("qW"), 16)
        sync.dma_start(out=scl[:, 0:1], in_=I[("qsc", l)][:, :]).then_inc(sem("qW"), 16)
        sync.dma_start(out=scl[:, 1:2], in_=I[("ksc", l)][:, :]).then_inc(sem("qW"), 16)
        wq = 0
        for i, (o, w) in enumerate(TK):
            if i >= 1:
                sync.wait_ge(sem("qMM"), SW("qMM", GS[(i - 1, ("v", TK[i - 1][1] // 128 - 1))] + 1))
            sync.dma_start(
                out=xt3[:, :, 0:w + 1],
                in_=x_src.rearrange("(c p) n -> p c n", p=128)[:, :, X0 + o - 1:X0 + o + w],
            ).then_inc(sem("qX"), 16)
            if i >= 1:
                sync.wait_ge(sem("qRotD"), SW("qRotD", 8 * i))
            for ti, nm in enumerate(["rAq", "rBq", "rAk", "rBk"]):
                sync.dma_start(out=tbl[:, 512 * ti:512 * ti + w],
                               in_=I[(nm, l)][:, o:o + w]).then_inc(sem("qT"), 16)
            for ci in range(8):
                if wq >= 2:
                    sync.wait_ge(sem("qMM"), SW("qMM", FMG[wq - 2] + 1))
                sync.dma_start(
                    out=wqc4[:, wq % 2, :, :],
                    in_=I[("wqkv", l)].rearrange("(c p) n -> p c n", p=128)
                    [:, :, 128 * ci:128 * (ci + 1)],
                ).then_inc(sem(f"qWQ{wq % 2}"), 16)
                wq += 1

    @block.gpsimd
    def _(gpsimd):
        for gs, gv, rel in gates:
            gpsimd.wait_ge(sem(gs), SW(gs, gv) if rel else gv)
        gpsimd.memset(vtb[:], 1.0)
        for i, (o, w) in enumerate(TK):
            nsub = w // 128
            gpsimd.wait_ge(sem("qFIN"), SW("qFIN", 8 * i + 4))
            for blk in range(4):
                so = 32 * ((blk + 1) if blk % 2 == 0 else (blk - 1))
                gpsimd.dma_start(out=qsw[32 * blk:32 * (blk + 1), :],
                                 in_=qfb[so:so + 32, :]).then_inc(sem("qSWQ"), 16)
            gpsimd.wait_ge(sem("qFIN"), SW("qFIN", 8 * i + 8))
            for blk in range(4):
                so = 32 * ((blk + 1) if blk % 2 == 0 else (blk - 1))
                gpsimd.dma_start(out=ksw[32 * blk:32 * (blk + 1), :],
                                 in_=kfb[so:so + 32, :]).then_inc(sem("qSWK"), 16)
            gpsimd.wait_ge(sem("qRotD"), SW("qRotD", 8 * i + 4))
            gpsimd.dma_start(
                out=S["qr_scr"].rearrange("(c p) n -> p c n", p=128)[:, :, o:o + w],
                in_=qfb[:].rearrange("p (c n) -> p c n", n=512)[:, :, 0:w],
            ).then_inc(sem("qQS"), 16)
            gpsimd.wait_ge(sem("qRotD"), SW("qRotD", 8 * i + 8))
            gpsimd.dma_start(
                out=S["kr_scr"].rearrange("(c p) n -> p c n", p=128)[:, :, o:o + w],
                in_=kfb[:].rearrange("p (c n) -> p c n", n=512)[:, :, 0:w],
            ).then_inc(sem("qKS"), 16)
            gpsimd.wait_ge(sem("qVE"), SW("qVE", VTOT[i]))
            gpsimd.dma_start(
                out=S["vt_scr"][:, 544 * (o // 128):544 * (o // 128 + nsub)],
                in_=vtb[:, 0:544 * nsub],
            ).then_inc(sem("qVS"), 16)

    @block.tensor
    def _(tensor):
        tensor.wait_ge(sem("qW"), SW("qW", 48))
        wq = 0
        for i, (o, w) in enumerate(TK):
            nsub = w // 128
            tensor.wait_ge(sem("qX"), SW("qX", 16 * (i + 1)))

            def t_blk(ci):
                nn = FMN[(i, ci)]
                tensor.wait_ge(sem("qSQ"), SW("qSQ", nn + 1))
                if nn > 0:
                    tensor.wait_ge(sem("qMX"), SW("qMX", nn))
                tensor.matmul(p_s[:, 0:w], blk2_sb[:, 0:2],
                              sq_s[:, 512 * (ci % 2):512 * (ci % 2) + w],
                              start=True, stop=True).then_inc(sem("qBLK"), 1)

            def t_bc(ci):
                nn = FMN[(i, ci)]
                tensor.wait_ge(sem("qRS"), SW("qRS", nn + 1))
                if nn > 0:
                    tensor.wait_ge(sem("qFIN"), SW("qFIN", nn))
                tensor.matmul(p_b[:, 0:w], blk2T_sb[0:2, :],
                              ss_s[:, 512 * (ci % 2):512 * (ci % 2) + w],
                              start=True, stop=True).then_inc(sem("qBC"), 1)

            for ci in range(8):
                g = GS[(i, ("fm", ci))]
                tensor.wait_ge(sem(f"qWQ{wq % 2}"), SW(f"qWQ{wq % 2}", 16 * (wq // 2 + 1)))
                if g >= 2:
                    kind, idx = CONS[g - 2]
                    tensor.wait_ge(sem(kind), SW(kind, idx + 1))
                for dc in range(8):
                    rhs = xt3[:, dc, 1:1 + w] if dc < 4 else xt3[:, dc, 0:w]
                    mm = tensor.matmul(p_qk[g % 2][:, 0:w], wqc4[:, wq % 2, dc, :],
                                       rhs, start=(dc == 0), stop=(dc == 7))
                mm.then_inc(sem("qMM"), 1)
                wq += 1
                t_blk(ci)
                if ci >= 1:
                    t_bc(ci - 1)
            t_bc(7)
            for u in range(nsub):
                g = GS[(i, ("v", u))]
                if g >= 2:
                    kind, idx = CONS[g - 2]
                    tensor.wait_ge(sem(kind), SW(kind, idx + 1))
                for dc in range(8):
                    lhs = xt3[:, dc, 1 + 128 * u:1 + 128 * (u + 1)] if dc < 4 \
                        else xt3[:, dc, 128 * u:128 * (u + 1)]
                    mm = tensor.matmul(p_qk[g % 2][:], lhs, wv3[:, dc, :],
                                       start=(dc == 0), stop=(dc == 7))
                mm.then_inc(sem("qMM"), 1)

    @block.scalar
    def _(scalar):
        for i, (o, w) in enumerate(TK):
            def s_sqrt(ci):
                nn = FMN[(i, ci)]
                scalar.wait_ge(sem("qMX"), SW("qMX", nn + 1))
                sl = ss_s[:, 512 * (ci % 2):512 * (ci % 2) + w]
                scalar.activation(sl, sl, AF.Sqrt).then_inc(sem("qRS"), 1)

            for ci in range(8):
                nn = FMN[(i, ci)]
                g = GS[(i, ("fm", ci))]
                scalar.wait_ge(sem("qMM"), SW("qMM", g + 1))
                if nn >= 2:
                    scalar.wait_ge(sem("qBLK"), SW("qBLK", nn - 1))
                    scalar.wait_ge(sem("qFIN"), SW("qFIN", nn - 1))
                sl2 = 512 * (ci % 2)
                scalar.activation(sq_s[:, sl2:sl2 + w], p_qk[g % 2][:, 0:w], AF.Square)
                scalar.copy(cp_s[:, sl2:sl2 + w], p_qk[g % 2][:, 0:w]
                            ).then_inc(sem("qSQ"), 1)
                if ci >= 1:
                    s_sqrt(ci - 1)
            s_sqrt(7)

    @block.vector
    def _(vector):
        for i, (o, w) in enumerate(TK):
            nsub = w // 128

            def v_mx(ci):
                nn = FMN[(i, ci)]
                vector.wait_ge(sem("qBLK"), SW("qBLK", nn + 1))
                if nn >= 2:
                    vector.wait_ge(sem("qBC"), SW("qBC", nn - 1))
                sl = ss_s[:, 512 * (ci % 2):512 * (ci % 2) + w]
                vector.tensor_scalar_max(sl, p_s[:, 0:w], 1e-24)
                if ci < 4:
                    vector.tensor_scalar_mul(sl, sl, 1.0 / 64.0)
                vector.reciprocal(sl, sl).then_inc(sem("qMX"), 1)

            def v_fin(ci):
                nn = FMN[(i, ci)]
                vector.wait_ge(sem("qBC"), SW("qBC", nn + 1))
                srcap = cp_s[:, 512 * (ci % 2):512 * (ci % 2) + w]
                if ci < 4:
                    if i > 0:
                        vector.wait_ge(sem("qQS"), SW("qQS", 16 * i))
                    dst = qfb[:, 512 * ci:512 * ci + w]
                    sc = scl[:, 0:1]
                else:
                    if i > 0:
                        vector.wait_ge(sem("qKS"), SW("qKS", 16 * i))
                    dst = kfb[:, 512 * (ci - 4):512 * (ci - 4) + w]
                    sc = scl[:, 1:2]
                vector.scalar_tensor_tensor(dst, srcap, sc, p_b[:, 0:w],
                                            ALU.mult, ALU.mult
                                            ).then_inc(sem("qFIN"), 1)

            for ci in range(8):
                v_mx(ci)
                if ci >= 1:
                    v_fin(ci - 1)
            v_fin(7)
            for u in range(nsub):
                g = GS[(i, ("v", u))]
                vector.wait_ge(sem("qMM"), SW("qMM", g + 1))
                if i > 0 and u == 0:
                    vector.wait_ge(sem("qVS"), SW("qVS", 16 * i))
                vector.tensor_copy(
                    vtb[:, 544 * u:544 * (u + 1)].rearrange(
                        "p (h e) -> p h e", e=68)[:, :, 0:64],
                    p_qk[g % 2][:].rearrange("p (h e) -> p h e", e=64),
                ).then_inc(sem("qVE"), 1)
            vector.wait_ge(sem("qSWQ"), SW("qSWQ", 64 * (i + 1)))
            vector.wait_ge(sem("qT"), SW("qT", 64 * (i + 1)))
            for ci in range(4):
                q_ap = qfb[:, 512 * ci:512 * ci + w]
                vector.tensor_mul(rsc[:, 0:w], q_ap, tbl[:, 0:w])
                vector.tensor_mul(q_ap, qsw[:, 512 * ci:512 * ci + w],
                                  tbl[:, 512:512 + w])
                vector.tensor_add(q_ap, q_ap, rsc[:, 0:w]).then_inc(sem("qRotD"), 1)
            vector.wait_ge(sem("qSWK"), SW("qSWK", 64 * (i + 1)))
            for ci in range(4):
                k_ap = kfb[:, 512 * ci:512 * ci + w]
                vector.tensor_mul(rsc[:, 0:w], k_ap, tbl[:, 1024:1024 + w])
                vector.tensor_mul(k_ap, ksw[:, 512 * ci:512 * ci + w],
                                  tbl[:, 1536:1536 + w])
                vector.tensor_add(k_ap, k_ap, rsc[:, 0:w]).then_inc(sem("qRotD"), 1)

    bump("qW", 48)
    bump("qX", 16 * NT)
    bump("qT", 64 * NT)
    bump("qWQ0", 16 * ((8 * NT + 1) // 2))
    bump("qWQ1", 16 * (8 * NT // 2))
    bump("qMM", N_G)
    bump("qSQ", N_FM)
    bump("qBLK", N_FM)
    bump("qMX", N_FM)
    bump("qRS", N_FM)
    bump("qBC", N_FM)
    bump("qFIN", N_FM)
    bump("qVE", N_V)
    bump("qSWQ", 64 * NT)
    bump("qSWK", 64 * NT)
    bump("qRotD", 8 * NT)
    bump("qQS", 16 * NT)
    bump("qKS", 16 * NT)
    bump("qVS", 16 * NT)
    ctx["q_tot"] = dict(NT=NT)


def emit_ffn_attn(ctx, es, l):
    nc, block, sem, I, S = ctx["nc"], ctx["block"], ctx["sem"], ctx["I"], ctx["S"]
    SW, bump = ctx["SW"], ctx["bump"]
    ones = ctx["ones"]
    sb = lambda nm, shape: es.enter_context(nc.sbuf_tensor(f"L{l}{nm}", shape, F32))
    ps = lambda nm, shape: es.enter_context(nc.psum_tensor(f"L{l}{nm}", shape, F32))

    x_src = S["x1_scr"] if l == 0 else S["x2_scr"]
    X0 = (T0_L - KA_L[0]) if l == 0 else (ML_L[0] - KA_L[1])
    xmid = S["xmid_scr"]
    QT = _tiles(NQA[l])
    NQT = len(QT)
    NT_Q = len(_tiles(NKA[l]))

    q_sb = sb("aQ", [128, 4 * 512])
    k_sb = sb("aK", [128, 4 * 1024])
    vt_sb = sb("aVt", [128, 8 * 544])
    msk = sb("aMsk", [128, 8 * 512])
    xt = sb("aXt", [128, 8 * 512])
    wos = sb("aWos", [128, 4 * 1024])
    ptE = sb("aPtE", [128, 4 * 1024])
    of = sb("aOf", [128, 4 * 512])
    otm = sb("aOtm", [128, 2 * 512])
    orc = sb("aOrc", [128, 2 * 512])
    xm = sb("aXm", [128, 8 * 512])
    ngM = sb("aNgM", [128, 1])
    row = sb("aRow", [1, 256])

    p_se = [ps("aPse0", [128, 1024]), ps("aPse1", [128, 1024])]
    p_oe = [ps("aPoe0", [128, 512]), ps("aPoe1", [128, 512])]
    p_wo = [ps("aPwo0", [128, 512]), ps("aPwo1", [128, 512])]

    k_sb3 = k_sb[:].rearrange("p (c n) -> p c n", n=1024)
    xt3 = xt[:].rearrange("p (c n) -> p c n", n=512)
    wos3 = wos[:].rearrange("p (c n) -> p c n", n=1024)
    xm3 = xm[:].rearrange("p (c n) -> p c n", n=512)

    moff, mc = [], 0
    for (qo, qw) in QT:
        moff.append(mc)
        mc += ((qw + 512) // 128) * qw

    FL_COL = QA_L[l] - 1
    FL_QT = next(qi for qi, (qo, qw) in enumerate(QT) if qo <= FL_COL < qo + qw)
    FL_LOC = FL_COL - QT[FL_QT][0]

    STM, OMC, RCC, AS, WIDE = {}, {}, {}, {}, []
    stm = om = rc = 0
    for qi, (qo, qw) in enumerate(QT):
        asub = (qw + 512) // 128
        awp = (asub + 1) // 2
        AS[qi] = (asub, awp)
        for p in range(4):
            for wp in range(awp):
                STM[(qi, p, wp)] = stm
                WIDE.append((qi, p, wp))
                stm += 1
            for u in range(asub):
                OMC[(qi, p, u)] = om
                om += 1
            RCC[(qi, p)] = rc
            rc += 1
    N_STM, N_OM, N_RC = stm, om, rc

    def om_of_wide(sgl):
        qi, p, wp = WIDE[sgl]
        asub, _ = AS[qi]
        last_u = min(2, asub - 2 * wp) + 2 * wp - 1
        return OMC[(qi, p, last_u)] + 1

    @block.sync
    def _(sync):
        sync.wait_ge(sem("qQS"), SW("qQS", 0))
        sync.wait_ge(sem("qKS"), SW("qKS", 0))
        sync.wait_ge(sem("qVS"), SW("qVS", 0))
        sync.dma_start(out=wos3, in_=I[("wo", l)].rearrange("(c p) n -> p c n", p=128)
                       ).then_inc(sem("aW"), 16)
        sync.dma_start(out=row[0:1, 0:64], in_=I[("qsr", l)][0:1, :]).then_inc(sem("aW"), 16)
        sync.dma_start(out=row[0:1, 64:128], in_=I[("ksr", l)][0:1, :]).then_inc(sem("aW"), 16)
        for qi, (qo, qw) in enumerate(QT):
            asub, awp = AS[qi]
            if qi >= 1:
                sync.wait_ge(sem("aSTM"), SW("aSTM", STM[(qi - 1, 3, AS[qi - 1][1] - 1)] + 1))
            sync.dma_start(
                out=q_sb[:].rearrange("p (c n) -> p c n", n=512)[:, :, 0:qw],
                in_=S["qr_scr"].rearrange("(c p) n -> p c n", p=128)
                [:, :, qo + 512:qo + 512 + qw],
            ).then_inc(sem("aQL"), 16)
            sync.dma_start(
                out=k_sb3[:, :, 0:qw + 512],
                in_=S["kr_scr"].rearrange("(c p) n -> p c n", p=128)
                [:, :, qo:qo + qw + 512],
            ).then_inc(sem("aQL"), 16)
            if qi >= 1:
                sync.wait_ge(sem("aOM"), SW("aOM", OMC[(qi - 1, 3, AS[qi - 1][0] - 1)] + 1))
            sync.dma_start(
                out=vt_sb[:, 0:544 * asub],
                in_=S["vt_scr"][:, 544 * (qo // 128):544 * (qo // 128 + asub)],
            ).then_inc(sem("aVL"), 16)
            if qi >= 1:
                sync.wait_ge(sem("aMS"), SW("aMS", STM[(qi - 1, 3, AS[qi - 1][1] - 1)] + 1))
            sync.dma_start(
                out=msk[:, 0:asub * qw],
                in_=I[("mask", l)][:, moff[qi]:moff[qi] + asub * qw],
            ).then_inc(sem("aML"), 16)
            if qi >= 1:
                sync.wait_ge(sem("aXM"), SW("aXM", 8 * qi))
            sync.dma_start(
                out=xt3[:, :, 0:qw],
                in_=x_src.rearrange("(c p) n -> p c n", p=128)
                [:, :, X0 + qo + 512:X0 + qo + 512 + qw],
            ).then_inc(sem("aXL"), 16)

    @block.tensor
    def _(tensor):
        tensor.wait_ge(sem("aNG"), SW("aNG", 1))
        tensor.matmul(p_wo[0][:, 0:1], ones[0:1, :], row[0:1, 72:73],
                      start=True, stop=True).then_inc(sem("aNG"), 1)
        for qi, (qo, qw) in enumerate(QT):
            asub, awp = AS[qi]
            tensor.wait_ge(sem("aQL"), SW("aQL", 32 * (qi + 1)))
            tensor.wait_ge(sem("aVL"), SW("aVL", 16 * (qi + 1)))
            for p in range(4):
                pidx = RCC[(qi, p)]
                for wp in range(awp):
                    nsu = min(2, asub - 2 * wp)
                    sgl = STM[(qi, p, wp)]
                    if sgl >= 1:
                        tensor.wait_ge(sem("aEX"), SW("aEX", 2 * sgl))
                    if wp == 0 and pidx >= 1:
                        tensor.wait_ge(sem("aOF"), SW("aOF", 2 * pidx))
                    for u2 in range(nsu):
                        u = 2 * wp + u2
                        for j in range(2):
                            mm = tensor.matmul(
                                p_se[j][:, 512 * u2:512 * u2 + qw],
                                k_sb3[64 * j:64 * (j + 1), p, 128 * u:128 * (u + 1)],
                                q_sb[64 * j:64 * (j + 1), 512 * p:512 * p + qw],
                                start=True, stop=True, tile_position=(64 * j, 0))
                    mm.then_inc(sem("aSTM"), 1)
                    tensor.wait_ge(sem("aEX"), SW("aEX", 2 * (sgl + 1)))
                    for u2 in range(nsu):
                        u = 2 * wp + u2
                        par = sgl % 2
                        h0 = 2 * p
                        h1 = 2 * p + 1
                        mm = tensor.matmul(
                            p_oe[0][0:65, 0:qw],
                            vt_sb[:, 544 * u + 68 * h0:544 * u + 68 * h0 + 65],
                            ptE[:, 1024 * (0 + 2 * par) + 512 * u2:
                                1024 * (0 + 2 * par) + 512 * u2 + qw],
                            start=(u == 0), stop=(u == asub - 1))
                        mm2 = tensor.matmul(
                            p_oe[1][64:128, 0:qw],
                            vt_sb[:, 544 * u + 68 * h1:544 * u + 68 * h1 + 64],
                            ptE[:, 1024 * (1 + 2 * par) + 512 * u2:
                                1024 * (1 + 2 * par) + 512 * u2 + qw],
                            start=(u == 0), stop=(u == asub - 1),
                            tile_position=(0, 64))
                        mm3 = tensor.matmul(
                            p_oe[1][0:1, 0:qw],
                            vt_sb[:, 544 * u + 68 * h1 + 64:544 * u + 68 * h1 + 65],
                            ptE[:, 1024 * (1 + 2 * par) + 512 * u2:
                                1024 * (1 + 2 * par) + 512 * u2 + qw],
                            start=(u == 0), stop=(u == asub - 1),
                            tile_position=(0, 0))
                        mm3.then_inc(sem("aOM"), 1)
                tensor.wait_ge(sem("aRC"), SW("aRC", 2 * pidx + 2))
                tensor.matmul(p_se[0][:, 0:qw], ones[64:65, :],
                              orc[64:65, 0:qw], start=True, stop=True,
                              tile_position=(64, 0)).then_inc(sem("aB2"), 1)
                tensor.matmul(p_se[0][:, 512:512 + qw], ones[0:1, :],
                              orc[0:1, 512:512 + qw], start=True, stop=True
                              ).then_inc(sem("aB2"), 1)
            tensor.wait_ge(sem("aOF"), SW("aOF", 8 * (qi + 1)))
            tensor.wait_ge(sem("aXL"), SW("aXL", 16 * (qi + 1)))
            for dc in range(8):
                nwo = 8 * qi + dc
                if nwo >= 2:
                    tensor.wait_ge(sem("aXM"), SW("aXM", nwo - 1))
                for ic in range(4):
                    mm = tensor.matmul(p_wo[dc % 2][:, 0:qw],
                                       wos3[:, ic, 128 * dc:128 * (dc + 1)],
                                       of[:, 512 * ic:512 * ic + qw],
                                       start=(ic == 0), stop=(ic == 3))
                mm.then_inc(sem("aWOM"), 1)

    @block.scalar
    def _(scalar):
        for qi, (qo, qw) in enumerate(QT):
            asub, awp = AS[qi]
            for p in range(4):
                for wp in range(awp):
                    nsu = min(2, asub - 2 * wp)
                    sgl = STM[(qi, p, wp)]
                    par = sgl % 2
                    scalar.wait_ge(sem("aMS"), SW("aMS", sgl + 1))
                    if sgl >= 2:
                        scalar.wait_ge(sem("aOM"), SW("aOM", om_of_wide(sgl - 2)))
                    for j in range(2):
                        scalar.activation(
                            ptE[:, 1024 * (j + 2 * par):
                                1024 * (j + 2 * par) + 512 * nsu].rearrange(
                                "p (u e) -> p u e", e=512)[:, :, 0:qw],
                            p_se[j][:, 0:512 * nsu].rearrange(
                                "p (u e) -> p u e", e=512)[:, :, 0:qw],
                            AF.Exp, bias=ngM[:, 0:1],
                        ).then_inc(sem("aEX"), 1)

    @block.vector
    def _(vector):
        vector.wait_ge(sem("aW"), SW("aW", 48))
        vector.reduce_max(row[0:1, 128:129], row[0:1, 0:64], AX.X,
                          apply_absolute_value=True)
        vector.reduce_max(row[0:1, 129:130], row[0:1, 64:128], AX.X,
                          apply_absolute_value=True)
        vector.tensor_mul(row[0:1, 130:131], row[0:1, 128:129], row[0:1, 129:130])
        vector.tensor_scalar(row[0:1, 72:73], row[0:1, 130:131], -8.0, -1.0,
                             ALU.mult, ALU.add).then_inc(sem("aNG"), 1)
        vector.wait_ge(sem("aNG"), SW("aNG", 2))
        vector.tensor_copy(ngM[:, 0:1], p_wo[0][:, 0:1]).then_inc(sem("aNG"), 1)
        for qi, (qo, qw) in enumerate(QT):
            asub, awp = AS[qi]
            vector.wait_ge(sem("aML"), SW("aML", 16 * (qi + 1)))
            for p in range(4):
                pidx = RCC[(qi, p)]
                for wp in range(awp):
                    nsu = min(2, asub - 2 * wp)
                    sgl = STM[(qi, p, wp)]
                    vector.wait_ge(sem("aSTM"), SW("aSTM", sgl + 1))
                    mrow = (2 * wp) * qw
                    for j in range(2):
                        ins = vector.scalar_tensor_tensor(
                            p_se[j][:, 0:512 * nsu].rearrange(
                                "p (u e) -> p u e", e=512)[:, :, 0:qw],
                            p_se[j][:, 0:512 * nsu].rearrange(
                                "p (u e) -> p u e", e=512)[:, :, 0:qw],
                            0.0,
                            msk[:, mrow:mrow + nsu * qw].rearrange(
                                "p (u e) -> p u e", e=qw),
                            ALU.add, ALU.add)
                    ins.then_inc(sem("aMS"), 1)
                vector.wait_ge(sem("aOM"), SW("aOM", OMC[(qi, p, asub - 1)] + 1))
                vector.reciprocal(orc[64:65, 0:qw],
                                  p_oe[0][64:65, 0:qw]).then_inc(sem("aRC"), 1)
                vector.reciprocal(orc[0:1, 512:512 + qw],
                                  p_oe[1][0:1, 0:qw]).then_inc(sem("aRC"), 1)
                vector.wait_ge(sem("aB2"), SW("aB2", 2 * pidx + 2))
                vector.tensor_copy(otm[0:64, 0:qw], p_oe[0][0:64, 0:qw])
                vector.tensor_tensor(of[0:64, 512 * p:512 * p + qw],
                                     otm[0:64, 0:qw],
                                     p_se[0][0:64, 0:qw],
                                     ALU.mult).then_inc(sem("aOF"), 1)
                vector.tensor_copy(otm[64:128, 512:512 + qw], p_oe[1][64:128, 0:qw])
                vector.tensor_tensor(of[64:128, 512 * p:512 * p + qw],
                                     otm[64:128, 512:512 + qw],
                                     p_se[0][64:128, 512:512 + qw],
                                     ALU.mult).then_inc(sem("aOF"), 1)
            for dc in range(8):
                vector.wait_ge(sem("aWOM"), SW("aWOM", 8 * qi + dc + 1))
                ins = vector.scalar_tensor_tensor(
                    xm3[:, dc, 0:qw], p_wo[dc % 2][:, 0:qw], 0.0,
                    xt3[:, dc, 0:qw], ALU.add, ALU.add)
                ins.then_inc(sem("aXM"), 1)
            if qi == FL_QT:
                vector.tensor_scalar_mul(
                    xm3[:, :, FL_LOC:FL_LOC + 1], xm3[:, :, FL_LOC:FL_LOC + 1],
                    ctx["fl_sb"][:, 0:1]).then_inc(sem("aFLG"), 1)

    @block.gpsimd
    def _(gpsimd):
        for qi, (qo, qw) in enumerate(QT):
            gpsimd.wait_ge(sem("aXM"), SW("aXM", 8 * (qi + 1)))
            if qi == FL_QT:
                gpsimd.wait_ge(sem("aFLG"), SW("aFLG", 1))
            gpsimd.dma_start(
                out=xmid.rearrange("(c p) n -> p c n", p=128)[:, :, qo:qo + qw],
                in_=xm3[:, :, 0:qw],
            ).then_inc(sem("aXMS"), 16)

    bump("aW", 48)
    bump("aNG", 3)
    bump("aQL", 32 * NQT)
    bump("aVL", 16 * NQT)
    bump("aML", 16 * NQT)
    bump("aXL", 16 * NQT)
    bump("aSTM", N_STM)
    bump("aEX", 2 * N_STM)
    bump("aMS", N_STM)
    bump("aOM", N_OM)
    bump("aRC", 2 * N_RC)
    bump("aB2", 2 * N_RC)
    bump("aOF", 2 * N_RC)
    bump("aWOM", 8 * NQT)
    bump("aXM", 8 * NQT)
    bump("aFLG", 1)
    bump("aXMS", 16 * NQT)
    ctx["a_stores"] = 16 * NQT


def emit_ffn_mlp(ctx, es, l):
    nc, block, sem, I, S = ctx["nc"], ctx["block"], ctx["sem"], ctx["I"], ctx["S"]
    SW, bump = ctx["SW"], ctx["bump"]
    ones = ctx["ones"]
    y_out = ctx["y_out"]
    sb = lambda nm, shape: es.enter_context(nc.sbuf_tensor(f"L{l}{nm}", shape, F32))
    sbb = lambda nm, shape: es.enter_context(nc.sbuf_tensor(f"L{l}{nm}", shape, BF16))
    ps = lambda nm, shape: es.enter_context(nc.psum_tensor(f"L{l}{nm}", shape, F32))

    xmid = S["xmid_scr"]
    TM = _tiles(NML[l])
    NTM = len(TM)
    MO = QA_L[l] - ML_L[l]

    xmt = sb("mXmt", [128, 8 * 513])
    w1c = sb("mW1c", [128, 2 * 8 * 256])
    w1h = sbb("mW1h", [128, 2 * 8 * 256])
    w1l = sbb("mW1l", [128, 2 * 8 * 256])
    w2c = sb("mW2c", [128, 2 * 22 * 128])
    w2h = sbb("mW2h", [128, 2 * 22 * 128])
    w2l = sbb("mW2l", [128, 2 * 22 * 128])
    hsq = sb("mHsq", [128, 8 * 512])
    hnt = sb("mHnt", [128, 2 * 512])
    hnh = sbb("mHnh", [128, 8 * 512])
    hnl = sbb("mHnl", [128, 8 * 512])
    hidt = sb("mHidt", [128, 2 * 512])
    hidh = sbb("mHidh", [128, 22 * 512])
    hidl = sbb("mHidl", [128, 22 * 512])
    x2o = sb("mX2o", [128, 8 * 512])
    x1s = sb("mX1s", [128, 2 * 512]) if l == 1 else None
    rowm = sb("mRow", [1, 1536])
    cns = sb("mCns", [128, 2 * NHP + 24])

    p_m1 = ps("mPm1", [1, 512])
    p_m2 = ps("mPm2", [1, 512])
    p_bm = ps("mPbm", [128, 512])
    p_br = ps("mPbr", [128, 512])
    p_a = ps("mPa", [128, 512])
    p_g = ps("mPg", [128, 512])
    p_w = [ps("mPw0", [128, 512]), ps("mPw1", [128, 512])]

    xmt3 = xmt[:].rearrange("p (c n) -> p c n", n=513)
    w1c4 = w1c[:].rearrange("p (b c n) -> p b c n", b=2, n=256)
    w1h4 = w1h[:].rearrange("p (b c n) -> p b c n", b=2, n=256)
    w1l4 = w1l[:].rearrange("p (b c n) -> p b c n", b=2, n=256)
    w2c4 = w2c[:].rearrange("p (b c n) -> p b c n", b=2, n=128)
    w2h4 = w2h[:].rearrange("p (b c n) -> p b c n", b=2, n=128)
    w2l4 = w2l[:].rearrange("p (b c n) -> p b c n", b=2, n=128)
    hsq3 = hsq[:].rearrange("p (c n) -> p c n", n=512)
    hnt2 = hnt[:].rearrange("p (b n) -> p b n", b=2)
    hnh3 = hnh[:].rearrange("p (c n) -> p c n", n=512)
    hnl3 = hnl[:].rearrange("p (c n) -> p c n", n=512)
    hidt2 = hidt[:].rearrange("p (b n) -> p b n", b=2)
    hidh3 = hidh[:].rearrange("p (c n) -> p c n", n=512)
    hidl3 = hidl[:].rearrange("p (c n) -> p c n", n=512)
    x2o3 = x2o[:].rearrange("p (c n) -> p c n", n=512)
    x1s2 = x1s[:].rearrange("p (b n) -> p b n", b=2) if l == 1 else None
    B1C = 0
    B2C = 2 * NHP
    LGC = 2 * NHP + 8
    LBC = 2 * NHP + 16

    FLM_COL = ML_L[l] - 1
    if l == 0:
        FLM_MT = FLM_COL // 512
        FLM_LOC = FLM_COL - 512 * FLM_MT
    dst = S["x2_scr"] if l == 0 else None

    def hm_ap(dc, W):
        return xmt3[:, dc, 1:1 + W] if dc < 4 else xmt3[:, dc, 0:W]

    @block.sync
    def _(sync):
        sync.wait_ge(sem("aXMS"), SW("aXMS", 0))
        sync.dma_start(out=cns[:, B1C:B1C + 2 * NHP], in_=I[("b1c", l)][:, :]
                       ).then_inc(sem("mW"), 16)
        sync.dma_start(out=cns[:, B2C:B2C + 8], in_=I[("b2c", l)][:, :]
                       ).then_inc(sem("mW"), 16)
        sync.dma_start(out=cns[:, LGC:LGC + 8], in_=I[("lgc", l)][:, :]
                       ).then_inc(sem("mW"), 16)
        sync.dma_start(out=cns[:, LBC:LBC + 8], in_=I[("lbc", l)][:, :]
                       ).then_inc(sem("mW"), 16)
        wq = 0
        w2q = 0
        for mi, (mo, W) in enumerate(TM):
            if mi >= 1:
                sync.wait_ge(sem("mX2E"), SW("mX2E", 8 * mi))
            sync.dma_start(
                out=xmt3[:, :, 0:W + 1],
                in_=xmid.rearrange("(c p) n -> p c n", p=128)
                [:, :, MO + mo - 1:MO + mo + W],
            ).then_inc(sem("mXL"), 16)
            for hp in range(NHP):
                if wq >= 2:
                    sync.wait_ge(sem("mW1M"), SW("mW1M", wq - 1))
                sync.dma_start(
                    out=w1c4[:, wq % 2, :, :],
                    in_=I[("w1r", l)].rearrange("(c p) n -> p c n", p=128)
                    [:, :, 256 * hp:256 * (hp + 1)],
                ).then_inc(sem(f"mWQ{wq % 2}"), 16)
                wq += 1
            for dc in range(8):
                if w2q >= 2:
                    sync.wait_ge(sem("mW2M"), SW("mW2M", w2q - 1))
                sync.dma_start(
                    out=w2c4[:, w2q % 2, :, :],
                    in_=I[("w2p", l)].rearrange("(c p) n -> p c n", p=128)
                    [:, :, 128 * dc:128 * (dc + 1)],
                ).then_inc(sem(f"mW2Q{w2q % 2}"), 16)
                w2q += 1

    @block.tensor
    def _(tensor):
        tensor.wait_ge(sem("mW"), SW("mW", 64))
        wq = 0
        w2q = 0
        for mi, (mo, W) in enumerate(TM):
            nrow = 4 if l == 0 else 8
            tensor.wait_ge(sem("mXL"), SW("mXL", 16 * (mi + 1)))
            if mi > 0:
                tensor.wait_ge(sem("mRow"), SW("mRow", nrow * mi))
            for dc in range(8):
                mm = tensor.matmul(p_m1[:, 0:W], ones[:, 0:1], hm_ap(dc, W),
                                   start=(dc == 0), stop=(dc == 7))
            mm.then_inc(sem("mS1"), 1)
            tensor.wait_ge(sem("mSq"), SW("mSq", (2 if l == 1 else 1) * mi + 1))
            for dc in range(8):
                mm = tensor.matmul(p_m2[:, 0:W], ones[:, 0:1], hsq3[:, dc, 0:W],
                                   start=(dc == 0), stop=(dc == 7))
            mm.then_inc(sem("mS2"), 1)
            tensor.wait_ge(sem("mRow"), SW("mRow", nrow * mi + 4))
            if mi > 0:
                tensor.wait_ge(sem("mAP"), SW("mAP", 24 * mi))
            tensor.matmul(p_bm[:, 0:W], ones[0:1, :], rowm[0:1, 0:W],
                          start=True, stop=True)
            tensor.matmul(p_br[:, 0:W], ones[0:1, :], rowm[0:1, 512:512 + W],
                          start=True, stop=True).then_inc(sem("mBC"), 1)
            for hp in range(NHP):
                hpg = NHP * mi + hp
                tensor.wait_ge(sem("mSP"), SW("mSP", 60 * mi + 9 + 2 * hp))
                if hpg > 0:
                    tensor.wait_ge(sem("mHP"), SW("mHP", hpg))
                    tensor.wait_ge(sem("mGE"), SW("mGE", hpg))
                W1P = [(w1h4, hnh3), (w1h4, hnl3), (w1l4, hnh3)]
                for pi, (wv, xv) in enumerate(W1P):
                    for dc in range(8):
                        mm = tensor.matmul(p_a[:, 0:W], wv[:, wq % 2, dc, 0:128],
                                           xv[:, dc, 0:W],
                                           start=(pi == 0 and dc == 0),
                                           stop=(pi == 2 and dc == 7))
                for pi, (wv, xv) in enumerate(W1P):
                    for dc in range(8):
                        mm2 = tensor.matmul(p_g[:, 0:W], wv[:, wq % 2, dc, 128:256],
                                            xv[:, dc, 0:W],
                                            start=(pi == 0 and dc == 0),
                                            stop=(pi == 2 and dc == 7))
                mm2.then_inc(sem("mW1M"), 1)
                wq += 1
            for dc in range(8):
                ndc = 8 * mi + dc
                tensor.wait_ge(sem("mSP"), SW("mSP", 60 * mi + 53 + dc))
                tensor.wait_ge(sem("mHP"), SW("mHP", NHP * (mi + 1)))
                if ndc >= 2:
                    tensor.wait_ge(sem("mX2E"), SW("mX2E", ndc - 1))
                W2P = [(w2h4, hidh3), (w2h4, hidl3), (w2l4, hidh3)]
                for pi, (wv, xv) in enumerate(W2P):
                    for hp in range(NHP):
                        mm = tensor.matmul(p_w[dc % 2][:, 0:W], wv[:, w2q % 2, hp, :],
                                           xv[:, hp, 0:W],
                                           start=(pi == 0 and hp == 0),
                                           stop=(pi == 2 and hp == NHP - 1))
                mm.then_inc(sem("mW2M"), 1)
                w2q += 1
            if l == 1:
                tensor.wait_ge(sem("mY3"), SW("mY3", 8 * (mi + 1)))
                tensor.wait_ge(sem("mRow"), SW("mRow", nrow * mi + 1))
                for dc in range(8):
                    mm = tensor.matmul(p_m1[:, 0:W], ones[:, 0:1], x2o3[:, dc, 0:W],
                                       start=(dc == 0), stop=(dc == 7))
                mm.then_inc(sem("mS1"), 1)
                tensor.wait_ge(sem("mSq"), SW("mSq", 2 * mi + 2))
                for dc in range(8):
                    mm = tensor.matmul(p_m2[:, 0:W], ones[:, 0:1], hsq3[:, dc, 0:W],
                                       start=(dc == 0), stop=(dc == 7))
                mm.then_inc(sem("mS2"), 1)
                tensor.wait_ge(sem("mRow"), SW("mRow", nrow * mi + 8))
                tensor.wait_ge(sem("mAP"), SW("mAP", 24 * (mi + 1)))
                if mi > 0:
                    tensor.wait_ge(sem("mAP2"), SW("mAP2", 16 * mi))
                tensor.matmul(p_bm[:, 0:W], ones[0:1, :], rowm[0:1, 0:W],
                              start=True, stop=True)
                tensor.matmul(p_br[:, 0:W], ones[0:1, :], rowm[0:1, 512:512 + W],
                              start=True, stop=True).then_inc(sem("mBC"), 1)

    @block.scalar
    def _(scalar):
        wq = 0
        w2q = 0
        for mi, (mo, W) in enumerate(TM):
            nrow = 4 if l == 0 else 8
            scalar.wait_ge(sem("mXL"), SW("mXL", 16 * (mi + 1)))
            if mi > 0:
                scalar.wait_ge(sem("mS2"), SW("mS2", (2 if l == 1 else 1) * mi))
                scalar.wait_ge(sem("mAP"), SW("mAP", 24 * mi))
                if l == 1:
                    scalar.wait_ge(sem("mAP2"), SW("mAP2", 16 * mi))
            scalar.activation(hsq3[:, 0:4, 0:W], xmt3[:, 0:4, 1:1 + W], AF.Square)
            scalar.activation(hsq3[:, 4:8, 0:W], xmt3[:, 4:8, 0:W], AF.Square
                              ).then_inc(sem("mSq"), 1)
            scalar.wait_ge(sem("mRow"), SW("mRow", nrow * mi + 1))
            scalar.activation(rowm[0:1, 1024:1024 + W], rowm[0:1, 0:W], AF.Square
                              ).then_inc(sem("mRow"), 1)
            scalar.wait_ge(sem("mRow"), SW("mRow", nrow * mi + 3))
            scalar.activation(rowm[0:1, 512:512 + W], rowm[0:1, 512:512 + W], AF.Sqrt
                              ).then_inc(sem("mRow"), 1)
            for dc in range(8):
                if dc == 0 and (l > 0 or mi > 0):
                    scalar.wait_ge(sem("mW1M"), SW("mW1M", NHP * mi))
                scalar.wait_ge(sem("mAP"), SW("mAP", 24 * mi + 3 * (dc + 1)))
                scalar.activation(hnh3[:, dc, 0:W], hnt2[:, dc % 2, 0:W], AF.Copy
                                  ).then_inc(sem("mSH"), 1)
            for hp in range(NHP):
                hpg = NHP * mi + hp
                scalar.wait_ge(sem(f"mWQ{wq % 2}"), SW(f"mWQ{wq % 2}", 16 * (wq // 2 + 1)))
                if wq >= 2:
                    scalar.wait_ge(sem("mW1M"), SW("mW1M", wq - 1))
                scalar.activation(w1h4[:, wq % 2, :, :], w1c4[:, wq % 2, :, :],
                                  AF.Copy).then_inc(sem("mSH"), 1)
                scalar.wait_ge(sem("mW1M"), SW("mW1M", hpg + 1))
                if hpg > 0:
                    scalar.wait_ge(sem("mHP"), SW("mHP", hpg))
                scalar.activation(hnt2[:, hp % 2, 0:W], p_g[:, 0:W], AF.Gelu,
                                  bias=cns[:, B1C + 2 * hp + 1:B1C + 2 * hp + 2]
                                  ).then_inc(sem("mGE"), 1)
                scalar.wait_ge(sem("mSP"), SW("mSP", 60 * mi + 10 + 2 * hp))
                scalar.activation(hidh3[:, hp, 0:W], hidt2[:, hp % 2, 0:W],
                                  AF.Copy).then_inc(sem("mSH"), 1)
                wq += 1
            for dc in range(8):
                scalar.wait_ge(sem(f"mW2Q{w2q % 2}"), SW(f"mW2Q{w2q % 2}", 16 * (w2q // 2 + 1)))
                if w2q >= 2:
                    scalar.wait_ge(sem("mW2M"), SW("mW2M", w2q - 1))
                scalar.activation(w2h4[:, w2q % 2, :, :], w2c4[:, w2q % 2, :, :],
                                  AF.Copy).then_inc(sem("mSH"), 1)
                w2q += 1
            if l == 1:
                scalar.wait_ge(sem("mY3"), SW("mY3", 8 * (mi + 1)))
                scalar.activation(hsq[:, 0:8 * 512].rearrange(
                    "p (c n) -> p c n", n=512)[:, :, 0:W],
                    x2o3[:, :, 0:W], AF.Square).then_inc(sem("mSq"), 1)
                scalar.wait_ge(sem("mRow"), SW("mRow", nrow * mi + 5))
                scalar.activation(rowm[0:1, 1024:1024 + W], rowm[0:1, 0:W], AF.Square
                                  ).then_inc(sem("mRow"), 1)
                scalar.wait_ge(sem("mRow"), SW("mRow", nrow * mi + 7))
                scalar.activation(rowm[0:1, 512:512 + W], rowm[0:1, 512:512 + W], AF.Sqrt
                                  ).then_inc(sem("mRow"), 1)

    @block.vector
    def _(vector):
        wq = 0
        w2q = 0
        for mi, (mo, W) in enumerate(TM):
            nrow = 4 if l == 0 else 8
            vector.wait_ge(sem("mS1"), SW("mS1", (2 if l == 1 else 1) * mi + 1))
            vector.tensor_scalar_mul(rowm[0:1, 0:W], p_m1[0:1, 0:W], 1.0 / 1024.0
                                     ).then_inc(sem("mRow"), 1)
            vector.wait_ge(sem("mRow"), SW("mRow", nrow * mi + 2))
            vector.wait_ge(sem("mS2"), SW("mS2", (2 if l == 1 else 1) * mi + 1))
            vector.tensor_scalar_mul(rowm[0:1, 512:512 + W], p_m2[0:1, 0:W],
                                     1.0 / 1024.0)
            vector.tensor_sub(rowm[0:1, 512:512 + W], rowm[0:1, 512:512 + W],
                              rowm[0:1, 1024:1024 + W])
            vector.tensor_scalar_add(rowm[0:1, 512:512 + W], rowm[0:1, 512:512 + W],
                                     1e-5)
            vector.reciprocal(rowm[0:1, 512:512 + W], rowm[0:1, 512:512 + W]
                              ).then_inc(sem("mRow"), 1)
            vector.wait_ge(sem("mBC"), SW("mBC", (2 if l == 1 else 1) * mi + 1))
            for dc in range(8):
                if dc == 0 and (l > 0 or mi > 0):
                    vector.wait_ge(sem("mW1M"), SW("mW1M", NHP * mi))
                if dc >= 2:
                    vector.wait_ge(sem("mSH"), SW("mSH", 60 * mi + dc - 1))
                vector.tensor_sub(hsq3[:, dc, 0:W], hm_ap(dc, W), p_bm[:, 0:W]
                                  ).then_inc(sem("mAP"), 1)
                vector.tensor_tensor(hnt2[:, dc % 2, 0:W], hsq3[:, dc, 0:W],
                                     p_br[:, 0:W],
                                     ALU.mult).then_inc(sem("mAP"), 1)
                vector.tensor_scalar(hnt2[:, dc % 2, 0:W], hnt2[:, dc % 2, 0:W],
                                     cns[:, LGC + dc:LGC + dc + 1],
                                     cns[:, LBC + dc:LBC + dc + 1],
                                     ALU.mult, ALU.add).then_inc(sem("mAP"), 1)
                vector.wait_ge(sem("mSH"), SW("mSH", 60 * mi + dc + 1))
                vector.tensor_sub(hnl3[:, dc, 0:W], hnt2[:, dc % 2, 0:W],
                                  hnh3[:, dc, 0:W]).then_inc(sem("mSP"), 1)
            for hp in range(NHP):
                hpg = NHP * mi + hp
                vector.wait_ge(sem(f"mWQ{wq % 2}"), SW(f"mWQ{wq % 2}", 16 * (wq // 2 + 1)))
                vector.wait_ge(sem("mSH"), SW("mSH", 60 * mi + 9 + 2 * hp))
                if wq >= 2:
                    vector.wait_ge(sem("mW1M"), SW("mW1M", wq - 1))
                vector.tensor_sub(w1l4[:, wq % 2, :, :], w1c4[:, wq % 2, :, :],
                                  w1h4[:, wq % 2, :, :]).then_inc(sem("mSP"), 1)
                vector.wait_ge(sem("mGE"), SW("mGE", hpg + 1))
                if hp >= 2:
                    vector.wait_ge(sem("mSH"), SW("mSH", 60 * mi + 6 + 2 * hp))
                elif l > 0 or mi > 0:
                    vector.wait_ge(sem("mSH"), SW("mSH", 60 * mi - 10 + 2 * hp))
                ins = vector.scalar_tensor_tensor(
                    hidt2[:, hp % 2, 0:W], p_a[:, 0:W],
                    cns[:, B1C + 2 * hp:B1C + 2 * hp + 1], hnt2[:, hp % 2, 0:W],
                    ALU.add, ALU.mult)
                ins.then_inc(sem("mSP"), 1)
                vector.wait_ge(sem("mSH"), SW("mSH", 60 * mi + 10 + 2 * hp))
                vector.tensor_sub(hidl3[:, hp, 0:W], hidt2[:, hp % 2, 0:W],
                                  hidh3[:, hp, 0:W]).then_inc(sem("mHP"), 1)
                wq += 1
            for k in range(10):
                if k < 8:
                    vector.wait_ge(sem(f"mW2Q{w2q % 2}"), SW(f"mW2Q{w2q % 2}", 16 * (w2q // 2 + 1)))
                    vector.wait_ge(sem("mSH"), SW("mSH", 60 * mi + 53 + k))
                    if w2q >= 2:
                        vector.wait_ge(sem("mW2M"), SW("mW2M", w2q - 1))
                    vector.tensor_sub(w2l4[:, w2q % 2, :, :], w2c4[:, w2q % 2, :, :],
                                      w2h4[:, w2q % 2, :, :]).then_inc(sem("mSP"), 1)
                    w2q += 1
                if k >= 2:
                    dc = k - 2
                    ndc = 8 * mi + dc
                    if l == 1 and dc == 0 and mi > 0:
                        vector.wait_ge(sem("mST"), SW("mST", 16 * mi))
                    vector.wait_ge(sem("mW2M"), SW("mW2M", ndc + 1))
                    ins = vector.scalar_tensor_tensor(
                        x2o3[:, dc, 0:W], p_w[dc % 2][:, 0:W],
                        cns[:, B2C + dc:B2C + dc + 1], xmt3[:, dc, 1:1 + W],
                        ALU.add, ALU.add)
                    ins.then_inc(sem("mX2E"), 1)
            if l == 0:
                if mi == FLM_MT:
                    vector.tensor_scalar_mul(
                        x2o3[:, :, FLM_LOC:FLM_LOC + 1],
                        x2o3[:, :, FLM_LOC:FLM_LOC + 1],
                        ctx["fl_sb"][:, 0:1]).then_inc(sem("mFLG"), 1)
            else:
                for dc in range(8):
                    vector.wait_ge(sem("mX1L"), SW("mX1L", 16 * (8 * mi + dc + 1)))
                    vector.tensor_add(x2o3[:, dc, 0:W], x2o3[:, dc, 0:W],
                                      x1s2[:, dc % 2, 0:W]).then_inc(sem("mY3"), 1)
                vector.wait_ge(sem("mS1"), SW("mS1", 2 * mi + 2))
                vector.tensor_scalar_mul(rowm[0:1, 0:W], p_m1[0:1, 0:W], 1.0 / 1024.0
                                         ).then_inc(sem("mRow"), 1)
                vector.wait_ge(sem("mRow"), SW("mRow", 8 * mi + 6))
                vector.wait_ge(sem("mS2"), SW("mS2", 2 * mi + 2))
                vector.tensor_scalar_mul(rowm[0:1, 512:512 + W], p_m2[0:1, 0:W],
                                         1.0 / 1024.0)
                vector.tensor_sub(rowm[0:1, 512:512 + W], rowm[0:1, 512:512 + W],
                                  rowm[0:1, 1024:1024 + W])
                vector.tensor_scalar_add(rowm[0:1, 512:512 + W],
                                         rowm[0:1, 512:512 + W], 1e-5)
                vector.reciprocal(rowm[0:1, 512:512 + W], rowm[0:1, 512:512 + W]
                                  ).then_inc(sem("mRow"), 1)
                vector.wait_ge(sem("mBC"), SW("mBC", 2 * mi + 2))
                if mi > 0:
                    vector.wait_ge(sem("mST"), SW("mST", 16 * mi))
                for dc in range(8):
                    vector.tensor_sub(hsq3[:, dc, 0:W], x2o3[:, dc, 0:W], p_bm[:, 0:W]
                                      ).then_inc(sem("mAP2"), 1)
                    vector.tensor_tensor(x2o3[:, dc, 0:W], hsq3[:, dc, 0:W],
                                         p_br[:, 0:W], ALU.mult
                                         ).then_inc(sem("mAP2"), 1)

    @block.gpsimd
    def _(gpsimd):
        for mi, (mo, W) in enumerate(TM):
            if l == 0:
                gpsimd.wait_ge(sem("mX2E"), SW("mX2E", 8 * (mi + 1)))
                if mi == FLM_MT:
                    gpsimd.wait_ge(sem("mFLG"), SW("mFLG", 1))
                gpsimd.dma_start(
                    out=dst.rearrange("(c p) n -> p c n", p=128)[:, :, mo:mo + W],
                    in_=x2o3[:, :, 0:W],
                ).then_inc(sem("mX2"), 16)
            else:
                for dc in range(8):
                    n1 = 8 * mi + dc
                    if n1 >= 2:
                        gpsimd.wait_ge(sem("mY3"), SW("mY3", n1 - 1))
                    gpsimd.dma_start(
                        out=x1s2[:, dc % 2, 0:W],
                        in_=S["x1_scr"].rearrange("(c p) n -> p c n", p=128)
                        [:, dc, T0_L + mo:T0_L + mo + W],
                    ).then_inc(sem("mX1L"), 16)
                gpsimd.wait_ge(sem("mAP2"), SW("mAP2", 16 * (mi + 1)))
                gpsimd.dma_start(
                    out=y_out.rearrange("(c p) n -> p c n", p=128)[:, :, mo:mo + W],
                    in_=x2o3[:, :, 0:W],
                ).then_inc(sem("mST"), 16)

    n_ln = 2 if l == 1 else 1
    bump("mW", 64)
    bump("mXL", 16 * NTM)
    bump("mWQ0", 16 * ((NHP * NTM + 1) // 2))
    bump("mWQ1", 16 * (NHP * NTM // 2))
    bump("mW2Q0", 16 * ((8 * NTM + 1) // 2))
    bump("mW2Q1", 16 * (8 * NTM // 2))
    bump("mS1", n_ln * NTM)
    bump("mS2", n_ln * NTM)
    bump("mSq", n_ln * NTM)
    bump("mRow", (4 if l == 0 else 8) * NTM)
    bump("mBC", n_ln * NTM)
    bump("mAP", 24 * NTM)
    bump("mW1M", NHP * NTM)
    bump("mGE", NHP * NTM)
    bump("mHP", NHP * NTM)
    bump("mSH", 60 * NTM)
    bump("mSP", 60 * NTM)
    bump("mW2M", 8 * NTM)
    bump("mX2E", 8 * NTM)
    if l == 0:
        bump("mFLG", 1)
        bump("mX2", 16 * NTM)
        ctx["m0_tiles"] = NTM
    else:
        bump("mY3", 8 * NTM)
        bump("mX1L", 128 * NTM)
        bump("mAP2", 16 * NTM)
        bump("mST", 16 * NTM)
        ctx["done"].append(("mST", ctx["SW"]("mST", 0)))

def emit_finish(ctx):
    block, sem = ctx["block"], ctx["sem"]
    done = list(ctx["done"])
    if ctx["dbg"] == "l1a":
        done.append(("aXMS", ctx["SW"]("aXMS", 0)))

    @block.gpsimd
    def _(gpsimd):
        for name, val in done:
            gpsimd.wait_ge(sem(name), val)



def kernel(**inputs):
    in_maps = _prepare(inputs)
    nc = build_nc(DEBUG_STAGE)
    res = run_bass_kernel_spmd(nc, in_maps, core_ids=list(range(8)))
    if DEBUG_STAGE is not None:
        return [r["y_out"] for r in res.results]
    out = np.empty((2, SEQ, D), dtype=np.float32)
    for c in range(8):
        b, g = c // 4, c % 4
        out[b, 2048 * g:2048 * (g + 1)] = res.results[c]["y_out"].T
    return out


# version 03 (final)
# speedup vs baseline: 1.0299x; 1.0180x over previous
import os
os.environ.setdefault("JAX_COMPILATION_CACHE_DIR", "/tmp/jaxcache")
os.environ.setdefault("JAX_PERSISTENT_CACHE_MIN_COMPILE_TIME_SECS", "1")
os.environ.setdefault("JAX_PERSISTENT_CACHE_MIN_ENTRY_SIZE_BYTES", "-1")

import contextlib
import numpy as np

import concourse.bass as bass
import concourse.mybir as mybir
from concourse.bass_utils import run_bass_kernel_spmd

F32 = mybir.dt.float32
BF16 = mybir.dt.bfloat16
AF = mybir.ActivationFunctionType
ALU = mybir.AluOpType
AX = mybir.AxisListType

D = 1024
SEQ = 8192
DH = 64
WINDOW = 512
HID = 2730
HID_PAD = 2816
NHP = HID_PAD // 128
QK_SCALE = 8.0

T0_L = 1536
KA_L = [1280, 640]
QA_L = [768, 128]
ML_L = [704, 0]
NT0 = 2048 + T0_L
NKA = [2048 + KA_L[0], 2048 + KA_L[1]]
NQA = [2048 + QA_L[0], 2048 + QA_L[1]]
NML = [2048 + ML_L[0], 2048]

NEG = np.float32(-1e30)
DEBUG_STAGE = None


def _tiles(n, t=512):
    out, o = [], 0
    while o < n:
        w = min(t, n - o)
        out.append((o, w))
        o += w
    return out



def _rotary_tables(pos):
    half = np.arange(0, DH, 2, dtype=np.float64)
    inv_freq = 1.0 / 10000.0 ** (half / DH)
    t = pos.astype(np.float64)
    freqs = np.concatenate([t[:, None] * inv_freq[None, :]] * 2, axis=-1)
    base = (half + 0.4 * DH) / (1.4 * DH)
    power = (t - SEQ // 2) / float(WINDOW // 2)
    sc = base[None, :] ** power[:, None]
    sc2 = np.concatenate([sc, sc], axis=-1)
    cos, sin = np.cos(freqs), np.sin(freqs)
    sgn = np.concatenate([-np.ones(32), np.ones(32)])
    Aq = (cos * sc2).T
    Bq = (sgn[None, :] * sin * sc2).T
    Ak = (cos / sc2).T
    Bk = (sgn[None, :] * sin / sc2).T
    t2 = lambda m: np.ascontiguousarray(np.tile(m.astype(np.float32), (2, 1)))
    return t2(Aq), t2(Bq), t2(Ak), t2(Bk)


def _ffn_masks(s, li):
    qa0 = s - QA_L[li]
    cols = []
    for (qo, qw) in _tiles(NQA[li]):
        q_abs = qa0 + qo + np.arange(qw)
        k_lo = (qa0 + qo) - 512
        nsub = (qw + 512) // 128
        for u in range(nsub):
            k_abs = k_lo + u * 128 + np.arange(128)
            i = q_abs[None, :]
            j = k_abs[:, None]
            m = (i < j) | (i > j + WINDOW) | (j < 0)
            first = (j == k_lo) & (i < 0)
            m = m & ~first
            cols.append(np.where(m, NEG, np.float32(0.0)))
    return np.ascontiguousarray(np.concatenate(cols, axis=1), dtype=np.float32)


def _mask_cols(li):
    return sum(((qw + 512) // 128) * qw for (qo, qw) in _tiles(NQA[li]))


def _prepare(inputs):
    f32 = lambda a: np.asarray(a, dtype=np.float32)
    x = f32(inputs["L0_attn_x"])
    w_q = f32(inputs["L0_attn_w_q"])
    w_kv = f32(inputs["L0_attn_w_kv"])
    w_out = f32(inputs["L0_attn_w_out"])
    b_out = f32(inputs["L0_attn_b_out"])
    scale = f32(inputs["L0_attn_scale"]).reshape(16)
    wqkv = f32(inputs["L0_ffn_wqkv"])
    qsc = f32(inputs["L0_ffn_q_scale"])
    ksc = f32(inputs["L0_ffn_k_scale"])
    wo = f32(inputs["L0_ffn_wo"])
    ln_g = f32(inputs["L0_ffn_ln_g"])
    ln_b = f32(inputs["L0_ffn_ln_b"])
    w1 = f32(inputs["L0_ffn_w1"])
    b1 = f32(inputs["L0_ffn_b1"])
    w2 = f32(inputs["L0_ffn_w2"])
    b2 = f32(inputs["L0_ffn_b2"])

    def reorder_w1(w1l, b1l):
        a = np.pad(w1l[:, :HID], ((0, 0), (0, HID_PAD - HID)))
        g = np.pad(w1l[:, HID:], ((0, 0), (0, HID_PAD - HID)))
        blocks = []
        for i in range(NHP):
            blocks += [a[:, i * 128:(i + 1) * 128], g[:, i * 128:(i + 1) * 128]]
        w1r = np.ascontiguousarray(np.concatenate(blocks, axis=1), dtype=np.float32)
        ab = np.pad(b1l[:HID], (0, HID_PAD - HID))
        gb = np.pad(b1l[HID:], (0, HID_PAD - HID))
        b1r = np.concatenate([np.concatenate([ab[i * 128:(i + 1) * 128],
                                              gb[i * 128:(i + 1) * 128]])
                              for i in range(NHP)]).astype(np.float32)
        return w1r, np.ascontiguousarray(b1r.reshape(2 * NHP, 128).T)

    w1r_b1c = [reorder_w1(w1[l], b1[l]) for l in range(2)]
    w2p = [np.ascontiguousarray(np.pad(w2[l], ((0, HID_PAD - HID), (0, 0))))
           for l in range(2)]
    colmaj = lambda v: np.ascontiguousarray(v.reshape(-1, 128).T)

    blk2 = np.zeros((128, 2), dtype=np.float32)
    blk2[0:64, 0] = 1.0
    blk2[64:128, 1] = 1.0
    blk2T = np.ascontiguousarray(blk2.T)

    in_maps = []
    for c in range(8):
        b, g = c // 4, c % 4
        s = 2048 * g
        xb = np.ascontiguousarray(x[b].T)
        idx = np.arange(s - T0_L, s + 2048)
        xres = np.ascontiguousarray(xb[:, idx % SEQ] * (idx >= 0))
        m = {
            "xg": xb,
            "x_res": xres,
            "wq_g": np.ascontiguousarray(w_q[:, g * 256:(g + 1) * 256]),
            "wk_g": np.ascontiguousarray(w_kv[:, g * 256:(g + 1) * 256]),
            "wv_g": np.ascontiguousarray(w_kv[:, 1024 + g * 256:1024 + (g + 1) * 256]),
            "w_out": w_out,
            "b_out_c": colmaj(b_out),
            "scale_my": np.ascontiguousarray(scale[g * 4:(g + 1) * 4].reshape(1, 4)),
            "flag": np.full((128, 1), 0.0 if g == 0 else 1.0, dtype=np.float32),
            "blk2": blk2,
            "blk2T": blk2T,
        }
        for l in range(2):
            ka0 = s - KA_L[l]
            Aq, Bq, Ak, Bk = _rotary_tables(np.arange(ka0, s + 2048))
            m[f"rAq{l}"], m[f"rBq{l}"] = Aq, Bq
            m[f"rAk{l}"], m[f"rBk{l}"] = Ak, Bk
            m[f"mask{l}"] = _ffn_masks(s, l)
            m[f"wqkv{l}"] = np.ascontiguousarray(wqkv[l])
            m[f"wo{l}"] = np.ascontiguousarray(wo[l])
            m[f"w1r{l}"] = w1r_b1c[l][0]
            m[f"b1c{l}"] = w1r_b1c[l][1]
            m[f"w2p{l}"] = w2p[l]
            m[f"b2c{l}"] = colmaj(b2[l])
            m[f"lgc{l}"] = colmaj(ln_g[l])
            m[f"lbc{l}"] = colmaj(ln_b[l])
            m[f"qsc{l}"] = np.ascontiguousarray(np.tile(qsc[l], 2).reshape(128, 1))
            m[f"ksc{l}"] = np.ascontiguousarray(np.tile(ksc[l], 2).reshape(128, 1))
            m[f"qsr{l}"] = np.ascontiguousarray(qsc[l].reshape(1, 64))
            m[f"ksr{l}"] = np.ascontiguousarray(ksc[l].reshape(1, 64))
        in_maps.append(m)
    return in_maps



def build_nc(debug_stage=None):
    nc = bass.Bass("TRN2", target_bir_lowering=False, debug=False,
                   detect_race_conditions=False)
    dt = lambda name, shape, kind: nc.dram_tensor(name, shape, F32, kind=kind).ap()
    I = {}
    for nm, shape in [
        ("xg", [D, SEQ]), ("x_res", [D, NT0]),
        ("wq_g", [D, 256]), ("wk_g", [D, 256]), ("wv_g", [D, 256]),
        ("w_out", [D, D]), ("b_out_c", [128, 8]), ("scale_my", [1, 4]),
        ("flag", [128, 1]), ("blk2", [128, 2]), ("blk2T", [2, 128]),
    ]:
        I[nm] = dt(nm, shape, "ExternalInput")
    for l in range(2):
        for nm, shape in [
            ("rAq", [128, NKA[l]]), ("rBq", [128, NKA[l]]),
            ("rAk", [128, NKA[l]]), ("rBk", [128, NKA[l]]),
            ("mask", [128, _mask_cols(l)]),
            ("wqkv", [D, 1536]), ("wo", [512, D]),
            ("w1r", [D, 2 * HID_PAD]), ("b1c", [128, 2 * NHP]),
            ("w2p", [HID_PAD, D]), ("b2c", [128, 8]),
            ("lgc", [128, 8]), ("lbc", [128, 8]),
            ("qsc", [128, 1]), ("ksc", [128, 1]),
            ("qsr", [1, 64]), ("ksr", [1, 64]),
        ]:
            I[(nm, l)] = dt(f"{nm}{l}", shape, "ExternalInput")

    S = {
        "q_scr": dt("q_scr", [256, SEQ], "Internal"),
        "cc_in": dt("cc_in", [256, SEQ], "Internal"),
        "cc_out0": dt("cc_out0", [512, SEQ], "Internal"),
        "cc_out1": dt("cc_out1", [512, SEQ], "Internal"),
        "att_w": dt("att_w", [D, NT0], "Internal"),
        "x1_scr": dt("x1_scr", [D, NT0], "Internal"),
        "xmid_scr": dt("xmid_scr", [D, max(NQA)],
                       "ExternalOutput" if debug_stage in ("x2", "l1a") else "Internal"),
        "x2_scr": dt("x2_scr", [D, NML[0]],
                     "ExternalOutput" if debug_stage in ("x2", "l1a") else "Internal"),
        "qr_scr": dt("qr_scr", [512, NKA[0]],
                     "ExternalOutput" if debug_stage in ("x2", "l1a") else "Internal"),
        "kr_scr": dt("kr_scr", [512, NKA[0]],
                     "ExternalOutput" if debug_stage in ("x2", "l1a") else "Internal"),
        "vt_scr": dt("vt_scr", [128, (NKA[0] // 128) * 544],
                     "ExternalOutput" if debug_stage in ("x2", "l1a") else "Internal"),
    }
    out_w = NT0 if debug_stage == "x1" else (NML[0] if debug_stage == "x2" else 2048)
    y_out = dt("y_out", [D, out_w], "ExternalOutput")

    es = contextlib.ExitStack()
    with es:
        block = es.enter_context(nc.Block())
        sems = {}

        def sem(name):
            if name not in sems:
                sems[name] = es.enter_context(nc.semaphore(name))
            return sems[name]

        bases = {}

        def SW(name, rel):
            return bases.get(name, 0) + rel

        def bump(name, total):
            bases[name] = bases.get(name, 0) + total

        ctx = dict(nc=nc, block=block, sem=sem, I=I, S=S, y_out=y_out,
                   dbg=debug_stage, done=[], SW=SW, bump=bump)
        ctx["ones"] = es.enter_context(nc.sbuf_tensor("ones", [128, 128], F32))
        ctx["blk2_sb"] = es.enter_context(nc.sbuf_tensor("blk2_sb", [128, 2], F32))
        ctx["blk2T_sb"] = es.enter_context(nc.sbuf_tensor("blk2T_sb", [2, 128], F32))
        ctx["fl_sb"] = es.enter_context(nc.sbuf_tensor("fl_sb", [128, 1], F32))
        with contextlib.ExitStack() as es_ab:
            emit_phase_AB(ctx, es_ab)
        emit_phase_C(ctx)
        with contextlib.ExitStack() as es_d:
            emit_phase_D(ctx, es_d)
        if debug_stage != "x1":
            for l in range(1 if debug_stage == "x2" else 2):
                with contextlib.ExitStack() as es_q:
                    emit_ffn_qkv(ctx, es_q, l)
                with contextlib.ExitStack() as es_e:
                    emit_ffn_attn(ctx, es_e, l)
                if debug_stage == "l1a" and l == 1:
                    break
                with contextlib.ExitStack() as es_m:
                    emit_ffn_mlp(ctx, es_m, l)
        emit_finish(ctx)
    return nc



def emit_phase_AB(ctx, es):
    nc, block, sem, I, S = ctx["nc"], ctx["block"], ctx["sem"], ctx["I"], ctx["S"]
    sb = lambda nm, shape: es.enter_context(nc.sbuf_tensor(nm, shape, F32))

    k_n = sb("k_n", [128, 2 * SEQ])
    v_tm = sb("v_tm", [128, 64 * 272])
    negm = sb("negm", [128, 4])
    ones = ctx["ones"]
    es_pa = contextlib.ExitStack()
    sba = lambda nm, shape: es_pa.enter_context(nc.sbuf_tensor(nm, shape, F32))
    a_w = sba("a_w", [128, 8 * 3 * 256])
    a_x = sba("a_x", [128, 2 * 8 * 512])
    a_sq = sba("a_sq", [128, 2 * 512])
    a_cp = sba("a_cp", [128, 2 * 512])
    a_ss = sba("a_ss", [2, 2 * 512])
    a_row = sba("a_row", [2, 64])
    a_out = sba("a_out", [128, 2 * 512])
    blk2_sb = ctx["blk2_sb"]
    blk2T_sb = ctx["blk2T_sb"]

    psa = lambda nm, shape: es_pa.enter_context(nc.psum_tensor(nm, shape, F32))
    p_qk = [psa(f"p_qk{i}", [128, 512]) for i in range(4)]
    p_v = [psa(f"p_v{i}", [128, 256]) for i in range(2)]
    p_s = psa("p_s", [2, 512])
    p_b = psa("p_b", [128, 512])

    NT = 16
    xg, q_scr = I["xg"], S["q_scr"]
    a_w4 = a_w[:].rearrange("p (c k n) -> p c k n", k=3, n=256)
    a_x4 = a_x[:].rearrange("p (b c n) -> p b c n", b=2, n=512)

    def grp(t, gi):
        return 8 * t + gi + 1

    @block.sync
    def _(sync):
        for ki, w in enumerate([I["wq_g"], I["wk_g"], I["wv_g"]]):
            sync.dma_start(out=a_w4[:, :, ki, :],
                           in_=w.rearrange("(c p) n -> p c n", p=128)
                           ).then_inc(sem("AW"), 16)
        sync.dma_start(out=a_row[0:1, 0:4], in_=I["scale_my"][0:1, :]
                       ).then_inc(sem("AW"), 16)
        sync.dma_start(out=blk2_sb[:], in_=I["blk2"][:, :]).then_inc(sem("AW"), 16)
        sync.dma_start(out=blk2T_sb[:], in_=I["blk2T"][:, :]).then_inc(sem("AW"), 16)
        sync.dma_start(out=ctx["fl_sb"][:], in_=I["flag"][:, :]).then_inc(sem("AW"), 16)
        for t in range(NT):
            if t >= 2:
                sync.wait_ge(sem("AMM"), grp(t - 2, 7))
            sync.dma_start(
                out=a_x4[:, t % 2],
                in_=xg.rearrange("(c p) s -> p c s", p=128)[:, :, 512 * t:512 * (t + 1)],
            ).then_inc(sem(f"AX{t % 2}"), 16)

    @block.gpsimd
    def _(gpsimd):
        gpsimd.memset(v_tm[:], 1.0)
        gpsimd.memset(ones[:], 1.0).then_inc(sem("AOn"), 1)
        gpsimd.wait_ge(sem("ANg"), 1)
        for j in range(2):
            gpsimd.dma_start(
                out=a_row[j:j + 1, 32:34],
                in_=bass.AP(a_row[0:1, 0:1].tensor, 4 + j, [[64, 1], [2, 2]]),
                allow_slow_non_contiguous=True,
            ).then_inc(sem("AEsc"), 16)
        for t in range(NT):
            for oc in range(2):
                gpsimd.wait_ge(sem("AFin"), 4 * t + oc + 1)
                gpsimd.dma_start(
                    out=q_scr[128 * oc:128 * (oc + 1), 512 * t:512 * (t + 1)],
                    in_=a_out[:, 512 * oc:512 * (oc + 1)],
                ).then_inc(sem(f"AQs{oc}"), 16)

    @block.scalar
    def _(scalar):
        scalar.wait_ge(sem("AW"), 112)
        scalar.activation(a_row[0:1, 4:8], a_row[0:1, 0:4], AF.Exp
                          ).then_inc(sem("ANg"), 1)
        def a_square(t, ci):
            n = 4 * t + ci
            sl2 = 512 * (ci % 2)
            scalar.wait_ge(sem("AMM"), grp(t, ci))
            if n >= 2:
                scalar.wait_ge(sem("ABlk"), n - 1)
                scalar.wait_ge(sem("AFin"), n - 1)
            scalar.activation(a_sq[:, sl2:sl2 + 512], p_qk[ci][:], AF.Square)
            scalar.copy(a_cp[:, sl2:sl2 + 512], p_qk[ci][:]).then_inc(sem("ASq"), 1)

        def a_sqrt(t, ci):
            sl2 = 512 * (ci % 2)
            scalar.wait_ge(sem("AMx"), 4 * t + ci + 1)
            sl = a_ss[:, sl2:sl2 + 512]
            ins = scalar.activation(sl, sl, AF.Sqrt)
            if ci < 2:
                if t == 0 and ci == 0:
                    scalar.wait_ge(sem("AEsc"), 32)
                ins = scalar.mul(sl, sl, a_row[0:2, 32 + ci:33 + ci])
            ins.then_inc(sem("ARs"), 1)

        for t in range(NT):
            for ci in range(4):
                a_square(t, ci)
                if ci >= 1:
                    a_sqrt(t, ci - 1)
            a_sqrt(t, 3)

    @block.vector
    def _(vector):
        vector.wait_ge(sem("ANg"), 1)
        vector.tensor_scalar(a_row[0:1, 8:12], a_row[0:1, 4:8], -1.0, -0.01,
                             ALU.mult, ALU.add).then_inc(sem("ANg"), 1)
        vector.wait_ge(sem("ANg"), 3)
        vector.tensor_copy(negm[:], p_b[:, 0:4]).then_inc(sem("ANg"), 1)
        for t in range(NT):
            for vs in range(4):
                vector.wait_ge(sem("AMM"), grp(t, 4 + vs))
                kc = 4 * t + vs
                vector.tensor_copy(
                    v_tm[:, 272 * kc:272 * (kc + 1)].rearrange(
                        "p (h n) -> p h n", n=68)[:, :, 0:64],
                    p_v[vs % 2][:].rearrange("p (h n) -> p h n", n=64),
                ).then_inc(sem("AVe"), 1)
            def a_mx(ci):
                n = 4 * t + ci
                vector.wait_ge(sem("ABlk"), n + 1)
                if n >= 2:
                    vector.wait_ge(sem("ABc"), n - 1)
                sl = a_ss[:, 512 * (ci % 2):512 * (ci % 2) + 512]
                vector.tensor_scalar_max(sl, p_s[:], 1e-24)
                vector.reciprocal(sl, sl).then_inc(sem("AMx"), 1)

            def a_fin(ci):
                vector.wait_ge(sem("ABc"), 4 * t + ci + 1)
                srcap = a_cp[:, 512 * (ci % 2):512 * (ci % 2) + 512]
                if ci < 2:
                    if t > 0:
                        vector.wait_ge(sem(f"AQs{ci}"), 16 * t)
                    dst = a_out[:, 512 * ci:512 * (ci + 1)]
                else:
                    dst = k_n[:, SEQ * (ci - 2) + 512 * t:SEQ * (ci - 2) + 512 * (t + 1)]
                vector.tensor_tensor(dst, srcap, p_b[:], ALU.mult
                                     ).then_inc(sem("AFin"), 1)

            for ci in range(4):
                a_mx(ci)
                if ci >= 1:
                    a_fin(ci - 1)
            a_fin(3)

    @block.tensor
    def _(tensor):
        tensor.wait_ge(sem("AW"), 112)
        tensor.wait_ge(sem("AOn"), 1)
        tensor.wait_ge(sem("ANg"), 2)
        tensor.matmul(p_b[:, 0:4], ones[0:1, :], a_row[0:1, 8:12],
                      start=True, stop=True).then_inc(sem("ANg"), 1)
        for t in range(NT):
            tensor.wait_ge(sem(f"AX{t % 2}"), 16 * (t // 2 + 1))
            xt = a_x4[:, t % 2]
            for ci in range(4):
                ki = 0 if ci < 2 else 1
                base = 128 * (ci % 2)
                if t > 0:
                    tensor.wait_ge(sem("AFin"), 4 * (t - 1) + ci + 1)
                for dc in range(8):
                    mm = tensor.matmul(p_qk[ci][:], a_w4[:, dc, ki, base:base + 128],
                                       xt[:, dc, :], start=(dc == 0), stop=(dc == 7))
                mm.then_inc(sem("AMM"), 1)
            for vs in range(4):
                n = 4 * t + vs
                if n >= 2:
                    tensor.wait_ge(sem("AVe"), n - 1)
                for dc in range(8):
                    mm = tensor.matmul(p_v[vs % 2][:], xt[:, dc, 128 * vs:128 * (vs + 1)],
                                       a_w4[:, dc, 2, :], start=(dc == 0), stop=(dc == 7))
                mm.then_inc(sem("AMM"), 1)
            def a_blk(ci):
                tensor.wait_ge(sem("ASq"), 4 * t + ci + 1)
                if 4 * t + ci > 0:
                    tensor.wait_ge(sem("AMx"), 4 * t + ci)
                tensor.matmul(p_s[:], blk2_sb[:, 0:2],
                              a_sq[:, 512 * (ci % 2):512 * (ci % 2) + 512],
                              start=True, stop=True).then_inc(sem("ABlk"), 1)

            def a_bc(ci):
                tensor.wait_ge(sem("ARs"), 4 * t + ci + 1)
                if t == 0 and ci == 0:
                    tensor.wait_ge(sem("ANg"), 4)
                else:
                    tensor.wait_ge(sem("AFin"), 4 * t + ci)
                tensor.matmul(p_b[:], blk2T_sb[0:2, :],
                              a_ss[:, 512 * (ci % 2):512 * (ci % 2) + 512],
                              start=True, stop=True).then_inc(sem("ABc"), 1)

            for ci in range(4):
                a_blk(ci)
                if ci >= 1:
                    a_bc(ci - 1)
            a_bc(3)

    es_pa.close()
    ctx["k_n"], ctx["v_tm"], ctx["negm"] = k_n, v_tm, negm
    emit_phase_B(ctx, es)



def emit_phase_B(ctx, es):
    nc, block, sem = ctx["nc"], ctx["block"], ctx["sem"]
    k_n, v_tm, negm, ones = ctx["k_n"], ctx["v_tm"], ctx["negm"], ctx["ones"]
    q_scr, cc_in = ctx["S"]["q_scr"], ctx["S"]["cc_in"]
    sb = lambda nm, shape: es.enter_context(nc.sbuf_tensor(nm, shape, F32))

    qld = sb("b_qld", [128, 2 * 512])
    pt = sb("b_pt", [128, 4 * 1536])
    oo = sb("b_oo", [128, 4 * 512])
    otmp = sb("b_otmp", [128, 2 * 512])
    orc = sb("b_orc", [128, 2 * 512])

    es_pb = contextlib.ExitStack()
    psb = lambda nm, shape: es_pb.enter_context(nc.psum_tensor(nm, shape, F32))
    ps_s = [psb("b_s0", [128, 1536]), psb("b_s1", [128, 1536])]
    po = [psb("b_o0", [128, 512]), psb("b_o1", [128, 512])]

    GROUPS = [3] * 21 + [1]
    NG = len(GROUPS)
    NIT = 32

    def base(it):
        return it * NG

    def o_mms(tensor, it, p, g, kc0, gw):
        for u in range(gw):
            kc = kc0 + u
            for j in range(2):
                mm = tensor.matmul(
                    po[j][0:65, :],
                    v_tm[:, 272 * kc + 68 * (2 * p + j):272 * kc + 68 * (2 * p + j) + 65],
                    pt[:, 1536 * (2 * j + g % 2) + 512 * u:
                       1536 * (2 * j + g % 2) + 512 * (u + 1)],
                    start=(kc == 0), stop=(kc == 63))
        mm.then_inc(sem("bO"), 1)

    @block.sync
    def _(sync):
        for it in range(NIT):
            p, qt = it // 16, it % 16
            if it == 0:
                sync.wait_ge(sem("AQs0"), 16 * 16)
                sync.wait_ge(sem("AQs1"), 16 * 16)
            if it >= 2:
                sync.wait_ge(sem("bS"), base(it - 2) + NG)
            sync.dma_start(
                out=qld[:, 512 * (it % 2):512 * (it % 2 + 1)],
                in_=q_scr[128 * p:128 * (p + 1), 512 * qt:512 * (qt + 1)],
            ).then_inc(sem(f"bQ{it % 2}"), 16)

    @block.tensor
    def _(tensor):
        tensor.wait_ge(sem("AFin"), 64)
        tensor.wait_ge(sem("AVe"), 64)
        for it in range(NIT):
            p, qt = it // 16, it % 16
            q_ap = qld[:, 512 * (it % 2):512 * (it % 2 + 1)]
            tensor.wait_ge(sem(f"bQ{it % 2}"), 16 * (it // 2 + 1))
            if it > 0:
                tensor.wait_ge(sem("bT"), 2 * it)
            kc = 0
            for g, gw in enumerate(GROUPS):
                if g > 0:
                    tensor.wait_ge(sem("bE"), 2 * base(it) + 2 * g)
                for u in range(gw):
                    for j in range(2):
                        mm = tensor.matmul(
                            ps_s[j][:, 512 * u:512 * (u + 1)],
                            k_n[64 * j:64 * (j + 1),
                                SEQ * p + 128 * (kc + u):SEQ * p + 128 * (kc + u + 1)],
                            q_ap[64 * j:64 * (j + 1), :],
                            start=True, stop=True, tile_position=(64 * j, 0))
                mm.then_inc(sem("bS"), 1)
                if g > 0:
                    o_mms(tensor, it, p, g - 1, kc - GROUPS[g - 1], GROUPS[g - 1])
                kc += gw
            tensor.wait_ge(sem("bE"), 2 * base(it) + 2 * NG)
            o_mms(tensor, it, p, NG - 1, kc - GROUPS[-1], GROUPS[-1])
            for j in range(2):
                tensor.wait_ge(sem("bR"), 2 * it + j + 1)
                tensor.matmul(ps_s[0][:, 512 * j:512 * (j + 1)],
                              ones[64:65, :], orc[64:65, 512 * j:512 * (j + 1)],
                              start=True, stop=True,
                              tile_position=(64, 0)).then_inc(sem("bB"), 1)

    @block.scalar
    def _(scalar):
        for it in range(NIT):
            p = it // 16
            for g, gw in enumerate(GROUPS):
                G = base(it) + g
                scalar.wait_ge(sem("bS"), G + 1)
                if G >= 2:
                    scalar.wait_ge(sem("bO"), G - 1)
                for j in range(2):
                    scalar.activation(
                        pt[:, 1536 * (2 * j + g % 2):1536 * (2 * j + g % 2) + 512 * gw],
                        ps_s[j][:, 0:512 * gw], AF.Exp,
                        bias=negm[:, 2 * p + j:2 * p + j + 1],
                    ).then_inc(sem("bE"), 1)

    @block.vector
    def _(vector):
        for it in range(NIT):
            vector.wait_ge(sem("bO"), base(it) + NG)
            for j in range(2):
                vector.reciprocal(orc[64:65, 512 * j:512 * (j + 1)],
                                  po[j][64:65, 0:512]).then_inc(sem("bR"), 1)
            for j in range(2):
                vector.wait_ge(sem("bB"), 2 * it + j + 1)
                if it > 1:
                    vector.wait_ge(sem("bSt"), 32 * (it - 1))
                sl = 2 * (it % 2) + j
                vector.tensor_copy(otmp[0:64, 512 * j:512 * (j + 1)],
                                   po[j][0:64, :])
                vector.tensor_tensor(oo[0:64, 512 * sl:512 * (sl + 1)],
                                     otmp[0:64, 512 * j:512 * (j + 1)],
                                     ps_s[0][0:64, 512 * j:512 * (j + 1)],
                                     ALU.mult).then_inc(sem("bT"), 1)

    @block.gpsimd
    def _(gpsimd):
        for it in range(NIT):
            p, qt = it // 16, it % 16
            if it == 16:
                gpsimd.wait_ge(sem("bSt"), 32 * 16)
                gpsimd.collective_compute(
                    "AllGather", ALU.bypass,
                    replica_groups=[[0, 1, 2, 3], [4, 5, 6, 7]],
                    ins=[cc_in[0:128, :]], outs=[ctx["S"]["cc_out0"][:]],
                ).then_inc(sem("cAG"), 1)
                gpsimd.wait_ge(sem("cAG"), 1)
                _emit_attw_copy(ctx, gpsimd, 0)
            for j in range(2):
                gpsimd.wait_ge(sem("bT"), 2 * it + j + 1)
                hh = 2 * p + j
                sl = 2 * (it % 2) + j
                gpsimd.dma_start(
                    out=cc_in[64 * hh:64 * (hh + 1), 512 * qt:512 * (qt + 1)],
                    in_=oo[0:64, 512 * sl:512 * (sl + 1)],
                ).then_inc(sem("bSt"), 16)

    es_pb.close()



def _emit_attw_copy(ctx, gpsimd, half):
    sem, S = ctx["sem"], ctx["S"]
    att_w = S["att_w"]
    src = S[f"cc_out{half}"]
    for arm in gpsimd.switch_core_id(n=128):
        g = (arm.logical % 8) % 4
        s = 2048 * g
        lo = s - T0_L
        for gg in range(4):
            dst = att_w[256 * gg + 128 * half:256 * gg + 128 * half + 128, :]
            sg = src[128 * gg:128 * (gg + 1), :]
            if lo < 0:
                gpsimd.dma_start(out=dst[:, 0:-lo],
                                 in_=sg[:, SEQ + lo:SEQ]).then_inc(sem("cW"), 16)
                gpsimd.dma_start(out=dst[:, -lo:NT0],
                                 in_=sg[:, 0:s + 2048]).then_inc(sem("cW"), 16)
            else:
                mid = NT0 // 2
                gpsimd.dma_start(out=dst[:, 0:mid],
                                 in_=sg[:, lo:lo + mid]).then_inc(sem("cW"), 16)
                gpsimd.dma_start(out=dst[:, mid:NT0],
                                 in_=sg[:, lo + mid:s + 2048]).then_inc(sem("cW"), 16)


def emit_phase_C(ctx):
    block, sem, S = ctx["block"], ctx["sem"], ctx["S"]
    cc_in = S["cc_in"]

    @block.gpsimd
    def _(gpsimd):
        gpsimd.wait_ge(sem("bSt"), 32 * 32)
        gpsimd.collective_compute(
            "AllGather", ALU.bypass,
            replica_groups=[[0, 1, 2, 3], [4, 5, 6, 7]],
            ins=[cc_in[128:256, :]], outs=[S["cc_out1"][:]],
        ).then_inc(sem("cAG"), 1)
        gpsimd.wait_ge(sem("cAG"), 2)
        _emit_attw_copy(ctx, gpsimd, 1)



def emit_phase_D(ctx, es):
    nc, block, sem, I, S = ctx["nc"], ctx["block"], ctx["sem"], ctx["I"], ctx["S"]
    att_w, x1_scr = S["att_w"], S["x1_scr"]
    y_out, dbg = ctx["y_out"], ctx["dbg"]
    sb = lambda nm, shape: es.enter_context(nc.sbuf_tensor(nm, shape, F32))

    d_w = sb("d_w", [128, 8 * 1024])
    d_bo = sb("d_bo", [128, 8])
    d_at = sb("d_at", [128, 2 * 8 * 512])
    d_xr = sb("d_xr", [128, 2 * 8 * 512])
    d_y = sb("d_y", [128, 8 * 512])
    d_ysq = sb("d_ysq", [128, 8 * 512])
    d_t1 = sb("d_t1", [128, 8 * 512])
    d_x1 = sb("d_x1", [128, 8 * 512])
    d_row = sb("d_row", [1, 2048])
    ones = ctx["ones"]

    psd = lambda nm, shape: es.enter_context(nc.psum_tensor(nm, shape, F32))
    p_d = [psd("p_d0", [128, 512]), psd("p_d1", [128, 512])]
    p_s1 = psd("p_s1", [1, 512])
    p_s2 = psd("p_s2", [1, 512])
    p_bm = psd("p_bm", [128, 512])
    p_br = psd("p_br", [128, 512])

    NTD = 7
    d_at4 = d_at[:].rearrange("p (b c n) -> p b c n", b=2, n=512)
    d_xr4 = d_xr[:].rearrange("p (b c n) -> p b c n", b=2, n=512)
    d_y3 = d_y[:].rearrange("p (c n) -> p c n", n=512)
    d_ysq3 = d_ysq[:].rearrange("p (c n) -> p c n", n=512)
    d_t13 = d_t1[:].rearrange("p (c n) -> p c n", n=512)
    d_x13 = d_x1[:].rearrange("p (c n) -> p c n", n=512)
    d_w3 = d_w[:].rearrange("p (c n) -> p c n", n=1024)

    apc = [8 + (1 if j == 2 else 0) for j in range(NTD)]
    apcum = [sum(apc[:j + 1]) for j in range(NTD)]

    @block.sync
    def _(sync):
        sync.wait_ge(sem("bSt"), 32 * 32)
        sync.dma_start(out=d_w3, in_=I["w_out"].rearrange("(c p) n -> p c n", p=128)
                       ).then_inc(sem("dW"), 16)
        sync.dma_start(out=d_bo[:], in_=I["b_out_c"][:, :]).then_inc(sem("dW"), 16)
        sync.dma_start(out=d_bo[:, 0:8], in_=I["b_out_c"][:, :]).then_inc(sem("dW"), 16) if False else None
        sync.nop() if False else None
        for j in range(NTD):
            if j == 0:
                sync.wait_ge(sem("cW"), 256)
            if j >= 2:
                sync.wait_ge(sem("dMM"), 8 * (j - 1))
                sync.wait_ge(sem("dY"), 8 * (j - 1))
            sync.dma_start(
                out=d_at4[:, j % 2],
                in_=att_w.rearrange("(c p) n -> p c n", p=128)[:, :, 512 * j:512 * (j + 1)],
            ).then_inc(sem(f"dA{j % 2}"), 16)
            sync.dma_start(
                out=d_xr4[:, j % 2],
                in_=I["x_res"].rearrange("(c p) n -> p c n", p=128)[:, :, 512 * j:512 * (j + 1)],
            ).then_inc(sem(f"dA{j % 2}"), 16)

    @block.tensor
    def _(tensor):
        tensor.wait_ge(sem("dW"), 32)
        for j in range(NTD):
            tensor.wait_ge(sem(f"dA{j % 2}"), 32 * (j // 2 + 1))
            at = d_at4[:, j % 2]
            for dc in range(8):
                n = 8 * j + dc
                if n >= 2:
                    tensor.wait_ge(sem("dY"), n - 1)
                for ic in range(8):
                    mm = tensor.matmul(p_d[dc % 2][:],
                                       d_w3[:, ic, 128 * dc:128 * (dc + 1)],
                                       at[:, ic, :],
                                       start=(ic == 0), stop=(ic == 7))
                mm.then_inc(sem("dMM"), 1)
            tensor.wait_ge(sem("dY"), 8 * (j + 1))
            if j > 0:
                tensor.wait_ge(sem("dRX"), 4 * j)
            for dc in range(8):
                mm = tensor.matmul(p_s1[:], ones[:, 0:1], d_y3[:, dc, :],
                                   start=(dc == 0), stop=(dc == 7))
            mm.then_inc(sem("dS1"), 1)
            tensor.wait_ge(sem("dSq"), j + 1)
            for dc in range(8):
                mm = tensor.matmul(p_s2[:], ones[:, 0:1], d_ysq3[:, dc, :],
                                   start=(dc == 0), stop=(dc == 7))
            mm.then_inc(sem("dS2"), 1)
            tensor.wait_ge(sem("dRX"), 4 * j + 4)
            if j > 0:
                tensor.wait_ge(sem("dAp"), apcum[j - 1])
            tensor.matmul(p_bm[:], ones[0:1, :], d_row[0:1, 0:512],
                          start=True, stop=True)
            tensor.matmul(p_br[:], ones[0:1, :], d_row[0:1, 512:1024],
                          start=True, stop=True).then_inc(sem("dBc"), 1)

    @block.scalar
    def _(scalar):
        for j in range(NTD):
            scalar.wait_ge(sem("dY"), 8 * (j + 1))
            if j > 0:
                scalar.wait_ge(sem("dS2"), j)
            scalar.activation(d_ysq[:], d_y[:], AF.Square).then_inc(sem("dSq"), 1)
            scalar.wait_ge(sem("dRX"), 4 * j + 1)
            scalar.activation(d_row[0:1, 1024:1536], d_row[0:1, 0:512], AF.Square
                              ).then_inc(sem("dRX"), 1)
            scalar.wait_ge(sem("dRX"), 4 * j + 3)
            scalar.activation(d_row[0:1, 512:1024], d_row[0:1, 512:1024], AF.Sqrt
                              ).then_inc(sem("dRX"), 1)

    @block.vector
    def _(vector):
        for j in range(NTD):
            for dc in range(8):
                vector.wait_ge(sem("dMM"), 8 * j + dc + 1)
                if j > 0 and dc == 0:
                    vector.wait_ge(sem("dS1"), j)
                    vector.wait_ge(sem("dSq"), j)
                vector.scalar_tensor_tensor(
                    d_y3[:, dc, :], p_d[dc % 2][:], d_bo[:, dc:dc + 1],
                    d_xr4[:, j % 2, dc, :], ALU.add, ALU.add,
                ).then_inc(sem("dY"), 1)
            vector.wait_ge(sem("dS1"), j + 1)
            vector.tensor_scalar_mul(d_row[0:1, 0:512], p_s1[0:1, :], 1.0 / 1024.0
                                     ).then_inc(sem("dRX"), 1)
            vector.wait_ge(sem("dS2"), j + 1)
            vector.tensor_scalar_mul(d_row[0:1, 1536:2048], p_s2[0:1, :], 1.0 / 1024.0)
            vector.wait_ge(sem("dRX"), 4 * j + 2)
            vector.tensor_sub(d_row[0:1, 512:1024], d_row[0:1, 1536:2048],
                              d_row[0:1, 1024:1536])
            vector.tensor_scalar_add(d_row[0:1, 512:1024], d_row[0:1, 512:1024], 1e-5)
            vector.reciprocal(d_row[0:1, 512:1024], d_row[0:1, 512:1024]
                              ).then_inc(sem("dRX"), 1)
            vector.wait_ge(sem("dBc"), j + 1)
            if j > 0:
                vector.wait_ge(sem("dX1"), 16 * j)
                if dbg == "x1":
                    vector.wait_ge(sem("dYo"), 16 * j)
            for dc in range(8):
                vector.tensor_sub(d_t13[:, dc, :], d_y3[:, dc, :], p_bm[:])
                vector.tensor_tensor(d_x13[:, dc, :], d_t13[:, dc, :], p_br[:],
                                     ALU.mult).then_inc(sem("dAp"), 1)
            if j == 2:
                vector.tensor_scalar_mul(
                    d_x1[:].rearrange("p (c n) -> p c n", n=512)[:, :, 511:512],
                    d_x1[:].rearrange("p (c n) -> p c n", n=512)[:, :, 511:512],
                    ctx["fl_sb"][:, 0:1]).then_inc(sem("dAp"), 1)

    @block.gpsimd
    def _(gpsimd):
        for j in range(NTD):
            gpsimd.wait_ge(sem("dAp"), apcum[j])
            gpsimd.dma_start(
                out=x1_scr.rearrange("(c p) n -> p c n", p=128)[:, :, 512 * j:512 * (j + 1)],
                in_=d_x13[:, :, :],
            ).then_inc(sem("dX1"), 16)
            if dbg == "x1":
                gpsimd.dma_start(
                    out=y_out.rearrange("(c p) n -> p c n", p=128)[:, :, 512 * j:512 * (j + 1)],
                    in_=d_x13[:, :, :],
                ).then_inc(sem("dYo"), 16)

    if dbg == "x1":
        ctx["done"].append(("dYo", 16 * NTD))
    ctx["done"].append(("dX1", 16 * NTD))



def emit_ffn_qkv(ctx, es, l):
    nc, block, sem, I, S = ctx["nc"], ctx["block"], ctx["sem"], ctx["I"], ctx["S"]
    SW, bump = ctx["SW"], ctx["bump"]
    sb = lambda nm, shape: es.enter_context(nc.sbuf_tensor(f"L{l}{nm}", shape, F32))
    ps = lambda nm, shape: es.enter_context(nc.psum_tensor(f"L{l}{nm}", shape, F32))

    x_src = S["x1_scr"] if l == 0 else S["x2_scr"]
    X0 = (T0_L - KA_L[0]) if l == 0 else (ML_L[0] - KA_L[1])
    TK = _tiles(NKA[l])
    NT = len(TK)

    xt = sb("qXt", [128, 8 * 513])
    tbl = sb("qTbl", [128, 4 * 512])
    wqc = sb("qWqc", [128, 2 * 8 * 128])
    wv = sb("qWv", [128, 8 * 512])
    qfb = sb("qQfb", [128, 4 * 512])
    kfb = sb("qKfb", [128, 4 * 512])
    qsw = sb("qQsw", [128, 4 * 512])
    ksw = sb("qKsw", [128, 4 * 512])
    vtb = sb("qVtb", [128, 4 * 544])
    rsc = sb("qRsc", [128, 512])
    sq_s = sb("qSqs", [128, 2 * 512])
    cp_s = sb("qCps", [128, 2 * 512])
    ss_s = sb("qSss", [2, 2 * 512])
    scl = sb("qScl", [128, 2])

    p_qk = [ps("qPqk0", [128, 512]), ps("qPqk1", [128, 512])]
    p_s = ps("qPs", [2, 512])
    p_b = ps("qPb", [128, 512])

    xt3 = xt[:].rearrange("p (c n) -> p c n", n=513)
    wqc4 = wqc[:].rearrange("p (b c n) -> p b c n", b=2, n=128)
    wv3 = wv[:].rearrange("p (c n) -> p c n", n=512)
    blk2_sb, blk2T_sb = ctx["blk2_sb"], ctx["blk2T_sb"]

    if l == 0:
        gates = [("dX1", 16 * 7, False)]
    else:
        gates = [("mX2", 0, True)]

    FMN, GS, CONS = {}, {}, {}
    n = g = nv = 0
    for i, (o, w) in enumerate(TK):
        for ci in range(8):
            FMN[(i, ci)] = n
            GS[(i, ("fm", ci))] = g
            CONS[g] = ("qSQ", n)
            n += 1
            g += 1
        for u in range(w // 128):
            GS[(i, ("v", u))] = g
            CONS[g] = ("qVE", nv)
            nv += 1
            g += 1
    N_FM, N_G, N_V = n, g, nv
    FMG = [GS[(i, ("fm", ci))] for i in range(NT) for ci in range(8)]
    VTOT = [sum(TK[j][1] // 128 for j in range(i + 1)) for i in range(NT)]

    @block.sync
    def _(sync):
        for gs, gv, rel in gates:
            sync.wait_ge(sem(gs), SW(gs, gv) if rel else gv)
        sync.dma_start(out=wv3, in_=I[("wqkv", l)].rearrange("(c p) n -> p c n", p=128)
                       [:, :, 1024:1536]).then_inc(sem("qW"), 16)
        sync.dma_start(out=scl[:, 0:1], in_=I[("qsc", l)][:, :]).then_inc(sem("qW"), 16)
        sync.dma_start(out=scl[:, 1:2], in_=I[("ksc", l)][:, :]).then_inc(sem("qW"), 16)
        wq = 0
        for i, (o, w) in enumerate(TK):
            if i >= 1:
                sync.wait_ge(sem("qMM"), SW("qMM", GS[(i - 1, ("v", TK[i - 1][1] // 128 - 1))] + 1))
            sync.dma_start(
                out=xt3[:, :, 0:w + 1],
                in_=x_src.rearrange("(c p) n -> p c n", p=128)[:, :, X0 + o - 1:X0 + o + w],
            ).then_inc(sem("qX"), 16)
            if i >= 1:
                sync.wait_ge(sem("qRotD"), SW("qRotD", 8 * i))
            for ti, nm in enumerate(["rAq", "rBq", "rAk", "rBk"]):
                sync.dma_start(out=tbl[:, 512 * ti:512 * ti + w],
                               in_=I[(nm, l)][:, o:o + w]).then_inc(sem("qT"), 16)
            for ci in range(8):
                if wq >= 2:
                    sync.wait_ge(sem("qMM"), SW("qMM", FMG[wq - 2] + 1))
                sync.dma_start(
                    out=wqc4[:, wq % 2, :, :],
                    in_=I[("wqkv", l)].rearrange("(c p) n -> p c n", p=128)
                    [:, :, 128 * ci:128 * (ci + 1)],
                ).then_inc(sem(f"qWQ{wq % 2}"), 16)
                wq += 1

    @block.gpsimd
    def _(gpsimd):
        for gs, gv, rel in gates:
            gpsimd.wait_ge(sem(gs), SW(gs, gv) if rel else gv)
        gpsimd.memset(vtb[:], 1.0)
        for i, (o, w) in enumerate(TK):
            nsub = w // 128
            gpsimd.wait_ge(sem("qFIN"), SW("qFIN", 8 * i + 4))
            for blk in range(4):
                so = 32 * ((blk + 1) if blk % 2 == 0 else (blk - 1))
                gpsimd.dma_start(out=qsw[32 * blk:32 * (blk + 1), :],
                                 in_=qfb[so:so + 32, :]).then_inc(sem("qSWQ"), 16)
            gpsimd.wait_ge(sem("qFIN"), SW("qFIN", 8 * i + 8))
            for blk in range(4):
                so = 32 * ((blk + 1) if blk % 2 == 0 else (blk - 1))
                gpsimd.dma_start(out=ksw[32 * blk:32 * (blk + 1), :],
                                 in_=kfb[so:so + 32, :]).then_inc(sem("qSWK"), 16)
            gpsimd.wait_ge(sem("qRotD"), SW("qRotD", 8 * i + 4))
            gpsimd.dma_start(
                out=S["qr_scr"].rearrange("(c p) n -> p c n", p=128)[:, :, o:o + w],
                in_=qfb[:].rearrange("p (c n) -> p c n", n=512)[:, :, 0:w],
            ).then_inc(sem("qQS"), 16)
            gpsimd.wait_ge(sem("qRotD"), SW("qRotD", 8 * i + 8))
            gpsimd.dma_start(
                out=S["kr_scr"].rearrange("(c p) n -> p c n", p=128)[:, :, o:o + w],
                in_=kfb[:].rearrange("p (c n) -> p c n", n=512)[:, :, 0:w],
            ).then_inc(sem("qKS"), 16)
            gpsimd.wait_ge(sem("qVE"), SW("qVE", VTOT[i]))
            gpsimd.dma_start(
                out=S["vt_scr"][:, 544 * (o // 128):544 * (o // 128 + nsub)],
                in_=vtb[:, 0:544 * nsub],
            ).then_inc(sem("qVS"), 16)

    @block.tensor
    def _(tensor):
        tensor.wait_ge(sem("qW"), SW("qW", 48))
        wq = 0
        for i, (o, w) in enumerate(TK):
            nsub = w // 128
            tensor.wait_ge(sem("qX"), SW("qX", 16 * (i + 1)))

            def t_blk(ci):
                nn = FMN[(i, ci)]
                tensor.wait_ge(sem("qSQ"), SW("qSQ", nn + 1))
                if nn > 0:
                    tensor.wait_ge(sem("qMX"), SW("qMX", nn))
                tensor.matmul(p_s[:, 0:w], blk2_sb[:, 0:2],
                              sq_s[:, 512 * (ci % 2):512 * (ci % 2) + w],
                              start=True, stop=True).then_inc(sem("qBLK"), 1)

            def t_bc(ci):
                nn = FMN[(i, ci)]
                tensor.wait_ge(sem("qRS"), SW("qRS", nn + 1))
                if nn > 0:
                    tensor.wait_ge(sem("qFIN"), SW("qFIN", nn))
                tensor.matmul(p_b[:, 0:w], blk2T_sb[0:2, :],
                              ss_s[:, 512 * (ci % 2):512 * (ci % 2) + w],
                              start=True, stop=True).then_inc(sem("qBC"), 1)

            for ci in range(8):
                g = GS[(i, ("fm", ci))]
                tensor.wait_ge(sem(f"qWQ{wq % 2}"), SW(f"qWQ{wq % 2}", 16 * (wq // 2 + 1)))
                if g >= 2:
                    kind, idx = CONS[g - 2]
                    tensor.wait_ge(sem(kind), SW(kind, idx + 1))
                for dc in range(8):
                    rhs = xt3[:, dc, 1:1 + w] if dc < 4 else xt3[:, dc, 0:w]
                    mm = tensor.matmul(p_qk[g % 2][:, 0:w], wqc4[:, wq % 2, dc, :],
                                       rhs, start=(dc == 0), stop=(dc == 7))
                mm.then_inc(sem("qMM"), 1)
                wq += 1
                t_blk(ci)
                if ci >= 1:
                    t_bc(ci - 1)
            t_bc(7)
            for u in range(nsub):
                g = GS[(i, ("v", u))]
                if g >= 2:
                    kind, idx = CONS[g - 2]
                    tensor.wait_ge(sem(kind), SW(kind, idx + 1))
                for dc in range(8):
                    lhs = xt3[:, dc, 1 + 128 * u:1 + 128 * (u + 1)] if dc < 4 \
                        else xt3[:, dc, 128 * u:128 * (u + 1)]
                    mm = tensor.matmul(p_qk[g % 2][:], lhs, wv3[:, dc, :],
                                       start=(dc == 0), stop=(dc == 7))
                mm.then_inc(sem("qMM"), 1)

    @block.scalar
    def _(scalar):
        for i, (o, w) in enumerate(TK):
            def s_sqrt(ci):
                nn = FMN[(i, ci)]
                scalar.wait_ge(sem("qMX"), SW("qMX", nn + 1))
                sl = ss_s[:, 512 * (ci % 2):512 * (ci % 2) + w]
                scalar.activation(sl, sl, AF.Sqrt).then_inc(sem("qRS"), 1)

            for ci in range(8):
                nn = FMN[(i, ci)]
                g = GS[(i, ("fm", ci))]
                scalar.wait_ge(sem("qMM"), SW("qMM", g + 1))
                if nn >= 2:
                    scalar.wait_ge(sem("qBLK"), SW("qBLK", nn - 1))
                    scalar.wait_ge(sem("qFIN"), SW("qFIN", nn - 1))
                sl2 = 512 * (ci % 2)
                scalar.activation(sq_s[:, sl2:sl2 + w], p_qk[g % 2][:, 0:w], AF.Square)
                scalar.copy(cp_s[:, sl2:sl2 + w], p_qk[g % 2][:, 0:w]
                            ).then_inc(sem("qSQ"), 1)
                if ci >= 1:
                    s_sqrt(ci - 1)
            s_sqrt(7)

    @block.vector
    def _(vector):
        for i, (o, w) in enumerate(TK):
            nsub = w // 128

            def v_mx(ci):
                nn = FMN[(i, ci)]
                vector.wait_ge(sem("qBLK"), SW("qBLK", nn + 1))
                if nn >= 2:
                    vector.wait_ge(sem("qBC"), SW("qBC", nn - 1))
                sl = ss_s[:, 512 * (ci % 2):512 * (ci % 2) + w]
                vector.tensor_scalar_max(sl, p_s[:, 0:w], 1e-24)
                if ci < 4:
                    vector.tensor_scalar_mul(sl, sl, 1.0 / 64.0)
                vector.reciprocal(sl, sl).then_inc(sem("qMX"), 1)

            def v_fin(ci):
                nn = FMN[(i, ci)]
                vector.wait_ge(sem("qBC"), SW("qBC", nn + 1))
                srcap = cp_s[:, 512 * (ci % 2):512 * (ci % 2) + w]
                if ci < 4:
                    if i > 0:
                        vector.wait_ge(sem("qQS"), SW("qQS", 16 * i))
                    dst = qfb[:, 512 * ci:512 * ci + w]
                    sc = scl[:, 0:1]
                else:
                    if i > 0:
                        vector.wait_ge(sem("qKS"), SW("qKS", 16 * i))
                    dst = kfb[:, 512 * (ci - 4):512 * (ci - 4) + w]
                    sc = scl[:, 1:2]
                vector.scalar_tensor_tensor(dst, srcap, sc, p_b[:, 0:w],
                                            ALU.mult, ALU.mult
                                            ).then_inc(sem("qFIN"), 1)

            for ci in range(8):
                v_mx(ci)
                if ci >= 1:
                    v_fin(ci - 1)
            v_fin(7)
            for u in range(nsub):
                g = GS[(i, ("v", u))]
                vector.wait_ge(sem("qMM"), SW("qMM", g + 1))
                if i > 0 and u == 0:
                    vector.wait_ge(sem("qVS"), SW("qVS", 16 * i))
                vector.tensor_copy(
                    vtb[:, 544 * u:544 * (u + 1)].rearrange(
                        "p (h e) -> p h e", e=68)[:, :, 0:64],
                    p_qk[g % 2][:].rearrange("p (h e) -> p h e", e=64),
                ).then_inc(sem("qVE"), 1)
            vector.wait_ge(sem("qSWQ"), SW("qSWQ", 64 * (i + 1)))
            vector.wait_ge(sem("qT"), SW("qT", 64 * (i + 1)))
            for ci in range(4):
                q_ap = qfb[:, 512 * ci:512 * ci + w]
                vector.tensor_mul(rsc[:, 0:w], q_ap, tbl[:, 0:w])
                vector.tensor_mul(q_ap, qsw[:, 512 * ci:512 * ci + w],
                                  tbl[:, 512:512 + w])
                vector.tensor_add(q_ap, q_ap, rsc[:, 0:w]).then_inc(sem("qRotD"), 1)
            vector.wait_ge(sem("qSWK"), SW("qSWK", 64 * (i + 1)))
            for ci in range(4):
                k_ap = kfb[:, 512 * ci:512 * ci + w]
                vector.tensor_mul(rsc[:, 0:w], k_ap, tbl[:, 1024:1024 + w])
                vector.tensor_mul(k_ap, ksw[:, 512 * ci:512 * ci + w],
                                  tbl[:, 1536:1536 + w])
                vector.tensor_add(k_ap, k_ap, rsc[:, 0:w]).then_inc(sem("qRotD"), 1)

    bump("qW", 48)
    bump("qX", 16 * NT)
    bump("qT", 64 * NT)
    bump("qWQ0", 16 * ((8 * NT + 1) // 2))
    bump("qWQ1", 16 * (8 * NT // 2))
    bump("qMM", N_G)
    bump("qSQ", N_FM)
    bump("qBLK", N_FM)
    bump("qMX", N_FM)
    bump("qRS", N_FM)
    bump("qBC", N_FM)
    bump("qFIN", N_FM)
    bump("qVE", N_V)
    bump("qSWQ", 64 * NT)
    bump("qSWK", 64 * NT)
    bump("qRotD", 8 * NT)
    bump("qQS", 16 * NT)
    bump("qKS", 16 * NT)
    bump("qVS", 16 * NT)
    ctx["q_tot"] = dict(NT=NT)


def emit_ffn_attn(ctx, es, l):
    nc, block, sem, I, S = ctx["nc"], ctx["block"], ctx["sem"], ctx["I"], ctx["S"]
    SW, bump = ctx["SW"], ctx["bump"]
    ones = ctx["ones"]
    sb = lambda nm, shape: es.enter_context(nc.sbuf_tensor(f"L{l}{nm}", shape, F32))
    ps = lambda nm, shape: es.enter_context(nc.psum_tensor(f"L{l}{nm}", shape, F32))

    x_src = S["x1_scr"] if l == 0 else S["x2_scr"]
    X0 = (T0_L - KA_L[0]) if l == 0 else (ML_L[0] - KA_L[1])
    xmid = S["xmid_scr"]
    QT = _tiles(NQA[l])
    NQT = len(QT)
    NT_Q = len(_tiles(NKA[l]))

    q_sb = sb("aQ", [128, 4 * 512])
    k_sb = sb("aK", [128, 4 * 1024])
    vt_sb = sb("aVt", [128, 8 * 544])
    msk = sb("aMsk", [128, 8 * 512])
    xt = sb("aXt", [128, 8 * 512])
    wos = sb("aWos", [128, 4 * 1024])
    ptE = sb("aPtE", [128, 4 * 1024])
    of = sb("aOf", [128, 4 * 512])
    otm = sb("aOtm", [128, 2 * 512])
    orc = sb("aOrc", [128, 2 * 512])
    xm = sb("aXm", [128, 8 * 512])
    ngM = sb("aNgM", [128, 1])
    row = sb("aRow", [1, 256])

    p_se = [ps("aPse0", [128, 1024]), ps("aPse1", [128, 1024])]
    p_oe = [ps("aPoe0", [128, 512]), ps("aPoe1", [128, 512])]
    p_wo = [ps("aPwo0", [128, 512]), ps("aPwo1", [128, 512])]

    k_sb3 = k_sb[:].rearrange("p (c n) -> p c n", n=1024)
    xt3 = xt[:].rearrange("p (c n) -> p c n", n=512)
    wos3 = wos[:].rearrange("p (c n) -> p c n", n=1024)
    xm3 = xm[:].rearrange("p (c n) -> p c n", n=512)

    moff, mc = [], 0
    for (qo, qw) in QT:
        moff.append(mc)
        mc += ((qw + 512) // 128) * qw

    FL_COL = QA_L[l] - 1
    FL_QT = next(qi for qi, (qo, qw) in enumerate(QT) if qo <= FL_COL < qo + qw)
    FL_LOC = FL_COL - QT[FL_QT][0]

    STM, OMC, RCC, AS, WIDE = {}, {}, {}, {}, []
    stm = om = rc = 0
    for qi, (qo, qw) in enumerate(QT):
        asub = (qw + 512) // 128
        awp = (asub + 1) // 2
        AS[qi] = (asub, awp)
        for p in range(4):
            for wp in range(awp):
                STM[(qi, p, wp)] = stm
                WIDE.append((qi, p, wp))
                stm += 1
            for u in range(asub):
                OMC[(qi, p, u)] = om
                om += 1
            RCC[(qi, p)] = rc
            rc += 1
    N_STM, N_OM, N_RC = stm, om, rc

    def om_of_wide(sgl):
        qi, p, wp = WIDE[sgl]
        asub, _ = AS[qi]
        last_u = min(2, asub - 2 * wp) + 2 * wp - 1
        return OMC[(qi, p, last_u)] + 1

    @block.sync
    def _(sync):
        sync.wait_ge(sem("qQS"), SW("qQS", 0))
        sync.wait_ge(sem("qKS"), SW("qKS", 0))
        sync.wait_ge(sem("qVS"), SW("qVS", 0))
        sync.dma_start(out=wos3, in_=I[("wo", l)].rearrange("(c p) n -> p c n", p=128)
                       ).then_inc(sem("aW"), 16)
        sync.dma_start(out=row[0:1, 0:64], in_=I[("qsr", l)][0:1, :]).then_inc(sem("aW"), 16)
        sync.dma_start(out=row[0:1, 64:128], in_=I[("ksr", l)][0:1, :]).then_inc(sem("aW"), 16)
        for qi, (qo, qw) in enumerate(QT):
            asub, awp = AS[qi]
            if qi >= 1:
                sync.wait_ge(sem("aSTM"), SW("aSTM", STM[(qi - 1, 3, AS[qi - 1][1] - 1)] + 1))
            sync.dma_start(
                out=q_sb[:].rearrange("p (c n) -> p c n", n=512)[:, :, 0:qw],
                in_=S["qr_scr"].rearrange("(c p) n -> p c n", p=128)
                [:, :, qo + 512:qo + 512 + qw],
            ).then_inc(sem("aQL"), 16)
            sync.dma_start(
                out=k_sb3[:, :, 0:qw + 512],
                in_=S["kr_scr"].rearrange("(c p) n -> p c n", p=128)
                [:, :, qo:qo + qw + 512],
            ).then_inc(sem("aQL"), 16)
            if qi >= 1:
                sync.wait_ge(sem("aOM"), SW("aOM", OMC[(qi - 1, 3, AS[qi - 1][0] - 1)] + 1))
            sync.dma_start(
                out=vt_sb[:, 0:544 * asub],
                in_=S["vt_scr"][:, 544 * (qo // 128):544 * (qo // 128 + asub)],
            ).then_inc(sem("aVL"), 16)
            if qi >= 1:
                sync.wait_ge(sem("aMS"), SW("aMS", STM[(qi - 1, 3, AS[qi - 1][1] - 1)] + 1))
            sync.dma_start(
                out=msk[:, 0:asub * qw],
                in_=I[("mask", l)][:, moff[qi]:moff[qi] + asub * qw],
            ).then_inc(sem("aML"), 16)
            if qi >= 1:
                sync.wait_ge(sem("aXM"), SW("aXM", 8 * qi))
            sync.dma_start(
                out=xt3[:, :, 0:qw],
                in_=x_src.rearrange("(c p) n -> p c n", p=128)
                [:, :, X0 + qo + 512:X0 + qo + 512 + qw],
            ).then_inc(sem("aXL"), 16)

    @block.tensor
    def _(tensor):
        tensor.wait_ge(sem("aNG"), SW("aNG", 1))
        tensor.matmul(p_wo[0][:, 0:1], ones[0:1, :], row[0:1, 72:73],
                      start=True, stop=True).then_inc(sem("aNG"), 1)
        for qi, (qo, qw) in enumerate(QT):
            asub, awp = AS[qi]
            tensor.wait_ge(sem("aQL"), SW("aQL", 32 * (qi + 1)))
            tensor.wait_ge(sem("aVL"), SW("aVL", 16 * (qi + 1)))
            for p in range(4):
                pidx = RCC[(qi, p)]
                for wp in range(awp):
                    nsu = min(2, asub - 2 * wp)
                    sgl = STM[(qi, p, wp)]
                    if sgl >= 1:
                        tensor.wait_ge(sem("aEX"), SW("aEX", 2 * sgl))
                    if wp == 0 and pidx >= 1:
                        tensor.wait_ge(sem("aOF"), SW("aOF", 2 * pidx))
                    for u2 in range(nsu):
                        u = 2 * wp + u2
                        for j in range(2):
                            mm = tensor.matmul(
                                p_se[j][:, 512 * u2:512 * u2 + qw],
                                k_sb3[64 * j:64 * (j + 1), p, 128 * u:128 * (u + 1)],
                                q_sb[64 * j:64 * (j + 1), 512 * p:512 * p + qw],
                                start=True, stop=True, tile_position=(64 * j, 0))
                    mm.then_inc(sem("aSTM"), 1)
                    tensor.wait_ge(sem("aEX"), SW("aEX", 2 * (sgl + 1)))
                    for u2 in range(nsu):
                        u = 2 * wp + u2
                        par = sgl % 2
                        h0 = 2 * p
                        h1 = 2 * p + 1
                        mm = tensor.matmul(
                            p_oe[0][0:65, 0:qw],
                            vt_sb[:, 544 * u + 68 * h0:544 * u + 68 * h0 + 65],
                            ptE[:, 1024 * (0 + 2 * par) + 512 * u2:
                                1024 * (0 + 2 * par) + 512 * u2 + qw],
                            start=(u == 0), stop=(u == asub - 1))
                        mm2 = tensor.matmul(
                            p_oe[1][64:128, 0:qw],
                            vt_sb[:, 544 * u + 68 * h1:544 * u + 68 * h1 + 64],
                            ptE[:, 1024 * (1 + 2 * par) + 512 * u2:
                                1024 * (1 + 2 * par) + 512 * u2 + qw],
                            start=(u == 0), stop=(u == asub - 1),
                            tile_position=(0, 64))
                        mm3 = tensor.matmul(
                            p_oe[1][0:1, 0:qw],
                            vt_sb[:, 544 * u + 68 * h1 + 64:544 * u + 68 * h1 + 65],
                            ptE[:, 1024 * (1 + 2 * par) + 512 * u2:
                                1024 * (1 + 2 * par) + 512 * u2 + qw],
                            start=(u == 0), stop=(u == asub - 1),
                            tile_position=(0, 0))
                        mm3.then_inc(sem("aOM"), 1)
                tensor.wait_ge(sem("aRC"), SW("aRC", 2 * pidx + 2))
                tensor.matmul(p_se[0][:, 0:qw], ones[64:65, :],
                              orc[64:65, 0:qw], start=True, stop=True,
                              tile_position=(64, 0)).then_inc(sem("aB2"), 1)
                tensor.matmul(p_se[0][:, 512:512 + qw], ones[0:1, :],
                              orc[0:1, 512:512 + qw], start=True, stop=True
                              ).then_inc(sem("aB2"), 1)
            tensor.wait_ge(sem("aOF"), SW("aOF", 8 * (qi + 1)))
            tensor.wait_ge(sem("aXL"), SW("aXL", 16 * (qi + 1)))
            for dc in range(8):
                nwo = 8 * qi + dc
                if nwo >= 2:
                    tensor.wait_ge(sem("aXM"), SW("aXM", nwo - 1))
                for ic in range(4):
                    mm = tensor.matmul(p_wo[dc % 2][:, 0:qw],
                                       wos3[:, ic, 128 * dc:128 * (dc + 1)],
                                       of[:, 512 * ic:512 * ic + qw],
                                       start=(ic == 0), stop=(ic == 3))
                mm.then_inc(sem("aWOM"), 1)

    @block.scalar
    def _(scalar):
        for qi, (qo, qw) in enumerate(QT):
            asub, awp = AS[qi]
            for p in range(4):
                for wp in range(awp):
                    nsu = min(2, asub - 2 * wp)
                    sgl = STM[(qi, p, wp)]
                    par = sgl % 2
                    scalar.wait_ge(sem("aMS"), SW("aMS", sgl + 1))
                    if sgl >= 2:
                        scalar.wait_ge(sem("aOM"), SW("aOM", om_of_wide(sgl - 2)))
                    for j in range(2):
                        scalar.activation(
                            ptE[:, 1024 * (j + 2 * par):
                                1024 * (j + 2 * par) + 512 * nsu].rearrange(
                                "p (u e) -> p u e", e=512)[:, :, 0:qw],
                            p_se[j][:, 0:512 * nsu].rearrange(
                                "p (u e) -> p u e", e=512)[:, :, 0:qw],
                            AF.Exp, bias=ngM[:, 0:1],
                        ).then_inc(sem("aEX"), 1)

    @block.vector
    def _(vector):
        vector.wait_ge(sem("aW"), SW("aW", 48))
        vector.reduce_max(row[0:1, 128:129], row[0:1, 0:64], AX.X,
                          apply_absolute_value=True)
        vector.reduce_max(row[0:1, 129:130], row[0:1, 64:128], AX.X,
                          apply_absolute_value=True)
        vector.tensor_mul(row[0:1, 130:131], row[0:1, 128:129], row[0:1, 129:130])
        vector.tensor_scalar(row[0:1, 72:73], row[0:1, 130:131], -8.0, -1.0,
                             ALU.mult, ALU.add).then_inc(sem("aNG"), 1)
        vector.wait_ge(sem("aNG"), SW("aNG", 2))
        vector.tensor_copy(ngM[:, 0:1], p_wo[0][:, 0:1]).then_inc(sem("aNG"), 1)
        for qi, (qo, qw) in enumerate(QT):
            asub, awp = AS[qi]
            vector.wait_ge(sem("aML"), SW("aML", 16 * (qi + 1)))
            for p in range(4):
                pidx = RCC[(qi, p)]
                for wp in range(awp):
                    nsu = min(2, asub - 2 * wp)
                    sgl = STM[(qi, p, wp)]
                    vector.wait_ge(sem("aSTM"), SW("aSTM", sgl + 1))
                    mrow = (2 * wp) * qw
                    for j in range(2):
                        ins = vector.scalar_tensor_tensor(
                            p_se[j][:, 0:512 * nsu].rearrange(
                                "p (u e) -> p u e", e=512)[:, :, 0:qw],
                            p_se[j][:, 0:512 * nsu].rearrange(
                                "p (u e) -> p u e", e=512)[:, :, 0:qw],
                            0.0,
                            msk[:, mrow:mrow + nsu * qw].rearrange(
                                "p (u e) -> p u e", e=qw),
                            ALU.add, ALU.add)
                    ins.then_inc(sem("aMS"), 1)
                vector.wait_ge(sem("aOM"), SW("aOM", OMC[(qi, p, asub - 1)] + 1))
                vector.reciprocal(orc[64:65, 0:qw],
                                  p_oe[0][64:65, 0:qw]).then_inc(sem("aRC"), 1)
                vector.reciprocal(orc[0:1, 512:512 + qw],
                                  p_oe[1][0:1, 0:qw]).then_inc(sem("aRC"), 1)
                vector.wait_ge(sem("aB2"), SW("aB2", 2 * pidx + 2))
                vector.tensor_copy(otm[0:64, 0:qw], p_oe[0][0:64, 0:qw])
                vector.tensor_tensor(of[0:64, 512 * p:512 * p + qw],
                                     otm[0:64, 0:qw],
                                     p_se[0][0:64, 0:qw],
                                     ALU.mult).then_inc(sem("aOF"), 1)
                vector.tensor_copy(otm[64:128, 512:512 + qw], p_oe[1][64:128, 0:qw])
                vector.tensor_tensor(of[64:128, 512 * p:512 * p + qw],
                                     otm[64:128, 512:512 + qw],
                                     p_se[0][64:128, 512:512 + qw],
                                     ALU.mult).then_inc(sem("aOF"), 1)
            for dc in range(8):
                vector.wait_ge(sem("aWOM"), SW("aWOM", 8 * qi + dc + 1))
                ins = vector.scalar_tensor_tensor(
                    xm3[:, dc, 0:qw], p_wo[dc % 2][:, 0:qw], 0.0,
                    xt3[:, dc, 0:qw], ALU.add, ALU.add)
                ins.then_inc(sem("aXM"), 1)
            if qi == FL_QT:
                vector.tensor_scalar_mul(
                    xm3[:, :, FL_LOC:FL_LOC + 1], xm3[:, :, FL_LOC:FL_LOC + 1],
                    ctx["fl_sb"][:, 0:1]).then_inc(sem("aFLG"), 1)

    @block.gpsimd
    def _(gpsimd):
        for qi, (qo, qw) in enumerate(QT):
            gpsimd.wait_ge(sem("aXM"), SW("aXM", 8 * (qi + 1)))
            if qi == FL_QT:
                gpsimd.wait_ge(sem("aFLG"), SW("aFLG", 1))
            gpsimd.dma_start(
                out=xmid.rearrange("(c p) n -> p c n", p=128)[:, :, qo:qo + qw],
                in_=xm3[:, :, 0:qw],
            ).then_inc(sem("aXMS"), 16)

    bump("aW", 48)
    bump("aNG", 3)
    bump("aQL", 32 * NQT)
    bump("aVL", 16 * NQT)
    bump("aML", 16 * NQT)
    bump("aXL", 16 * NQT)
    bump("aSTM", N_STM)
    bump("aEX", 2 * N_STM)
    bump("aMS", N_STM)
    bump("aOM", N_OM)
    bump("aRC", 2 * N_RC)
    bump("aB2", 2 * N_RC)
    bump("aOF", 2 * N_RC)
    bump("aWOM", 8 * NQT)
    bump("aXM", 8 * NQT)
    bump("aFLG", 1)
    bump("aXMS", 16 * NQT)
    ctx["a_stores"] = 16 * NQT


def emit_ffn_mlp(ctx, es, l):
    nc, block, sem, I, S = ctx["nc"], ctx["block"], ctx["sem"], ctx["I"], ctx["S"]
    SW, bump = ctx["SW"], ctx["bump"]
    ones = ctx["ones"]
    y_out = ctx["y_out"]
    sb = lambda nm, shape: es.enter_context(nc.sbuf_tensor(f"L{l}{nm}", shape, F32))
    sbb = lambda nm, shape: es.enter_context(nc.sbuf_tensor(f"L{l}{nm}", shape, BF16))
    ps = lambda nm, shape: es.enter_context(nc.psum_tensor(f"L{l}{nm}", shape, F32))

    xmid = S["xmid_scr"]
    TM = _tiles(NML[l])
    NTM = len(TM)
    MO = QA_L[l] - ML_L[l]

    xmt = sb("mXmt", [128, 8 * 513])
    w1c = sb("mW1c", [128, 2 * 8 * 256])
    w1h = sbb("mW1h", [128, 2 * 8 * 256])
    w1l = sbb("mW1l", [128, 2 * 8 * 256])
    w2c = sb("mW2c", [128, 2 * 22 * 128])
    w2h = sbb("mW2h", [128, 2 * 22 * 128])
    w2l = sbb("mW2l", [128, 2 * 22 * 128])
    hsq = sb("mHsq", [128, 8 * 512])
    hnt = sb("mHnt", [128, 2 * 512])
    hnh = sbb("mHnh", [128, 8 * 512])
    hnl = sbb("mHnl", [128, 8 * 512])
    hidt = sb("mHidt", [128, 2 * 512])
    hidh = sbb("mHidh", [128, 22 * 512])
    hidl = sbb("mHidl", [128, 22 * 512])
    x2o = sb("mX2o", [128, 8 * 512])
    x1s = sb("mX1s", [128, 2 * 512]) if l == 1 else None
    rowm = sb("mRow", [1, 1536])
    cns = sb("mCns", [128, 2 * NHP + 24])

    p_m1 = ps("mPm1", [1, 512])
    p_m2 = ps("mPm2", [1, 512])
    p_bm = ps("mPbm", [128, 512])
    p_br = ps("mPbr", [128, 512])
    p_a = ps("mPa", [128, 512])
    p_g = ps("mPg", [128, 512])
    p_w = [ps("mPw0", [128, 512]), ps("mPw1", [128, 512])]

    xmt3 = xmt[:].rearrange("p (c n) -> p c n", n=513)
    w1c4 = w1c[:].rearrange("p (b c n) -> p b c n", b=2, n=256)
    w1h4 = w1h[:].rearrange("p (b c n) -> p b c n", b=2, n=256)
    w1l4 = w1l[:].rearrange("p (b c n) -> p b c n", b=2, n=256)
    w2c4 = w2c[:].rearrange("p (b c n) -> p b c n", b=2, n=128)
    w2h4 = w2h[:].rearrange("p (b c n) -> p b c n", b=2, n=128)
    w2l4 = w2l[:].rearrange("p (b c n) -> p b c n", b=2, n=128)
    hsq3 = hsq[:].rearrange("p (c n) -> p c n", n=512)
    hnt2 = hnt[:].rearrange("p (b n) -> p b n", b=2)
    hnh3 = hnh[:].rearrange("p (c n) -> p c n", n=512)
    hnl3 = hnl[:].rearrange("p (c n) -> p c n", n=512)
    hidt2 = hidt[:].rearrange("p (b n) -> p b n", b=2)
    hidh3 = hidh[:].rearrange("p (c n) -> p c n", n=512)
    hidl3 = hidl[:].rearrange("p (c n) -> p c n", n=512)
    x2o3 = x2o[:].rearrange("p (c n) -> p c n", n=512)
    x1s2 = x1s[:].rearrange("p (b n) -> p b n", b=2) if l == 1 else None
    B1C = 0
    B2C = 2 * NHP
    LGC = 2 * NHP + 8
    LBC = 2 * NHP + 16

    FLM_COL = ML_L[l] - 1
    if l == 0:
        FLM_MT = FLM_COL // 512
        FLM_LOC = FLM_COL - 512 * FLM_MT
    dst = S["x2_scr"] if l == 0 else None

    def hm_ap(dc, W):
        return xmt3[:, dc, 1:1 + W] if dc < 4 else xmt3[:, dc, 0:W]

    @block.sync
    def _(sync):
        sync.wait_ge(sem("aXMS"), SW("aXMS", 0))
        sync.dma_start(out=cns[:, B1C:B1C + 2 * NHP], in_=I[("b1c", l)][:, :]
                       ).then_inc(sem("mW"), 16)
        sync.dma_start(out=cns[:, B2C:B2C + 8], in_=I[("b2c", l)][:, :]
                       ).then_inc(sem("mW"), 16)
        sync.dma_start(out=cns[:, LGC:LGC + 8], in_=I[("lgc", l)][:, :]
                       ).then_inc(sem("mW"), 16)
        sync.dma_start(out=cns[:, LBC:LBC + 8], in_=I[("lbc", l)][:, :]
                       ).then_inc(sem("mW"), 16)
        wq = 0
        w2q = 0
        for mi, (mo, W) in enumerate(TM):
            if mi >= 1:
                sync.wait_ge(sem("mX2E"), SW("mX2E", 8 * mi))
            sync.dma_start(
                out=xmt3[:, :, 0:W + 1],
                in_=xmid.rearrange("(c p) n -> p c n", p=128)
                [:, :, MO + mo - 1:MO + mo + W],
            ).then_inc(sem("mXL"), 16)
            for hp in range(NHP):
                if wq >= 2:
                    sync.wait_ge(sem("mW1M"), SW("mW1M", wq - 1))
                sync.dma_start(
                    out=w1c4[:, wq % 2, :, :],
                    in_=I[("w1r", l)].rearrange("(c p) n -> p c n", p=128)
                    [:, :, 256 * hp:256 * (hp + 1)],
                ).then_inc(sem(f"mWQ{wq % 2}"), 16)
                wq += 1
            for dc in range(8):
                if w2q >= 2:
                    sync.wait_ge(sem("mW2M"), SW("mW2M", w2q - 1))
                sync.dma_start(
                    out=w2c4[:, w2q % 2, :, :],
                    in_=I[("w2p", l)].rearrange("(c p) n -> p c n", p=128)
                    [:, :, 128 * dc:128 * (dc + 1)],
                ).then_inc(sem(f"mW2Q{w2q % 2}"), 16)
                w2q += 1

    @block.tensor
    def _(tensor):
        tensor.wait_ge(sem("mW"), SW("mW", 64))
        wq = 0
        w2q = 0
        for mi, (mo, W) in enumerate(TM):
            nrow = 4 if l == 0 else 8
            tensor.wait_ge(sem("mXL"), SW("mXL", 16 * (mi + 1)))
            if mi > 0:
                tensor.wait_ge(sem("mRow"), SW("mRow", nrow * mi))
            for dc in range(8):
                mm = tensor.matmul(p_m1[:, 0:W], ones[:, 0:1], hm_ap(dc, W),
                                   start=(dc == 0), stop=(dc == 7))
            mm.then_inc(sem("mS1"), 1)
            tensor.wait_ge(sem("mSq"), SW("mSq", (2 if l == 1 else 1) * mi + 1))
            for dc in range(8):
                mm = tensor.matmul(p_m2[:, 0:W], ones[:, 0:1], hsq3[:, dc, 0:W],
                                   start=(dc == 0), stop=(dc == 7))
            mm.then_inc(sem("mS2"), 1)
            tensor.wait_ge(sem("mRow"), SW("mRow", nrow * mi + 4))
            if mi > 0:
                tensor.wait_ge(sem("mAP"), SW("mAP", 24 * mi))
            tensor.matmul(p_bm[:, 0:W], ones[0:1, :], rowm[0:1, 0:W],
                          start=True, stop=True)
            tensor.matmul(p_br[:, 0:W], ones[0:1, :], rowm[0:1, 512:512 + W],
                          start=True, stop=True).then_inc(sem("mBC"), 1)
            for hp in range(NHP):
                hpg = NHP * mi + hp
                tensor.wait_ge(sem("mSH"), SW("mSH", 60 * mi + 9 + 2 * hp))
                if hp == 0:
                    tensor.wait_ge(sem("mSP"), SW("mSP", 60 * mi + 8))
                if hpg > 0:
                    tensor.wait_ge(sem("mHP"), SW("mHP", hpg))
                    tensor.wait_ge(sem("mGE"), SW("mGE", hpg))
                W1P = [(w1h4, hnh3), (w1h4, hnl3), (w1l4, hnh3)]
                for pi, (wv, xv) in enumerate(W1P):
                    if pi == 2:
                        tensor.wait_ge(sem("mSP"), SW("mSP", 60 * mi + 9 + 2 * hp))
                    for dc in range(8):
                        mm = tensor.matmul(p_a[:, 0:W], wv[:, wq % 2, dc, 0:128],
                                           xv[:, dc, 0:W],
                                           start=(pi == 0 and dc == 0),
                                           stop=(pi == 2 and dc == 7))
                for pi, (wv, xv) in enumerate(W1P):
                    for dc in range(8):
                        mm2 = tensor.matmul(p_g[:, 0:W], wv[:, wq % 2, dc, 128:256],
                                            xv[:, dc, 0:W],
                                            start=(pi == 0 and dc == 0),
                                            stop=(pi == 2 and dc == 7))
                mm2.then_inc(sem("mW1M"), 1)
                wq += 1
            for dc in range(8):
                ndc = 8 * mi + dc
                tensor.wait_ge(sem("mSH"), SW("mSH", 60 * mi + 53 + dc))
                tensor.wait_ge(sem("mHP"), SW("mHP", NHP * (mi + 1)))
                if ndc >= 2:
                    tensor.wait_ge(sem("mX2E"), SW("mX2E", ndc - 1))
                W2P = [(w2h4, hidh3), (w2h4, hidl3), (w2l4, hidh3)]
                for pi, (wv, xv) in enumerate(W2P):
                    if pi == 2:
                        tensor.wait_ge(sem("mSP"), SW("mSP", 60 * mi + 53 + dc))
                    for hp in range(NHP):
                        mm = tensor.matmul(p_w[dc % 2][:, 0:W], wv[:, w2q % 2, hp, :],
                                           xv[:, hp, 0:W],
                                           start=(pi == 0 and hp == 0),
                                           stop=(pi == 2 and hp == NHP - 1))
                mm.then_inc(sem("mW2M"), 1)
                w2q += 1
            if l == 1:
                tensor.wait_ge(sem("mY3"), SW("mY3", 8 * (mi + 1)))
                tensor.wait_ge(sem("mRow"), SW("mRow", nrow * mi + 1))
                for dc in range(8):
                    mm = tensor.matmul(p_m1[:, 0:W], ones[:, 0:1], x2o3[:, dc, 0:W],
                                       start=(dc == 0), stop=(dc == 7))
                mm.then_inc(sem("mS1"), 1)
                tensor.wait_ge(sem("mSq"), SW("mSq", 2 * mi + 2))
                for dc in range(8):
                    mm = tensor.matmul(p_m2[:, 0:W], ones[:, 0:1], hsq3[:, dc, 0:W],
                                       start=(dc == 0), stop=(dc == 7))
                mm.then_inc(sem("mS2"), 1)
                tensor.wait_ge(sem("mRow"), SW("mRow", nrow * mi + 8))
                tensor.wait_ge(sem("mAP"), SW("mAP", 24 * (mi + 1)))
                if mi > 0:
                    tensor.wait_ge(sem("mAP2"), SW("mAP2", 16 * mi))
                tensor.matmul(p_bm[:, 0:W], ones[0:1, :], rowm[0:1, 0:W],
                              start=True, stop=True)
                tensor.matmul(p_br[:, 0:W], ones[0:1, :], rowm[0:1, 512:512 + W],
                              start=True, stop=True).then_inc(sem("mBC"), 1)

    @block.scalar
    def _(scalar):
        wq = 0
        w2q = 0
        for mi, (mo, W) in enumerate(TM):
            nrow = 4 if l == 0 else 8
            scalar.wait_ge(sem("mXL"), SW("mXL", 16 * (mi + 1)))
            if mi > 0:
                scalar.wait_ge(sem("mS2"), SW("mS2", (2 if l == 1 else 1) * mi))
                scalar.wait_ge(sem("mAP"), SW("mAP", 24 * mi))
                if l == 1:
                    scalar.wait_ge(sem("mAP2"), SW("mAP2", 16 * mi))
            scalar.activation(hsq3[:, 0:4, 0:W], xmt3[:, 0:4, 1:1 + W], AF.Square)
            scalar.activation(hsq3[:, 4:8, 0:W], xmt3[:, 4:8, 0:W], AF.Square
                              ).then_inc(sem("mSq"), 1)
            scalar.wait_ge(sem("mRow"), SW("mRow", nrow * mi + 1))
            scalar.activation(rowm[0:1, 1024:1024 + W], rowm[0:1, 0:W], AF.Square
                              ).then_inc(sem("mRow"), 1)
            scalar.wait_ge(sem("mRow"), SW("mRow", nrow * mi + 3))
            scalar.activation(rowm[0:1, 512:512 + W], rowm[0:1, 512:512 + W], AF.Sqrt
                              ).then_inc(sem("mRow"), 1)
            for dc in range(8):
                if dc == 0 and (l > 0 or mi > 0):
                    scalar.wait_ge(sem("mW1M"), SW("mW1M", NHP * mi))
                scalar.wait_ge(sem("mAP"), SW("mAP", 24 * mi + 3 * (dc + 1)))
                scalar.activation(hnh3[:, dc, 0:W], hnt2[:, dc % 2, 0:W], AF.Copy
                                  ).then_inc(sem("mSH"), 1)
            for hp in range(NHP):
                hpg = NHP * mi + hp
                scalar.wait_ge(sem(f"mWQ{wq % 2}"), SW(f"mWQ{wq % 2}", 16 * (wq // 2 + 1)))
                if wq >= 2:
                    scalar.wait_ge(sem("mW1M"), SW("mW1M", wq - 1))
                scalar.activation(w1h4[:, wq % 2, :, :], w1c4[:, wq % 2, :, :],
                                  AF.Copy).then_inc(sem("mSH"), 1)
                scalar.wait_ge(sem("mW1M"), SW("mW1M", hpg + 1))
                if hpg > 0:
                    scalar.wait_ge(sem("mHP"), SW("mHP", hpg))
                scalar.activation(hnt2[:, hp % 2, 0:W], p_g[:, 0:W], AF.Gelu,
                                  bias=cns[:, B1C + 2 * hp + 1:B1C + 2 * hp + 2]
                                  ).then_inc(sem("mGE"), 1)
                scalar.wait_ge(sem("mSP"), SW("mSP", 60 * mi + 10 + 2 * hp))
                scalar.activation(hidh3[:, hp, 0:W], hidt2[:, hp % 2, 0:W],
                                  AF.Copy).then_inc(sem("mSH"), 1)
                wq += 1
            for dc in range(8):
                scalar.wait_ge(sem(f"mW2Q{w2q % 2}"), SW(f"mW2Q{w2q % 2}", 16 * (w2q // 2 + 1)))
                if w2q >= 2:
                    scalar.wait_ge(sem("mW2M"), SW("mW2M", w2q - 1))
                scalar.activation(w2h4[:, w2q % 2, :, :], w2c4[:, w2q % 2, :, :],
                                  AF.Copy).then_inc(sem("mSH"), 1)
                w2q += 1
            if l == 1:
                scalar.wait_ge(sem("mY3"), SW("mY3", 8 * (mi + 1)))
                scalar.activation(hsq[:, 0:8 * 512].rearrange(
                    "p (c n) -> p c n", n=512)[:, :, 0:W],
                    x2o3[:, :, 0:W], AF.Square).then_inc(sem("mSq"), 1)
                scalar.wait_ge(sem("mRow"), SW("mRow", nrow * mi + 5))
                scalar.activation(rowm[0:1, 1024:1024 + W], rowm[0:1, 0:W], AF.Square
                                  ).then_inc(sem("mRow"), 1)
                scalar.wait_ge(sem("mRow"), SW("mRow", nrow * mi + 7))
                scalar.activation(rowm[0:1, 512:512 + W], rowm[0:1, 512:512 + W], AF.Sqrt
                                  ).then_inc(sem("mRow"), 1)

    @block.vector
    def _(vector):
        wq = 0
        w2q = 0
        for mi, (mo, W) in enumerate(TM):
            nrow = 4 if l == 0 else 8
            vector.wait_ge(sem("mS1"), SW("mS1", (2 if l == 1 else 1) * mi + 1))
            vector.tensor_scalar_mul(rowm[0:1, 0:W], p_m1[0:1, 0:W], 1.0 / 1024.0
                                     ).then_inc(sem("mRow"), 1)
            vector.wait_ge(sem("mRow"), SW("mRow", nrow * mi + 2))
            vector.wait_ge(sem("mS2"), SW("mS2", (2 if l == 1 else 1) * mi + 1))
            vector.tensor_scalar_mul(rowm[0:1, 512:512 + W], p_m2[0:1, 0:W],
                                     1.0 / 1024.0)
            vector.tensor_sub(rowm[0:1, 512:512 + W], rowm[0:1, 512:512 + W],
                              rowm[0:1, 1024:1024 + W])
            vector.tensor_scalar_add(rowm[0:1, 512:512 + W], rowm[0:1, 512:512 + W],
                                     1e-5)
            vector.reciprocal(rowm[0:1, 512:512 + W], rowm[0:1, 512:512 + W]
                              ).then_inc(sem("mRow"), 1)
            vector.wait_ge(sem("mBC"), SW("mBC", (2 if l == 1 else 1) * mi + 1))
            for dc in range(8):
                if dc == 0 and (l > 0 or mi > 0):
                    vector.wait_ge(sem("mW1M"), SW("mW1M", NHP * mi))
                if dc >= 2:
                    vector.wait_ge(sem("mSH"), SW("mSH", 60 * mi + dc - 1))
                vector.tensor_sub(hsq3[:, dc, 0:W], hm_ap(dc, W), p_bm[:, 0:W]
                                  ).then_inc(sem("mAP"), 1)
                vector.tensor_tensor(hnt2[:, dc % 2, 0:W], hsq3[:, dc, 0:W],
                                     p_br[:, 0:W],
                                     ALU.mult).then_inc(sem("mAP"), 1)
                vector.tensor_scalar(hnt2[:, dc % 2, 0:W], hnt2[:, dc % 2, 0:W],
                                     cns[:, LGC + dc:LGC + dc + 1],
                                     cns[:, LBC + dc:LBC + dc + 1],
                                     ALU.mult, ALU.add).then_inc(sem("mAP"), 1)
                vector.wait_ge(sem("mSH"), SW("mSH", 60 * mi + dc + 1))
                vector.tensor_sub(hnl3[:, dc, 0:W], hnt2[:, dc % 2, 0:W],
                                  hnh3[:, dc, 0:W]).then_inc(sem("mSP"), 1)
            for hp in range(NHP):
                hpg = NHP * mi + hp
                vector.wait_ge(sem(f"mWQ{wq % 2}"), SW(f"mWQ{wq % 2}", 16 * (wq // 2 + 1)))
                vector.wait_ge(sem("mSH"), SW("mSH", 60 * mi + 9 + 2 * hp))
                if wq >= 2:
                    vector.wait_ge(sem("mW1M"), SW("mW1M", wq - 1))
                vector.tensor_sub(w1l4[:, wq % 2, :, :], w1c4[:, wq % 2, :, :],
                                  w1h4[:, wq % 2, :, :]).then_inc(sem("mSP"), 1)
                vector.wait_ge(sem("mGE"), SW("mGE", hpg + 1))
                if hp >= 2:
                    vector.wait_ge(sem("mSH"), SW("mSH", 60 * mi + 6 + 2 * hp))
                elif l > 0 or mi > 0:
                    vector.wait_ge(sem("mSH"), SW("mSH", 60 * mi - 10 + 2 * hp))
                ins = vector.scalar_tensor_tensor(
                    hidt2[:, hp % 2, 0:W], p_a[:, 0:W],
                    cns[:, B1C + 2 * hp:B1C + 2 * hp + 1], hnt2[:, hp % 2, 0:W],
                    ALU.add, ALU.mult)
                ins.then_inc(sem("mSP"), 1)
                vector.wait_ge(sem("mSH"), SW("mSH", 60 * mi + 10 + 2 * hp))
                vector.tensor_sub(hidl3[:, hp, 0:W], hidt2[:, hp % 2, 0:W],
                                  hidh3[:, hp, 0:W]).then_inc(sem("mHP"), 1)
                wq += 1
            for k in range(10):
                if k < 8:
                    vector.wait_ge(sem(f"mW2Q{w2q % 2}"), SW(f"mW2Q{w2q % 2}", 16 * (w2q // 2 + 1)))
                    vector.wait_ge(sem("mSH"), SW("mSH", 60 * mi + 53 + k))
                    if w2q >= 2:
                        vector.wait_ge(sem("mW2M"), SW("mW2M", w2q - 1))
                    vector.tensor_sub(w2l4[:, w2q % 2, :, :], w2c4[:, w2q % 2, :, :],
                                      w2h4[:, w2q % 2, :, :]).then_inc(sem("mSP"), 1)
                    w2q += 1
                if k >= 2:
                    dc = k - 2
                    ndc = 8 * mi + dc
                    if l == 1 and dc == 0 and mi > 0:
                        vector.wait_ge(sem("mST"), SW("mST", 16 * mi))
                    vector.wait_ge(sem("mW2M"), SW("mW2M", ndc + 1))
                    ins = vector.scalar_tensor_tensor(
                        x2o3[:, dc, 0:W], p_w[dc % 2][:, 0:W],
                        cns[:, B2C + dc:B2C + dc + 1], xmt3[:, dc, 1:1 + W],
                        ALU.add, ALU.add)
                    ins.then_inc(sem("mX2E"), 1)
            if l == 0:
                if mi == FLM_MT:
                    vector.tensor_scalar_mul(
                        x2o3[:, :, FLM_LOC:FLM_LOC + 1],
                        x2o3[:, :, FLM_LOC:FLM_LOC + 1],
                        ctx["fl_sb"][:, 0:1]).then_inc(sem("mFLG"), 1)
            else:
                for dc in range(8):
                    vector.wait_ge(sem("mX1L"), SW("mX1L", 16 * (8 * mi + dc + 1)))
                    vector.tensor_add(x2o3[:, dc, 0:W], x2o3[:, dc, 0:W],
                                      x1s2[:, dc % 2, 0:W]).then_inc(sem("mY3"), 1)
                vector.wait_ge(sem("mS1"), SW("mS1", 2 * mi + 2))
                vector.tensor_scalar_mul(rowm[0:1, 0:W], p_m1[0:1, 0:W], 1.0 / 1024.0
                                         ).then_inc(sem("mRow"), 1)
                vector.wait_ge(sem("mRow"), SW("mRow", 8 * mi + 6))
                vector.wait_ge(sem("mS2"), SW("mS2", 2 * mi + 2))
                vector.tensor_scalar_mul(rowm[0:1, 512:512 + W], p_m2[0:1, 0:W],
                                         1.0 / 1024.0)
                vector.tensor_sub(rowm[0:1, 512:512 + W], rowm[0:1, 512:512 + W],
                                  rowm[0:1, 1024:1024 + W])
                vector.tensor_scalar_add(rowm[0:1, 512:512 + W],
                                         rowm[0:1, 512:512 + W], 1e-5)
                vector.reciprocal(rowm[0:1, 512:512 + W], rowm[0:1, 512:512 + W]
                                  ).then_inc(sem("mRow"), 1)
                vector.wait_ge(sem("mBC"), SW("mBC", 2 * mi + 2))
                if mi > 0:
                    vector.wait_ge(sem("mST"), SW("mST", 16 * mi))
                for dc in range(8):
                    vector.tensor_sub(hsq3[:, dc, 0:W], x2o3[:, dc, 0:W], p_bm[:, 0:W]
                                      ).then_inc(sem("mAP2"), 1)
                    vector.tensor_tensor(x2o3[:, dc, 0:W], hsq3[:, dc, 0:W],
                                         p_br[:, 0:W], ALU.mult
                                         ).then_inc(sem("mAP2"), 1)

    @block.gpsimd
    def _(gpsimd):
        for mi, (mo, W) in enumerate(TM):
            if l == 0:
                gpsimd.wait_ge(sem("mX2E"), SW("mX2E", 8 * (mi + 1)))
                if mi == FLM_MT:
                    gpsimd.wait_ge(sem("mFLG"), SW("mFLG", 1))
                gpsimd.dma_start(
                    out=dst.rearrange("(c p) n -> p c n", p=128)[:, :, mo:mo + W],
                    in_=x2o3[:, :, 0:W],
                ).then_inc(sem("mX2"), 16)
            else:
                for dc in range(8):
                    n1 = 8 * mi + dc
                    if n1 >= 2:
                        gpsimd.wait_ge(sem("mY3"), SW("mY3", n1 - 1))
                    gpsimd.dma_start(
                        out=x1s2[:, dc % 2, 0:W],
                        in_=S["x1_scr"].rearrange("(c p) n -> p c n", p=128)
                        [:, dc, T0_L + mo:T0_L + mo + W],
                    ).then_inc(sem("mX1L"), 16)
                gpsimd.wait_ge(sem("mAP2"), SW("mAP2", 16 * (mi + 1)))
                gpsimd.dma_start(
                    out=y_out.rearrange("(c p) n -> p c n", p=128)[:, :, mo:mo + W],
                    in_=x2o3[:, :, 0:W],
                ).then_inc(sem("mST"), 16)

    n_ln = 2 if l == 1 else 1
    bump("mW", 64)
    bump("mXL", 16 * NTM)
    bump("mWQ0", 16 * ((NHP * NTM + 1) // 2))
    bump("mWQ1", 16 * (NHP * NTM // 2))
    bump("mW2Q0", 16 * ((8 * NTM + 1) // 2))
    bump("mW2Q1", 16 * (8 * NTM // 2))
    bump("mS1", n_ln * NTM)
    bump("mS2", n_ln * NTM)
    bump("mSq", n_ln * NTM)
    bump("mRow", (4 if l == 0 else 8) * NTM)
    bump("mBC", n_ln * NTM)
    bump("mAP", 24 * NTM)
    bump("mW1M", NHP * NTM)
    bump("mGE", NHP * NTM)
    bump("mHP", NHP * NTM)
    bump("mSH", 60 * NTM)
    bump("mSP", 60 * NTM)
    bump("mW2M", 8 * NTM)
    bump("mX2E", 8 * NTM)
    if l == 0:
        bump("mFLG", 1)
        bump("mX2", 16 * NTM)
        ctx["m0_tiles"] = NTM
    else:
        bump("mY3", 8 * NTM)
        bump("mX1L", 128 * NTM)
        bump("mAP2", 16 * NTM)
        bump("mST", 16 * NTM)
        ctx["done"].append(("mST", ctx["SW"]("mST", 0)))

def emit_finish(ctx):
    block, sem = ctx["block"], ctx["sem"]
    done = list(ctx["done"])
    if ctx["dbg"] == "l1a":
        done.append(("aXMS", ctx["SW"]("aXMS", 0)))

    @block.gpsimd
    def _(gpsimd):
        for name, val in done:
            gpsimd.wait_ge(sem(name), val)



def kernel(**inputs):
    in_maps = _prepare(inputs)
    nc = build_nc(DEBUG_STAGE)
    res = run_bass_kernel_spmd(nc, in_maps, core_ids=list(range(8)))
    if DEBUG_STAGE is not None:
        return [r["y_out"] for r in res.results]
    out = np.empty((2, SEQ, D), dtype=np.float32)
    for c in range(8):
        b, g = c // 4, c % 4
        out[b, 2048 * g:2048 * (g + 1)] = res.results[c]["y_out"].T
    return out
